# version 06
# baseline (speedup 1.0000x reference)
import contextlib
import numpy as np

import concourse.bass as bass
import concourse.mybir as mybir
from concourse.bass_utils import run_bass_kernel_spmd

F32 = mybir.dt.float32
BF16 = mybir.dt.bfloat16
AF = mybir.ActivationFunctionType
ALU = mybir.AluOpType
AX = mybir.AxisListType

ENGINES = ("sync", "gpsimd", "vector", "scalar", "tensor")
LIVE = [512, 512, 384, 384, 256, 256, 128, 128]
D = 1024
NT = 32
NOWN = 16
NH = 16
NE = 32
PHASES_ALL = 99


class Build:

    def __init__(self):
        self.ops = []
        self.last_w = {}
        self.readers = {}

    def op(self, eng, fn, reads=(), writes=(), dma=False, force_await=False,
           sync_out=False):
        deps = set()
        for k in reads:
            w = self.last_w.get(k)
            if w is not None:
                deps.add(w)
        for k in writes:
            w = self.last_w.get(k)
            if w is not None:
                deps.add(w)
            for r in self.readers.get(k, ()):
                deps.add(r)
        i = len(self.ops)
        self.ops.append({"eng": eng, "fn": fn, "dma": dma, "deps": deps,
                         "force": force_await, "sync_out": sync_out})
        for k in reads:
            self.readers.setdefault(k, set()).add(i)
        for k in writes:
            self.last_w[k] = i
            self.readers[k] = set()
        return i

    def takeover(self, new_key, old_keys):
        rd = set()
        w = None
        for ok in old_keys:
            rd |= self.readers.get(ok, set())
            lw = self.last_w.get(ok)
            if lw is not None and (w is None or lw > w):
                w = lw
        if w is not None:
            self.last_w[new_key] = w
        self.readers[new_key] = rd

    def lower(self):
        seq = {e: [] for e in ENGINES}
        for i, o in enumerate(self.ops):
            seq[o["eng"]].append(i)
        def _edge_needed(od, o):
            if od["eng"] != o["eng"] or od["dma"]:
                return True
            if od["eng"] == "tensor":
                return False
            return True

        awaited = set(i for i, o in enumerate(self.ops) if o["force"])
        for o in self.ops:
            for d in o["deps"]:
                if _edge_needed(self.ops[d], o):
                    awaited.add(d)
        for e in ENGINES:
            c = 0
            for i in seq[e]:
                o = self.ops[i]
                if o["dma"]:
                    o["inc"] = 16
                elif i in awaited:
                    o["inc"] = 1
                else:
                    o["inc"] = 0
                c += o["inc"]
                o["cum"] = c
        for o in self.ops:
            w = {}
            for d in o["deps"]:
                od = self.ops[d]
                if not _edge_needed(od, o):
                    continue
                w[od["eng"]] = max(w.get(od["eng"], 0), od["cum"])
            o["waits"] = w
        self.seq = seq
        self.final = {e: (self.ops[seq[e][-1]]["cum"] if seq[e] else 0)
                      for e in ENGINES}

    def emit(self, ename, eng, sems):
        for i in self.seq[ename]:
            o = self.ops[i]
            for de in sorted(o["waits"]):
                v = o["waits"][de]
                if True:
                    eng.wait_ge(sems[de], v)
            inst = o["fn"](eng)
            if o["inc"]:
                inst.then_inc(sems[ename], o["inc"])


def build_nc(phases=PHASES_ALL, debug=()):
    nc = bass.Bass("TRN2", target_bir_lowering=False)

    x = nc.dram_tensor("x", [4096, D], F32, kind="ExternalInput").ap()
    wqkvT = nc.dram_tensor("wqkvT", [D, 3072], F32, kind="ExternalInput").ap()
    bqkv = nc.dram_tensor("bqkv", [128, 24], F32, kind="ExternalInput").ap()
    wfcT = nc.dram_tensor("wfcT", [D, D], F32, kind="ExternalInput").ap()
    bfc = nc.dram_tensor("bfc", [1, D], F32, kind="ExternalInput").ap()
    rwT = nc.dram_tensor("rwT", [D, 32], F32, kind="ExternalInput").ap()
    rb = nc.dram_tensor("rb", [1, 32], F32, kind="ExternalInput").ap()
    w1 = nc.dram_tensor("w1", [NE, D, 256], F32, kind="ExternalInput").ap()
    b1t = nc.dram_tensor("b1t", [128, 2 * NE], F32, kind="ExternalInput").ap()
    w2 = nc.dram_tensor("w2", [NE, 256, D], F32, kind="ExternalInput").ap()
    b2 = nc.dram_tensor("b2", [NE, D], F32, kind="ExternalInput").ap()
    masks = nc.dram_tensor("masks", [8, 128, 128], F32, kind="ExternalInput").ap()
    ident = nc.dram_tensor("ident", [128, 128], F32, kind="ExternalInput").ap()
    out = nc.dram_tensor("out", [2048, D], F32, kind="ExternalOutput").ap()

    xn_tm = nc.dram_tensor("xn_tm", [4096, D], BF16).ap()
    er_dram = nc.dram_tensor("er_dram", [16, 128], BF16).ap()

    dbg = {}
    for name in debug:
        shp = {"xn": [4096, D], "K": [128, 4 * 4096], "Q": [128, 4 * 2048],
               "Va": [128, 8 * 32 * 65], "aT": [128, 8 * 2048],
               "eid": [128, 16], "ffn": [128, NOWN * D],
               "lg": [128, 16 * 32]}[name]
        dt = F32 if name in ("eid", "lg") else (BF16 if name != "xn" else BF16)
        dbg[name] = nc.dram_tensor("dbg_" + name, shp, dt, kind="ExternalOutput").ap()

    b = Build()

    with contextlib.ExitStack() as stk:
        def sb(name, shape, dt):
            return stk.enter_context(nc.sbuf_tensor(name, shape, dt))

        def ps(name, shape, dt):
            return stk.enter_context(nc.psum_tensor(name, shape, dt))

        ident_bf = sb("ident_bf", [128, 128], BF16)
        masks_bf = sb("masks_bf", [128, 8 * 128], BF16)
        ones_bf = sb("ones_bf", [128, 512], BF16)
        eps_sb = sb("eps_sb", [128, 1], F32)
        bqkv_sb = sb("bqkv_sb", [128, 24], F32)
        b1t_sb = sb("b1t_sb", [128, 64], F32)
        rwT_bf = sb("rwT_bf", [128, 8 * 32], BF16)
        rb_bf = sb("rb_bf", [1, 32], BF16)
        bfc_bf = sb("bfc_bf", [1, D], BF16)
        iota_d = sb("iota_d", [128, 32], F32)
        x_t = [sb("x_t0", [128, D], F32), sb("x_t1", [128, D], F32)]
        xn_f = [sb("xn_f0", [128, D], F32), sb("xn_f1", [128, D], F32)]
        stats = sb("stats", [128, 16], F32)
        t1_f = sb("t1_f", [128, D], F32)
        xsq = t1_f
        A1 = sb("A1", [128, 16384], BF16)
        A2 = sb("A2", [128, 16640], BF16)
        A3 = sb("A3", [128, 16384], BF16)
        A4 = sb("A4", [128, 8704], BF16)
        K_sb = A1
        ffn_bf = A1
        Va_sb = A2[:, 0:16640]
        xno_sb = A2[:, 0:16384]
        wqkv_bf = A3[:, 0:12288]
        wfcT_bf = A3[:, 0:8192]
        w1_bf = [A3[:, 8192:10240], A3[:, 10240:12288]]
        w2_bf = [A3[:, 12288:14336], A3[:, 14336:16384]]
        Q_sb = A4[:, 0:8192]
        eid_bc = A4[:, 0:2048]
        maskb = A4[:, 2048:4096]
        y1m = A4[:, 4096:4608]
        eid_row = A4[0:1, 4608:6656]
        b2r = [A4[0:1, 6656:7680], A4[0:1, 7680:8704]]
        xnt_t = [sb("xnt0", [128, 8 * 512], BF16), sb("xnt1", [128, 8 * 512], BF16)]
        vfm = [sb("vfm0", [128, 512], BF16), sb("vfm1", [128, 512], BF16)]
        PT = [sb("PT0", [128, 1024], BF16), sb("PT1", [128, 1024], BF16)]
        aT = sb("aT", [128, 8 * 2048], BF16)
        lb_sb = sb("lb_sb", [128, 512], F32)
        stage = [sb("stage0", [128, 512], BF16), sb("stage1", [128, 512], BF16)]
        recip_sb = lb_sb
        recipb_sb = stage[0]
        logits_sb = sb("logits_sb", [128, 32], F32)
        eid_sb = sb("eid_sb", [128, 16], F32)
        eid_bf = sb("eid_bf", [128, 16], BF16)
        eidT_bf = sb("eidT_bf", [16, 128], BF16)
        _osb = sb("out_sb0", [128, D], F32)
        out_sb = [_osb, _osb]

        ps_st0 = ps("ps_st0", [128, 1024], F32)
        ps_st1 = ps("ps_st1", [128, 1024], F32)
        ps_av = ps("ps_av", [128, 1024], F32)
        ps_lb = ps("ps_lb", [128, 512], F32)
        ps_tbf = ps("ps_tbf", [128, 1024], BF16)
        ST = [(ps_st0, "st0a", 0), (ps_st0, "st0b", 512),
              (ps_st1, "st1a", 0), (ps_st1, "st1b", 512)]

        sems = {e: stk.enter_context(nc.semaphore("sem_" + e)) for e in ENGINES}
        block = stk.enter_context(nc.Block())

        b.op("sync", lambda e: e.dma_start(out=bqkv_sb[:], in_=bqkv),
             writes=["bqkv"], dma=True)
        b.op("gpsimd", lambda e: e.dma_start(out=ident_bf[:], in_=ident),
             writes=["ident"], dma=True)
        b.op("gpsimd", lambda e: e.dma_start(
            out=masks_bf[:], in_=masks.rearrange("r p q -> p r q")),
            writes=["masks"], dma=True)
        b.op("vector", lambda e: e.memset(ones_bf[:], 1.0), writes=["ones"])
        b.op("vector", lambda e: e.memset(eps_sb[:], 1e-5), writes=["eps"])
        b.op("sync", lambda e: e.dma_start(out=b1t_sb[:], in_=b1t),
             writes=["b1t"], dma=True)
        b.op("gpsimd", lambda e: e.dma_start(
            out=rwT_bf[:], in_=rwT.rearrange("(f p) e -> p f e", p=128)),
            writes=["rwT"], dma=True)
        b.op("gpsimd", lambda e: e.dma_start(out=rb_bf[:], in_=rb),
             writes=["rb"], dma=True)
        b.op("gpsimd", lambda e: e.dma_start(out=bfc_bf[:], in_=bfc),
             writes=["bfc"], dma=True)
        b.op("gpsimd", lambda e: e.iota(
            iota_d[:], pattern=[[-1, 32]], base=31, channel_multiplier=0,
            allow_small_or_imprecise_dtypes=True), writes=["iota"])

        S = stats

        def ln_tile(tt):
            bu = tt % 2
            xk, qk = f"x_t{bu}", f"xn_f{bu}"
            b.op("sync", lambda e, tt=tt, bu=bu: e.dma_start(
                out=x_t[bu][:], in_=x[tt * 128:(tt + 1) * 128, :]),
                writes=[xk], dma=True)
            b.op("vector", lambda e, bu=bu: e.scalar_tensor_tensor(
                out=xsq[:], in0=x_t[bu][:], scalar=1.0, in1=x_t[bu][:],
                op0=ALU.mult, op1=ALU.mult, accum_out=S[:, 1:2]),
                reads=[xk], writes=["xsq", "s2"], sync_out=True)
            b.op("vector", lambda e, bu=bu: e.reduce_sum(
                out=S[:, 0:1], in_=x_t[bu][:], axis=AX.X),
                reads=[xk], writes=["s1"], sync_out=True)
            b.op("vector", lambda e: e.tensor_scalar(
                out=S[:, 2:3], in0=S[:, 0:1], scalar1=1.0 / D, scalar2=None,
                op0=ALU.mult), reads=["s1"], writes=["mean"])
            b.op("vector", lambda e: e.tensor_scalar(
                out=S[:, 3:4], in0=S[:, 1:2], scalar1=1.0 / D, scalar2=None,
                op0=ALU.mult), reads=["s2"], writes=["ex2"])
            b.op("vector", lambda e: e.tensor_scalar(
                out=S[:, 4:5], in0=S[:, 2:3], scalar1=S[:, 2:3], scalar2=None,
                op0=ALU.mult), reads=["mean"], writes=["m2"])
            b.op("vector", lambda e: e.tensor_sub(
                out=S[:, 5:6], in0=S[:, 3:4], in1=S[:, 4:5]),
                reads=["ex2", "m2"], writes=["var"])
            b.op("scalar", lambda e: e.activation(
                out=S[:, 6:7], in_=S[:, 5:6], func=AF.Ln, bias=eps_sb[:]),
                reads=["var", "eps"], writes=["lnv"])
            b.op("scalar", lambda e: e.activation(
                out=S[:, 7:8], in_=S[:, 6:7], func=AF.Exp, scale=-0.5),
                reads=["lnv"], writes=["rstd"])
            b.op("vector", lambda e: e.tensor_scalar(
                out=S[:, 8:9], in0=S[:, 2:3], scalar1=S[:, 7:8], scalar2=-1.0,
                op0=ALU.mult, op1=ALU.mult), reads=["mean", "rstd"], writes=["nmr"])
            b.op("vector", lambda e, bu=bu: e.tensor_scalar(
                out=xn_f[bu][:], in0=x_t[bu][:], scalar1=S[:, 7:8],
                scalar2=S[:, 8:9], op0=ALU.mult, op1=ALU.add),
                reads=[xk, "rstd", "nmr"], writes=[qk])
            b.op("gpsimd", lambda e, tt=tt, bu=bu: e.dma_start(
                out=xn_tm[tt * 128:(tt + 1) * 128, :], in_=xn_f[bu][:]),
                reads=[qk], writes=[f"xntm{tt}"], dma=True)

        for tt in range(NT):
            ln_tile(tt)

        if "xn" in dbg:
            for tt in range(NT):
                bu = tt % 2
                b.op("gpsimd", lambda e, tt=tt, bu=bu: e.dma_start(
                    out=stage[bu][:], in_=xn_tm[tt * 128:(tt + 1) * 128, 0:512]),
                    reads=[f"xntm{tt}"], writes=[f"dstg{bu}"], dma=True)
                b.op("gpsimd", lambda e, tt=tt, bu=bu: e.dma_start(
                    out=dbg["xn"][tt * 128:(tt + 1) * 128, 0:512], in_=stage[bu][:]),
                    reads=[f"dstg{bu}"], dma=True, force_await=True)
                b.op("gpsimd", lambda e, tt=tt, bu=bu: e.dma_start(
                    out=stage[bu][:], in_=xn_tm[tt * 128:(tt + 1) * 128, 512:1024]),
                    reads=[f"xntm{tt}", f"dstg{bu}x"], writes=[f"dstg{bu}"], dma=True)
                b.op("gpsimd", lambda e, tt=tt, bu=bu: e.dma_start(
                    out=dbg["xn"][tt * 128:(tt + 1) * 128, 512:1024], in_=stage[bu][:]),
                    reads=[f"dstg{bu}"], dma=True, force_await=True)

        if phases < 2:
            b.lower()
            _finish(b, block, sems)
            return nc

        for w in range(2):
            if w == 1:
                for occ in range(4):
                    for tc in range(8):
                        b.takeover(f"K1_{occ}_{tc}", [f"K0_{occ}_{tc}"])
                        b.takeover(f"Q1_{occ}_{tc}", [f"Q0_{occ}_{tc}"])
                for hw in range(8):
                    for t in range(NT):
                        b.takeover(f"Va1_{hw}_{t}", [f"Va0_{hw}_{t}"])
            for f in range(8):
                b.op("gpsimd", lambda e, f=f, w=w: e.dma_start(
                    out=wqkv_bf[:, f * 1536:(f + 1) * 1536],
                    in_=wqkvT[f * 128:(f + 1) * 128, w * 1536:(w + 1) * 1536]),
                    writes=[f"wq{f}"], dma=True)
            b.op("vector", lambda e: e.memset(Va_sb[:], 1.0),
                 writes=[f"Va{w}_{hw}_{t}" for hw in range(8) for t in range(NT)])

            for tc in range(8):
                bu = tc % 2
                for f in range(8):
                    b.op("sync", lambda e, tc=tc, f=f, bu=bu: e.dma_start(
                        out=xnt_t[bu][:, f * 512:(f + 1) * 512],
                        in_=xn_tm[tc * 512:(tc + 1) * 512, f * 128:(f + 1) * 128],
                        transpose=True),
                        reads=[f"xntm{4 * tc + j}" for j in range(4)],
                        writes=[f"xnt{bu}_{f}"], dma=True)
                for oc in range(12):
                    pst, pk, po = ST[oc % 4]
                    comp = oc // 4
                    occ = oc % 4
                    gcol = w * 12 + oc
                    mm = []
                    for f in range(8):
                        mm.append(b.op("tensor", lambda e, f=f, oc=oc, pst=pst,
                                       po=po, bu=bu: e.matmul(
                            pst[:, po:po + 512],
                            wqkv_bf[:, f * 1536 + oc * 128: f * 1536 + (oc + 1) * 128],
                            xnt_t[bu][:, f * 512:(f + 1) * 512],
                            start=(f == 0), stop=(f == 7)),
                            reads=[f"wq{f}", f"xnt{bu}_{f}"], writes=[pk]))
                    if comp == 0:
                        b.op("vector", lambda e, pst=pst, po=po, occ=occ, tc=tc,
                             gcol=gcol, w=w: e.tensor_scalar(
                            out=Q_sb[:, occ * 2048 + tc * 256: occ * 2048 + (tc + 1) * 256]
                                .rearrange("p (a c) -> p a c", a=2),
                            in0=pst[:, po:po + 512].rearrange(
                                "p (a c) -> p a c", a=4)[:, 0::2, :],
                            scalar1=bqkv_sb[:, gcol:gcol + 1], scalar2=None,
                            op0=ALU.add),
                            reads=[pk, "bqkv"], writes=[f"Q{w}_{occ}_{tc}"])
                    elif comp == 1:
                        b.op("vector", lambda e, pst=pst, po=po, occ=occ, tc=tc,
                             gcol=gcol: e.tensor_scalar(
                            out=K_sb[:, occ * 4096 + tc * 512: occ * 4096 + (tc + 1) * 512],
                            in0=pst[:, po:po + 512],
                            scalar1=bqkv_sb[:, gcol:gcol + 1], scalar2=None,
                            op0=ALU.add),
                            reads=[pk, "bqkv"], writes=[f"K{w}_{occ}_{tc}"])
                    else:
                        vb = tc % 2
                        b.op("vector", lambda e, pst=pst, po=po, vb=vb,
                             gcol=gcol: e.tensor_scalar(
                            out=vfm[vb][:], in0=pst[:, po:po + 512],
                            scalar1=bqkv_sb[:, gcol:gcol + 1], scalar2=None,
                            op0=ALU.add),
                            reads=[pk, "bqkv"], writes=[f"vfm{vb}"])
                        for j in range(4):
                            tt = 4 * tc + j
                            hw0 = occ * 2
                            b.op("tensor", lambda e, vb=vb, j=j: e.transpose(
                                ps_tbf[:, 0:128], vfm[vb][:, j * 128:(j + 1) * 128],
                                ident_bf[:]),
                                reads=[f"vfm{vb}", "ident"], writes=["tbf"])
                            for hh in range(2):
                                hw = hw0 + hh
                                off = (hw * NT + tt) * 65
                                b.op("vector", lambda e, hh=hh, off=off: e.tensor_copy(
                                    out=Va_sb[:, off:off + 64],
                                    in_=ps_tbf[:, hh * 64:hh * 64 + 64]),
                                    reads=["tbf"],
                                    writes=[f"Va{w}_{hw0 + hh}_{tt}"])

            if "K" in dbg and w == 0:
                b.op("sync", lambda e: e.dma_start(out=dbg["K"], in_=K_sb[:]),
                     reads=[f"K{w}_{occ}_{tc}" for occ in range(4) for tc in range(8)],
                     dma=True, force_await=True)
            if "Q" in dbg and w == 0:
                b.op("sync", lambda e: e.dma_start(out=dbg["Q"], in_=Q_sb[:]),
                     reads=[f"Q{w}_{occ}_{tc}" for occ in range(4) for tc in range(8)],
                     dma=True, force_await=True)
            if "Va" in dbg and w == 0:
                b.op("sync", lambda e: e.dma_start(out=dbg["Va"], in_=Va_sb[:]),
                     reads=[f"Va{w}_{hw}_{t}" for hw in range(8) for t in range(NT)],
                     dma=True, force_await=True)

            if phases < 3:
                break

            for hp in range(4):
                for c in range(4):
                    nkt = 8 * c + 8
                    fch = w * 4 + hp
                    for kt in range(nkt):
                        live = 512 if kt < 8 * c else LIVE[kt - 8 * c]
                        qoff = 512 - live
                        pst, ptb = (ps_st0, 0) if kt % 2 == 0 else (ps_st1, 1)
                        pkeys = [("st0a", "st0b"), ("st1a", "st1b")][ptb]
                        for j in range(2):
                            hw = 2 * hp + j
                            b.op("tensor", lambda e, pst=pst, j=j, hp=hp, kt=kt,
                                 c=c, qoff=qoff, live=live: e.matmul(
                                pst[:, j * 512 + qoff: (j + 1) * 512],
                                K_sb[j * 64:(j + 1) * 64,
                                     hp * 4096 + kt * 128: hp * 4096 + (kt + 1) * 128],
                                Q_sb[j * 64:(j + 1) * 64,
                                     hp * 2048 + c * 512 + qoff: hp * 2048 + (c + 1) * 512],
                                start=True, stop=True,
                                tile_position=(64 * j, 0)),
                                reads=[f"K{w}_{hp}_{kt // 4}",
                                       f"Q{w}_{hp}_{2 * c}", f"Q{w}_{hp}_{2 * c + 1}"],
                                writes=[pkeys[j]])
                        b.op("scalar", lambda e, pst=pst, ptb=ptb: e.activation(
                            out=PT[ptb][:], in_=pst[:], func=AF.Exp, scale=0.125),
                            reads=list(pkeys), writes=[f"pt{ptb}"])
                        if kt >= 8 * c:
                            r = kt - 8 * c
                            for j in range(2):
                                b.op("vector", lambda e, ptb=ptb, j=j, qoff=qoff,
                                     r=r: e.tensor_mul(
                                    out=PT[ptb][:, j * 512 + qoff: j * 512 + qoff + 128],
                                    in0=PT[ptb][:, j * 512 + qoff: j * 512 + qoff + 128],
                                    in1=masks_bf[:, r * 128:(r + 1) * 128]),
                                    reads=[f"pt{ptb}", "masks"], writes=[f"pt{ptb}"])
                        for j in range(2):
                            hw = 2 * hp + j
                            b.op("tensor", lambda e, hw=hw, kt=kt, ptb=ptb, j=j,
                                 qoff=qoff, c=c, nkt=nkt: e.matmul(
                                ps_av[0:65, j * 512 + qoff: (j + 1) * 512],
                                Va_sb[:, (hw * NT + kt) * 65: (hw * NT + kt) * 65 + 65],
                                PT[ptb][:, j * 512 + qoff: (j + 1) * 512],
                                start=(kt == 0), stop=(kt == nkt - 1)),
                                reads=[f"pt{ptb}", f"Va{w}_{hw}_{kt}"],
                                writes=[f"av{j}"])
                    for j in range(2):
                        hv = f"av{j}"
                        b.op("vector", lambda e, j=j: e.reciprocal(
                            out=recip_sb[64:65, 0:512],
                            in_=ps_av[64:65, j * 512:(j + 1) * 512]),
                            reads=[hv], writes=["recip"], sync_out=True)
                        b.op("vector", lambda e: e.tensor_copy(
                            out=recipb_sb[64:65, 0:512], in_=recip_sb[64:65, 0:512]),
                            reads=["recip"], writes=["recipb"])
                        b.op("tensor", lambda e: e.matmul(
                            ps_lb[0:64, :], ones_bf[64:65, 0:64],
                            recipb_sb[64:65, 0:512], start=True, stop=True),
                            reads=["recipb", "ones"], writes=["lb"])
                        b.op("vector", lambda e: e.tensor_copy(
                            out=lb_sb[0:64, :], in_=ps_lb[0:64, :]),
                            reads=["lb"], writes=["lbs"])
                        if j == 0:
                            b.op("vector", lambda e, j=j, fch=fch, c=c: e.tensor_mul(
                                out=aT[0:64, fch * 2048 + c * 512: fch * 2048 + (c + 1) * 512],
                                in0=ps_av[0:64, j * 512:(j + 1) * 512],
                                in1=lb_sb[0:64, :]),
                                reads=[hv, "lbs"], writes=[f"aT_{fch}_{c}L"])
                        else:
                            sgb = (hp * 4 + c) % 2
                            b.op("vector", lambda e, j=j, sgb=sgb: e.tensor_mul(
                                out=stage[sgb][0:64, :],
                                in0=ps_av[0:64, j * 512:(j + 1) * 512],
                                in1=lb_sb[0:64, :]),
                                reads=[hv, "lbs"], writes=[f"stg{sgb}"])
                            b.op("tensor", lambda e, sgb=sgb: e.matmul(
                                ps_av[64:128, 0:512], ident_bf[0:64, 0:64],
                                stage[sgb][0:64, :], start=True, stop=True,
                                tile_position=(0, 64)),
                                reads=[f"stg{sgb}", "ident"],
                                writes=["avhop", "av0"])
                            b.op("vector", lambda e, fch=fch, c=c: e.tensor_copy(
                                out=aT[64:128, fch * 2048 + c * 512: fch * 2048 + (c + 1) * 512],
                                in_=ps_av[64:128, 0:512]),
                                reads=["avhop"], writes=[f"aT_{fch}_{c}H"])

        if "aT" in dbg:
            b.op("sync", lambda e: e.dma_start(out=dbg["aT"], in_=aT[:]),
                 reads=[f"aT_{fc_}_{c}{h}" for fc_ in range(8) for c in range(4)
                        for h in "LH"],
                 dma=True, force_await=True)

        if phases < 4:
            b.lower()
            _finish(b, block, sems)
            return nc

        _va_keys = [f"Va{w}_{hw}_{t}" for w in range(2) for hw in range(8)
                    for t in range(NT)]
        for f in range(8):
            for s_ in range(NOWN):
                b.takeover(f"xno_{f}_{s_}", _va_keys)
        _q_keys = [f"Q{w}_{occ}_{tc}" for w in range(2) for occ in range(4)
                   for tc in range(8)]
        for kk in ("eidbc", "mbk", "y1m", "eidrow", "b2r0", "b2r1"):
            b.takeover(kk, _q_keys)
        for f in range(8):
            for s in range(NOWN):
                b.op("sync", lambda e, f=f, s=s: e.dma_start(
                    out=xno_sb[:, f * 2048 + s * 128: f * 2048 + (s + 1) * 128],
                    in_=xn_tm[(2 * s) * 128:(2 * s + 1) * 128, f * 128:(f + 1) * 128],
                    transpose=True),
                    reads=[f"xntm{2 * s}"], writes=[f"xno_{f}_{s}"], dma=True)

        for s in range(NOWN):
            for f in range(8):
                b.op("tensor", lambda e, f=f, s=s: e.matmul(
                    ps_lb[:, 0:32],
                    xno_sb[:, f * 2048 + s * 128: f * 2048 + (s + 1) * 128],
                    rwT_bf[:, f * 32:(f + 1) * 32],
                    start=(f == 0), stop=False),
                    reads=[f"xno_{f}_{s}", "rwT"], writes=["lb"])
            b.op("tensor", lambda e: e.matmul(
                ps_lb[:, 0:32], ones_bf[0:1, 0:128], rb_bf[0:1, :],
                start=False, stop=True),
                reads=["ones", "rb"], writes=["lb"])
            b.op("vector", lambda e: e.tensor_copy(
                out=logits_sb[:], in_=ps_lb[:, 0:32]),
                reads=["lb"], writes=["lg"])
            if "lg" in dbg:
                b.op("sync", lambda e, s=s: e.dma_start(
                    out=dbg["lg"][:, s * 32:(s + 1) * 32], in_=logits_sb[:]),
                    reads=["lg"], dma=True, force_await=True)
            b.op("vector", lambda e: e.reduce_max(
                out=S[:, 10:11], in_=logits_sb[:], axis=AX.X),
                reads=["lg"], writes=["lmax"], sync_out=True)
            b.op("vector", lambda e: e.tensor_scalar(
                out=logits_sb[:], in0=logits_sb[:], scalar1=S[:, 10:11],
                scalar2=None, op0=ALU.is_ge), reads=["lg", "lmax"], writes=["lg"])
            b.op("vector", lambda e: e.tensor_mul(
                out=logits_sb[:], in0=logits_sb[:], in1=iota_d[:]),
                reads=["lg", "iota"], writes=["lg"])
            b.op("vector", lambda e: e.reduce_max(
                out=S[:, 11:12], in_=logits_sb[:], axis=AX.X),
                reads=["lg"], writes=["l31"], sync_out=True)
            b.op("vector", lambda e, s=s: e.tensor_scalar(
                out=eid_sb[:, s:s + 1], in0=S[:, 11:12], scalar1=-1.0,
                scalar2=31.0, op0=ALU.mult, op1=ALU.add),
                reads=["l31"], writes=[f"eid{s}"])

        if "eid" in dbg:
            b.op("sync", lambda e: e.dma_start(out=dbg["eid"], in_=eid_sb[:]),
                 reads=[f"eid{s}" for s in range(NOWN)], dma=True,
                 force_await=True)

        if phases < 5:
            b.lower()
            _finish(b, block, sems)
            return nc

        b.op("vector", lambda e: e.tensor_copy(out=eid_bf[:], in_=eid_sb[:]),
             reads=[f"eid{s}" for s in range(NOWN)], writes=["eidbf"])
        b.op("tensor", lambda e: e.transpose(
            ps_tbf[0:16, 0:128], eid_bf[:], ident_bf[:]),
            reads=["eidbf", "ident"], writes=["tbf"])
        b.op("vector", lambda e: e.tensor_copy(
            out=eidT_bf[:], in_=ps_tbf[0:16, 0:128]), reads=["tbf"], writes=["eidT"])
        b.op("gpsimd", lambda e: e.dma_start(out=er_dram[:, :], in_=eidT_bf[:]),
             reads=["eidT"], writes=["erd"], dma=True)
        b.op("gpsimd", lambda e: e.dma_start(
            out=eid_row[:], in_=er_dram.rearrange("a b -> (a b)")),
            reads=["erd"], writes=["eidrow"], dma=True)
        STL = ["st0a", "st0b", "st1a", "st1b"]
        STP = [(ps_st0, 0), (ps_st0, 512), (ps_st1, 0), (ps_st1, 512)]
        for i in range(4):
            pst, po = STP[i]
            b.op("tensor", lambda e, pst=pst, po=po, i=i: e.matmul(
                pst[:, po:po + 512],
                ones_bf[0:1, 0:128], eid_row[0:1, i * 512:(i + 1) * 512],
                start=True, stop=True),
                reads=["eidrow", "ones"], writes=[STL[i]])
        for i in range(4):
            pst, po = STP[i]
            b.op("vector", lambda e, pst=pst, po=po, i=i: e.tensor_copy(
                out=eid_bc[:, i * 512:(i + 1) * 512], in_=pst[:, po:po + 512]),
                reads=[STL[i]], writes=["eidbc"])

        _k_keys = [f"K{w}_{occ}_{tc}" for w in range(2) for occ in range(4)
                   for tc in range(8)]
        for t in range(NOWN):
            b.takeover(f"ffn{t}", _k_keys)
        _wq_keys = [f"wq{f}" for f in range(8)]
        for kk in ["w1b0", "w1b1", "w2b0", "w2b1"] + [f"wfc{f}" for f in range(8)]:
            b.takeover(kk, _wq_keys)
        b.op("vector", lambda e: e.memset(ffn_bf[:], 0.0),
             writes=[f"ffn{t}" for t in range(NOWN)])
        for e_ in range(NE):
            wb = e_ % 2
            w1k, w2k = f"w1b{wb}", f"w2b{wb}"
            b.op("gpsimd", lambda e, e_=e_, wb=wb: e.dma_start(
                out=w1_bf[wb][:], in_=w1[e_, :, :].rearrange(
                    "(f p) h -> p f h", p=128)),
                writes=[w1k], dma=True)
            b.op("gpsimd", lambda e, e_=e_, wb=wb: e.dma_start(
                out=w2_bf[wb][:], in_=w2[e_, :, :].rearrange(
                    "(hc p) o -> p hc o", p=128)),
                writes=[w2k], dma=True)
            b.op("gpsimd", lambda e, e_=e_, wb=wb: e.dma_start(
                out=b2r[wb], in_=b2[e_:e_ + 1, :]),
                writes=[f"b2r{wb}"], dma=True)
            b.op("vector", lambda e, e_=e_: e.tensor_scalar(
                out=maskb[:], in0=eid_bc[:], scalar1=float(e_), scalar2=None,
                op0=ALU.is_equal), reads=["eidbc"], writes=["mbk"])
            for tg in range(8):
                for hc in range(2):
                    po_y1 = hc * 512
                    for f in range(8):
                        b.op("tensor", lambda e, hc=hc, f=f, tg=tg, wb=wb,
                             po_y1=po_y1: e.matmul(
                            ps_av[:, po_y1:po_y1 + 256],
                            w1_bf[wb][:, f * 256 + hc * 128: f * 256 + (hc + 1) * 128],
                            xno_sb[:, f * 2048 + tg * 256: f * 2048 + (tg + 1) * 256],
                            start=(f == 0), stop=(f == 7)),
                            reads=[w1k, f"xno_{f}_{2 * tg}", f"xno_{f}_{2 * tg + 1}"],
                            writes=[f"av{hc}"])
                    b.op("vector", lambda e, hc=hc, e_=e_, tg=tg,
                         po_y1=po_y1: e.scalar_tensor_tensor(
                        out=y1m[:, hc * 256:(hc + 1) * 256],
                        in0=ps_av[:, po_y1:po_y1 + 256],
                        scalar=b1t_sb[:, 2 * e_ + hc: 2 * e_ + hc + 1],
                        in1=maskb[:, tg * 256:(tg + 1) * 256],
                        op0=ALU.add, op1=ALU.mult),
                        reads=[f"av{hc}", "b1t", "mbk"], writes=["y1m"])
                b.op("vector", lambda e: e.tensor_scalar(
                    out=y1m[:], in0=y1m[:], scalar1=0.0, scalar2=None,
                    op0=ALU.max), reads=["y1m"], writes=["y1m"])
                for tl in range(2):
                    tt = 2 * tg + tl
                    pst = ps_st0 if tl == 0 else ps_st1
                    pk = [("st0a", "st0b"), ("st1a", "st1b")][tl]
                    for half in range(2):
                        for hc in range(2):
                            b.op("tensor", lambda e, tl=tl, half=half, hc=hc,
                                 pst=pst, wb=wb: e.matmul(
                                pst[:, half * 512:(half + 1) * 512],
                                y1m[:, hc * 256 + tl * 128: hc * 256 + (tl + 1) * 128],
                                w2_bf[wb][:, hc * 1024 + half * 512: hc * 1024 + (half + 1) * 512],
                                start=(hc == 0), stop=False),
                                reads=["y1m", w2k], writes=[pk[half]])
                        b.op("tensor", lambda e, tl=tl, half=half, pst=pst,
                             e_=e_, tg=tg, wb=wb: e.matmul(
                            pst[:, half * 512:(half + 1) * 512],
                            maskb[0:1, tg * 256 + tl * 128: tg * 256 + (tl + 1) * 128],
                            b2r[wb][0:1, half * 512:(half + 1) * 512],
                            start=False, stop=True),
                            reads=["mbk", f"b2r{wb}"], writes=[pk[half]])
                    b.op("vector", lambda e, pst=pst, tt=tt: e.tensor_add(
                        out=ffn_bf[:, tt * D:(tt + 1) * D],
                        in0=ffn_bf[:, tt * D:(tt + 1) * D], in1=pst[:]),
                        reads=[f"ffn{tt}"] + list(pk), writes=[f"ffn{tt}"])

        if "ffn" in dbg:
            b.op("sync", lambda e: e.dma_start(out=dbg["ffn"], in_=ffn_bf[:]),
                 reads=[f"ffn{tt}" for tt in range(NOWN)], dma=True,
                 force_await=True)

        if phases < 6:
            b.lower()
            _finish(b, block, sems)
            return nc

        for f in range(8):
            b.op("gpsimd", lambda e, f=f: e.dma_start(
                out=wfcT_bf[:, f * 1024:(f + 1) * 1024],
                in_=wfcT[f * 128:(f + 1) * 128, :]),
                writes=[f"wfc{f}"], dma=True)

        for s in range(NOWN if phases >= 7 else 2):
            bu = s % 2
            b.op("sync", lambda e, s=s, bu=bu: e.dma_start(
                out=x_t[bu][:], in_=x[(2 * s) * 128:(2 * s + 1) * 128, :]),
            writes=[f"x_t{bu}"], dma=True)
            c = s // 4
            for f in range(8):
                for half in range(2):
                    b.op("tensor", lambda e, f=f, half=half, s=s, c=c: e.matmul(
                        ps_av[:, half * 512:(half + 1) * 512],
                        aT[:, f * 2048 + s * 128: f * 2048 + (s + 1) * 128],
                        wfcT_bf[:, f * 1024 + half * 512: f * 1024 + (half + 1) * 512],
                        start=(f == 0), stop=False),
                        reads=[f"wfc{f}", f"aT_{f}_{c}L", f"aT_{f}_{c}H"],
                        writes=[f"av{half}"])
            for half in range(2):
                b.op("tensor", lambda e, half=half: e.matmul(
                    ps_av[:, half * 512:(half + 1) * 512],
                    ones_bf[0:1, 0:128],
                    bfc_bf[0:1, half * 512:(half + 1) * 512],
                    start=False, stop=True),
                    reads=["ones", "bfc"], writes=[f"av{half}"])
            b.op("vector", lambda e, bu=bu: e.tensor_add(
                out=t1_f[:], in0=ps_av[:], in1=x_t[bu][:]),
                reads=["av0", "av1", f"x_t{bu}"], writes=["t1"])
            b.op("vector", lambda e, s=s, bu=bu: e.scalar_tensor_tensor(
                out=out_sb[bu][:], in0=ffn_bf[:, s * D:(s + 1) * D],
                scalar=0.0, in1=t1_f[:], op0=ALU.max, op1=ALU.add),
                reads=["t1", f"ffn{s}"], writes=["outsb"])
            b.op("sync", lambda e, s=s, bu=bu: e.dma_start(
                out=out[s * 128:(s + 1) * 128, :], in_=out_sb[bu][:]),
                reads=["outsb"], writes=[f"odma{s}"], dma=True,
                force_await=True)

        b.lower()
        _finish(b, block, sems)
    return nc


def _finish(b, block, sems):
    @block.sync
    def _(e):
        b.emit("sync", e, sems)
        for en in ENGINES:
            if b.final[en]:
                e.wait_ge(sems[en], b.final[en])

    @block.gpsimd
    def _(e):
        b.emit("gpsimd", e, sems)

    @block.vector
    def _(e):
        b.emit("vector", e, sems)

    @block.scalar
    def _(e):
        b.emit("scalar", e, sems)

    @block.tensor
    def _(e):
        b.emit("tensor", e, sems)



def _core_inputs(inputs, core):
    bidx, par = core // 2, core % 2
    x = np.asarray(inputs["L0_attn_x"], np.float32)[bidx]
    if par == 1:
        xr = x.reshape(32, 128, D)
        perm = np.arange(32)
        perm = perm + 1 - 2 * (perm % 2)
        x = np.ascontiguousarray(xr[perm].reshape(4096, D))
    else:
        x = np.ascontiguousarray(x)

    w_qkv = np.asarray(inputs["L0_attn_w_qkv"], np.float32)
    b_qkv = np.asarray(inputs["L0_attn_b_qkv"], np.float32)
    rows = []
    for w in range(2):
        for comp in range(3):
            for hw in range(8):
                h = w * 8 + hw
                base = comp * 1024 + h * 64
                rows.append(np.arange(base, base + 64))
    rows = np.concatenate(rows)
    wqkvT = np.ascontiguousarray(w_qkv[rows].T)
    bq = np.ascontiguousarray(b_qkv[rows].reshape(24, 128).T)

    w_fc = np.asarray(inputs["L0_attn_w_fc"], np.float32)
    wfcT = np.ascontiguousarray(w_fc.T)
    bfc = np.ascontiguousarray(inputs["L0_attn_b_fc"], dtype=np.float32).reshape(1, D)

    rwT = np.ascontiguousarray(np.asarray(inputs["L0_ffn_router_w"], np.float32).T)
    rb = np.ascontiguousarray(np.asarray(inputs["L0_ffn_router_b"], np.float32).reshape(1, 32))
    w1 = np.ascontiguousarray(np.asarray(inputs["L0_ffn_weights1"], np.float32))
    b1 = np.asarray(inputs["L0_ffn_bias1"], np.float32)
    b1t = np.ascontiguousarray(
        b1.reshape(NE, 2, 128).transpose(2, 0, 1).reshape(128, 2 * NE))
    w2 = np.ascontiguousarray(np.asarray(inputs["L0_ffn_weights2"], np.float32))
    b2 = np.ascontiguousarray(np.asarray(inputs["L0_ffn_bias2"], np.float32))

    mk = np.zeros((8, 128, 128), np.float32)
    tri = np.triu(np.ones((128, 128), np.float32))
    for r in range(8):
        if r % 2 == 0:
            mk[r] = tri
        else:
            mk[r] = 0.0 if par == 0 else 1.0

    return {"x": x, "wqkvT": wqkvT, "bqkv": bq, "wfcT": wfcT, "bfc": bfc,
            "rwT": rwT, "rb": rb, "w1": w1, "b1t": b1t, "w2": w2, "b2": b2,
            "masks": np.ascontiguousarray(mk),
            "ident": np.eye(128, dtype=np.float32)}


def kernel(**inputs):
    nc = build_nc()
    in_maps = [_core_inputs(inputs, c) for c in range(8)]
    res = run_bass_kernel_spmd(nc, in_maps, core_ids=list(range(8)))
    out = np.empty((4, 4096, D), np.float32)
    for c in range(8):
        bidx, par = c // 2, c % 2
        o = res.results[c]["out"].reshape(NOWN, 128, D)
        ob = out[bidx].reshape(32, 128, D)
        for s in range(NOWN):
            ob[2 * s + par] = o[s]
    return out


# version 11
# speedup vs baseline: 2.1591x; 2.1591x over previous
import contextlib
import numpy as np

import concourse.bass as bass
import concourse.mybir as mybir
from concourse.bass_core_rust import IndirectOffsetOnAxis
from concourse.bass_utils import run_bass_kernel_spmd

F32 = mybir.dt.float32
BF16 = mybir.dt.bfloat16
AF = mybir.ActivationFunctionType
ALU = mybir.AluOpType
AX = mybir.AxisListType

ENGINES = ("sync", "gpsimd", "vector", "scalar", "tensor")
LIVE = [512, 512, 384, 384, 256, 256, 128, 128]
D = 1024
NT = 32
NOWN = 16
NH = 16
NE = 32
PHASES_ALL = 99


class Build:

    def __init__(self):
        self.ops = []
        self.last_w = {}
        self.readers = {}

    def op(self, eng, fn, reads=(), writes=(), dma=False, force_await=False,
           sync_out=False):
        deps = set()
        for k in reads:
            w = self.last_w.get(k)
            if w is not None:
                deps.add(w)
        for k in writes:
            w = self.last_w.get(k)
            if w is not None:
                deps.add(w)
            for r in self.readers.get(k, ()):
                deps.add(r)
        i = len(self.ops)
        self.ops.append({"eng": eng, "fn": fn, "dma": dma, "deps": deps,
                         "force": force_await, "sync_out": sync_out})
        for k in reads:
            self.readers.setdefault(k, set()).add(i)
        for k in writes:
            self.last_w[k] = i
            self.readers[k] = set()
        return i

    def takeover(self, new_key, old_keys):
        rd = set()
        w = None
        for ok in old_keys:
            rd |= self.readers.get(ok, set())
            lw = self.last_w.get(ok)
            if lw is not None and (w is None or lw > w):
                w = lw
        if w is not None:
            self.last_w[new_key] = w
        self.readers[new_key] = rd

    def lower(self):
        seq = {e: [] for e in ENGINES}
        for i, o in enumerate(self.ops):
            seq[o["eng"]].append(i)
        def _edge_needed(od, o):
            if od["eng"] != o["eng"] or od["dma"]:
                return True
            if od["eng"] == "tensor":
                return False
            return True

        awaited = set(i for i, o in enumerate(self.ops) if o["force"])
        for o in self.ops:
            for d in o["deps"]:
                if _edge_needed(self.ops[d], o):
                    awaited.add(d)
        for e in ENGINES:
            c = 0
            for i in seq[e]:
                o = self.ops[i]
                if o["dma"]:
                    o["inc"] = 16
                elif i in awaited:
                    o["inc"] = 1
                else:
                    o["inc"] = 0
                c += o["inc"]
                o["cum"] = c
        for o in self.ops:
            w = {}
            for d in o["deps"]:
                od = self.ops[d]
                if not _edge_needed(od, o):
                    continue
                w[od["eng"]] = max(w.get(od["eng"], 0), od["cum"])
            o["waits"] = w
        self.seq = seq
        self.final = {e: (self.ops[seq[e][-1]]["cum"] if seq[e] else 0)
                      for e in ENGINES}

    def emit(self, ename, eng, sems):
        waited = {}
        for i in self.seq[ename]:
            o = self.ops[i]
            for de in sorted(o["waits"]):
                v = o["waits"][de]
                if waited.get(de, -1) < v:
                    eng.wait_ge(sems[de], v)
                    waited[de] = v
            inst = o["fn"](eng)
            if o["inc"]:
                inst.then_inc(sems[ename], o["inc"])


def build_nc(phases=PHASES_ALL, debug=()):
    nc = bass.Bass("TRN2", target_bir_lowering=False)

    x = nc.dram_tensor("x", [4096, D], F32, kind="ExternalInput").ap()
    wqkvT = nc.dram_tensor("wqkvT", [D, 3072], F32, kind="ExternalInput").ap()
    bqkv = nc.dram_tensor("bqkv", [128, 24], F32, kind="ExternalInput").ap()
    wfcT = nc.dram_tensor("wfcT", [D, D], F32, kind="ExternalInput").ap()
    bfc = nc.dram_tensor("bfc", [1, D], F32, kind="ExternalInput").ap()
    rwT = nc.dram_tensor("rwT", [D, 32], F32, kind="ExternalInput").ap()
    rb = nc.dram_tensor("rb", [1, 32], F32, kind="ExternalInput").ap()
    w1 = nc.dram_tensor("w1", [NE, D, 256], F32, kind="ExternalInput").ap()
    b1t = nc.dram_tensor("b1t", [128, 2 * NE], F32, kind="ExternalInput").ap()
    w2 = nc.dram_tensor("w2", [NE, 256, D], F32, kind="ExternalInput").ap()
    b2 = nc.dram_tensor("b2", [NE, D], F32, kind="ExternalInput").ap()
    masks = nc.dram_tensor("masks", [8, 128, 128], F32, kind="ExternalInput").ap()
    ltu = nc.dram_tensor("ltu", [128, 128], F32, kind="ExternalInput").ap()
    ident = nc.dram_tensor("ident", [128, 128], F32, kind="ExternalInput").ap()
    out = nc.dram_tensor("out", [2048, D], F32, kind="ExternalOutput").ap()

    xn_tm = nc.dram_tensor("xn_tm", [4096, D], BF16).ap()
    xs_dram = nc.dram_tensor("xs_dram", [NE * 192, D], BF16).ap()
    fcx_dram = nc.dram_tensor("fcx_dram", [2048, D], F32).ap()
    y2_dram = nc.dram_tensor("y2_dram", [NE * 192, D], F32).ap()

    dbg = {}
    for name in debug:
        shp = {"xn": [4096, D], "K": [128, 4 * 4096], "Q": [128, 4 * 2048],
               "Va": [128, 8 * 32 * 65], "aT": [128, 8 * 2048],
               "eid": [128, 16], "ffn": [128, NOWN * D],
               "lg": [128, 16 * 32]}[name]
        dt = F32 if name in ("eid", "lg") else (BF16 if name != "xn" else BF16)
        dbg[name] = nc.dram_tensor("dbg_" + name, shp, dt, kind="ExternalOutput").ap()

    b = Build()

    with contextlib.ExitStack() as stk:
        def sb(name, shape, dt):
            return stk.enter_context(nc.sbuf_tensor(name, shape, dt))

        def ps(name, shape, dt):
            return stk.enter_context(nc.psum_tensor(name, shape, dt))

        ident_bf = sb("ident_bf", [128, 128], BF16)
        masks_bf = sb("masks_bf", [128, 8 * 128], BF16)
        ones_bf = sb("ones_bf", [128, 512], BF16)
        eps_sb = sb("eps_sb", [128, 1], F32)
        bqkv_sb = sb("bqkv_sb", [128, 24], F32)
        b1t_sb = sb("b1t_sb", [128, 64], F32)
        rwT_bf = sb("rwT_bf", [128, 8 * 32], BF16)
        rb_bf = sb("rb_bf", [1, 32], BF16)
        bfc_bf = sb("bfc_bf", [1, D], BF16)
        iota_d = sb("iota_d", [128, 32], F32)
        x_t = [sb("x_t0", [128, D], F32), sb("x_t1", [128, D], F32)]
        xn_f = [sb("xn_f0", [128, D], F32), sb("xn_f1", [128, D], F32)]
        stats = sb("stats", [128, 32], F32)
        t1_f = sb("t1_f", [128, D], F32)
        xsq = t1_f
        A1 = sb("A1", [128, 16384], BF16)
        A2 = sb("A2", [128, 16640], BF16)
        A3 = sb("A3", [128, 16384], BF16)
        A4 = sb("A4", [128, 8704], BF16)
        K_sb = A1
        ffn_bf = A1
        Va_sb = A2[:, 0:16640]
        xno_sb = A2[:, 0:16384]
        wqkv_bf = A3[:, 0:12288]
        wfcT_bf = A3[:, 0:8192]
        w1_bf = [A3[:, 8192:10240], A3[:, 10240:12288]]
        w2_bf = [A3[:, 12288:14336], A3[:, 14336:16384]]
        Q_sb = A4[:, 0:8192]
        xnk = A4[:, 0:1024]
        xsT = [A4[:, 1024:2560], A4[:, 2560:4096]]
        y1s = [A4[:, 4096:4480], A4[:, 4480:4864]]
        _b2r_base = 4864
        b2r = [A4[0:1, 4864:5888], A4[0:1, 5888:6912]]
        O_sb = A4[:, 6912:6944]
        xnt_t = [sb("xnt0", [128, 8 * 512], BF16), sb("xnt1", [128, 8 * 512], BF16)]
        vfm = [sb("vfm0", [128, 512], BF16), sb("vfm1", [128, 512], BF16)]
        PT = [sb("PT0", [128, 1024], BF16), sb("PT1", [128, 1024], BF16)]
        aT = sb("aT", [128, 8 * 2048], BF16)
        lb_sb = sb("lb_sb", [128, 512], F32)
        stage = [sb("stage0", [128, 512], BF16), sb("stage1", [128, 512], BF16)]
        recip_sb = lb_sb
        recipb_sb = stage[0]
        logits_sb = sb("logits_sb", [128, 32], F32)
        eid_sb = sb("eid_sb", [128, 16], F32)
        ltu_bf = sb("ltu_bf", [128, 128], BF16)
        iota_a = sb("iota_a", [128, 32], F32)
        rank_t = sb("rank_t", [128, 32], F32)
        tmp32 = sb("tmp32", [128, 32], F32)
        idx_sb = sb("idx_sb", [128, 16], mybir.dt.int32)
        cnt_bf = sb("cnt_bf", [1, 16 * 32], BF16)
        y2sb = xn_f
        ffng = sb("ffng", [128, 1024], F32)
        _osb = sb("out_sb0", [128, D], F32)
        out_sb = [_osb, _osb]

        ps_st0 = ps("ps_st0", [128, 1024], F32)
        ps_st1 = ps("ps_st1", [128, 1024], F32)
        ps_av = ps("ps_av", [128, 1024], F32)
        ps_lb = ps("ps_lb", [128, 512], F32)
        ps_tbf = ps("ps_tbf", [128, 1024], BF16)
        ST = [(ps_st0, "st0a", 0), (ps_st0, "st0b", 512),
              (ps_st1, "st1a", 0), (ps_st1, "st1b", 512)]

        sems = {e: stk.enter_context(nc.semaphore("sem_" + e)) for e in ENGINES}
        block = stk.enter_context(nc.Block())

        b.op("sync", lambda e: e.dma_start(out=bqkv_sb[:], in_=bqkv),
             writes=["bqkv"], dma=True)
        b.op("gpsimd", lambda e: e.dma_start(out=ident_bf[:], in_=ident),
             writes=["ident"], dma=True)
        b.op("gpsimd", lambda e: e.dma_start(
            out=masks_bf[:], in_=masks.rearrange("r p q -> p r q")),
            writes=["masks"], dma=True)
        b.op("vector", lambda e: e.memset(ones_bf[:], 1.0), writes=["ones"])
        b.op("vector", lambda e: e.memset(eps_sb[:], 1e-5), writes=["eps"])
        b.op("sync", lambda e: e.dma_start(out=b1t_sb[:], in_=b1t),
             writes=["b1t"], dma=True)
        b.op("gpsimd", lambda e: e.dma_start(
            out=rwT_bf[:], in_=rwT.rearrange("(f p) e -> p f e", p=128)),
            writes=["rwT"], dma=True)
        b.op("gpsimd", lambda e: e.dma_start(out=rb_bf[:], in_=rb),
             writes=["rb"], dma=True)
        b.op("gpsimd", lambda e: e.dma_start(out=bfc_bf[:], in_=bfc),
             writes=["bfc"], dma=True)
        b.op("gpsimd", lambda e: e.iota(
            iota_d[:], pattern=[[-1, 32]], base=31, channel_multiplier=0,
            allow_small_or_imprecise_dtypes=True), writes=["iota"])
        b.op("gpsimd", lambda e: e.iota(
            iota_a[:], pattern=[[1, 32]], base=0, channel_multiplier=0,
            allow_small_or_imprecise_dtypes=True), writes=["iotaa"])
        b.op("gpsimd", lambda e: e.dma_start(out=ltu_bf[:], in_=ltu),
             writes=["ltu"], dma=True)


        S = stats

        def ln_tile(tt):
            bu = tt % 2
            xk, qk = f"x_t{bu}", f"xn_f{bu}"
            SB = 0 if bu == 0 else 16
            b.op("sync", lambda e, tt=tt, bu=bu: e.dma_start(
                out=x_t[bu][:], in_=x[tt * 128:(tt + 1) * 128, :]),
                writes=[xk], dma=True)
            b.op("vector", lambda e, bu=bu: e.scalar_tensor_tensor(
                out=xsq[:], in0=x_t[bu][:], scalar=1.0, in1=x_t[bu][:],
                op0=ALU.mult, op1=ALU.mult, accum_out=S[:, SB + 1:SB + 2]),
                reads=[xk], writes=["xsq" + str(bu), "s2" + str(bu)], sync_out=True)
            b.op("vector", lambda e, bu=bu: e.reduce_sum(
                out=S[:, SB + 0:SB + 1], in_=x_t[bu][:], axis=AX.X),
                reads=[xk], writes=["s1" + str(bu)], sync_out=True)
            b.op("vector", lambda e: e.tensor_scalar(
                out=S[:, SB + 2:SB + 3], in0=S[:, SB + 0:SB + 1], scalar1=1.0 / D, scalar2=None,
                op0=ALU.mult), reads=["s1" + str(bu)], writes=["mean" + str(bu)])
            b.op("vector", lambda e: e.tensor_scalar(
                out=S[:, SB + 3:SB + 4], in0=S[:, SB + 1:SB + 2], scalar1=1.0 / D, scalar2=None,
                op0=ALU.mult), reads=["s2" + str(bu)], writes=["ex2" + str(bu)])
            b.op("vector", lambda e: e.tensor_scalar(
                out=S[:, SB + 4:SB + 5], in0=S[:, SB + 2:SB + 3], scalar1=S[:, SB + 2:SB + 3], scalar2=None,
                op0=ALU.mult), reads=["mean" + str(bu)], writes=["m2" + str(bu)])
            b.op("vector", lambda e: e.tensor_sub(
                out=S[:, SB + 5:SB + 6], in0=S[:, SB + 3:SB + 4], in1=S[:, SB + 4:SB + 5]),
                reads=["ex2" + str(bu), "m2" + str(bu)], writes=["var" + str(bu)])
            b.op("scalar", lambda e: e.activation(
                out=S[:, SB + 6:SB + 7], in_=S[:, SB + 5:SB + 6], func=AF.Ln, bias=eps_sb[:]),
                reads=["var" + str(bu), "eps"], writes=["lnv" + str(bu)])
            b.op("scalar", lambda e: e.activation(
                out=S[:, SB + 7:SB + 8], in_=S[:, SB + 6:SB + 7], func=AF.Exp, scale=-0.5),
                reads=["lnv" + str(bu)], writes=["rstd" + str(bu)])
            b.op("vector", lambda e: e.tensor_scalar(
                out=S[:, SB + 8:SB + 9], in0=S[:, SB + 2:SB + 3], scalar1=S[:, SB + 7:SB + 8], scalar2=-1.0,
                op0=ALU.mult, op1=ALU.mult), reads=["mean" + str(bu), "rstd" + str(bu)], writes=["nmr" + str(bu)])
            b.op("vector", lambda e, bu=bu: e.tensor_scalar(
                out=xn_f[bu][:], in0=x_t[bu][:], scalar1=S[:, SB + 7:SB + 8],
                scalar2=S[:, SB + 8:SB + 9], op0=ALU.mult, op1=ALU.add),
                reads=[xk, "rstd" + str(bu), "nmr" + str(bu)], writes=[qk])
            b.op("gpsimd", lambda e, tt=tt, bu=bu: e.dma_start(
                out=xn_tm[tt * 128:(tt + 1) * 128, :], in_=xn_f[bu][:]),
                reads=[qk], writes=[f"xntm{tt}"], dma=True)

        for tt in range(NT):
            ln_tile(tt)

        if "xn" in dbg:
            for tt in range(NT):
                bu = tt % 2
                b.op("gpsimd", lambda e, tt=tt, bu=bu: e.dma_start(
                    out=stage[bu][:], in_=xn_tm[tt * 128:(tt + 1) * 128, 0:512]),
                    reads=[f"xntm{tt}"], writes=[f"dstg{bu}"], dma=True)
                b.op("gpsimd", lambda e, tt=tt, bu=bu: e.dma_start(
                    out=dbg["xn"][tt * 128:(tt + 1) * 128, 0:512], in_=stage[bu][:]),
                    reads=[f"dstg{bu}"], dma=True, force_await=True)
                b.op("gpsimd", lambda e, tt=tt, bu=bu: e.dma_start(
                    out=stage[bu][:], in_=xn_tm[tt * 128:(tt + 1) * 128, 512:1024]),
                    reads=[f"xntm{tt}", f"dstg{bu}x"], writes=[f"dstg{bu}"], dma=True)
                b.op("gpsimd", lambda e, tt=tt, bu=bu: e.dma_start(
                    out=dbg["xn"][tt * 128:(tt + 1) * 128, 512:1024], in_=stage[bu][:]),
                    reads=[f"dstg{bu}"], dma=True, force_await=True)

        if phases < 2:
            b.lower()
            _finish(b, block, sems)
            return nc

        for w in range(2):
            if w == 1:
                for occ in range(4):
                    for tc in range(8):
                        b.takeover(f"K1_{occ}_{tc}", [f"K0_{occ}_{tc}"])
                        b.takeover(f"Q1_{occ}_{tc}", [f"Q0_{occ}_{tc}"])
                for hw in range(8):
                    for t in range(NT):
                        b.takeover(f"Va1_{hw}_{t}", [f"Va0_{hw}_{t}"])
            for f in range(8):
                b.op("gpsimd", lambda e, f=f, w=w: e.dma_start(
                    out=wqkv_bf[:, f * 1536:(f + 1) * 1536],
                    in_=wqkvT[f * 128:(f + 1) * 128, w * 1536:(w + 1) * 1536]),
                    writes=[f"wq{f}"], dma=True)
            b.op("vector", lambda e: e.memset(Va_sb[:], 1.0),
                 writes=[f"Va{w}_{hw}_{t}" for hw in range(8) for t in range(NT)])

            for tc in range(8):
                bu = tc % 2
                for f in range(8):
                    b.op("sync", lambda e, tc=tc, f=f, bu=bu: e.dma_start(
                        out=xnt_t[bu][:, f * 512:(f + 1) * 512],
                        in_=xn_tm[tc * 512:(tc + 1) * 512, f * 128:(f + 1) * 128],
                        transpose=True),
                        reads=[f"xntm{4 * tc + j}" for j in range(4)],
                        writes=[f"xnt{bu}_{f}"], dma=True)
                for oc in range(12):
                    pst, pk, po = ST[oc % 4]
                    comp = oc // 4
                    occ = oc % 4
                    gcol = w * 12 + oc
                    mm = []
                    for f in range(8):
                        mm.append(b.op("tensor", lambda e, f=f, oc=oc, pst=pst,
                                       po=po, bu=bu: e.matmul(
                            pst[:, po:po + 512],
                            wqkv_bf[:, f * 1536 + oc * 128: f * 1536 + (oc + 1) * 128],
                            xnt_t[bu][:, f * 512:(f + 1) * 512],
                            start=(f == 0), stop=(f == 7)),
                            reads=[f"wq{f}", f"xnt{bu}_{f}"], writes=[pk]))
                    if comp == 0:
                        b.op("vector", lambda e, pst=pst, po=po, occ=occ, tc=tc,
                             gcol=gcol, w=w: e.tensor_scalar(
                            out=Q_sb[:, occ * 2048 + tc * 256: occ * 2048 + (tc + 1) * 256]
                                .rearrange("p (a c) -> p a c", a=2),
                            in0=pst[:, po:po + 512].rearrange(
                                "p (a c) -> p a c", a=4)[:, 0::2, :],
                            scalar1=bqkv_sb[:, gcol:gcol + 1], scalar2=None,
                            op0=ALU.add),
                            reads=[pk, "bqkv"], writes=[f"Q{w}_{occ}_{tc}"])
                    elif comp == 1:
                        b.op("vector", lambda e, pst=pst, po=po, occ=occ, tc=tc,
                             gcol=gcol: e.tensor_scalar(
                            out=K_sb[:, occ * 4096 + tc * 512: occ * 4096 + (tc + 1) * 512],
                            in0=pst[:, po:po + 512],
                            scalar1=bqkv_sb[:, gcol:gcol + 1], scalar2=None,
                            op0=ALU.add),
                            reads=[pk, "bqkv"], writes=[f"K{w}_{occ}_{tc}"])
                    else:
                        vb = tc % 2
                        b.op("vector", lambda e, pst=pst, po=po, vb=vb,
                             gcol=gcol: e.tensor_scalar(
                            out=vfm[vb][:], in0=pst[:, po:po + 512],
                            scalar1=bqkv_sb[:, gcol:gcol + 1], scalar2=None,
                            op0=ALU.add),
                            reads=[pk, "bqkv"], writes=[f"vfm{vb}"])
                        for j in range(4):
                            tt = 4 * tc + j
                            hw0 = occ * 2
                            b.op("tensor", lambda e, vb=vb, j=j: e.transpose(
                                ps_tbf[:, 0:128], vfm[vb][:, j * 128:(j + 1) * 128],
                                ident_bf[:]),
                                reads=[f"vfm{vb}", "ident"], writes=["tbf"])
                            for hh in range(2):
                                hw = hw0 + hh
                                off = (hw * NT + tt) * 65
                                b.op("vector", lambda e, hh=hh, off=off: e.tensor_copy(
                                    out=Va_sb[:, off:off + 64],
                                    in_=ps_tbf[:, hh * 64:hh * 64 + 64]),
                                    reads=["tbf"],
                                    writes=[f"Va{w}_{hw0 + hh}_{tt}"])

            if "K" in dbg and w == 0:
                b.op("sync", lambda e: e.dma_start(out=dbg["K"], in_=K_sb[:]),
                     reads=[f"K{w}_{occ}_{tc}" for occ in range(4) for tc in range(8)],
                     dma=True, force_await=True)
            if "Q" in dbg and w == 0:
                b.op("sync", lambda e: e.dma_start(out=dbg["Q"], in_=Q_sb[:]),
                     reads=[f"Q{w}_{occ}_{tc}" for occ in range(4) for tc in range(8)],
                     dma=True, force_await=True)
            if "Va" in dbg and w == 0:
                b.op("sync", lambda e: e.dma_start(out=dbg["Va"], in_=Va_sb[:]),
                     reads=[f"Va{w}_{hw}_{t}" for hw in range(8) for t in range(NT)],
                     dma=True, force_await=True)

            if phases < 3:
                break

            for hp in range(4):
                for c in range(4):
                    nkt = 8 * c + 8
                    fch = w * 4 + hp

                    def emit_st(kt, hp=hp, c=c, w=w):
                        live = 512 if kt < 8 * c else LIVE[kt - 8 * c]
                        qoff = 512 - live
                        pst = ps_st0 if kt % 2 == 0 else ps_st1
                        pkeys = [("st0a", "st0b"), ("st1a", "st1b")][kt % 2]
                        for j in range(2):
                            b.op("tensor", lambda e, pst=pst, j=j, hp=hp, kt=kt,
                                 c=c, qoff=qoff: e.matmul(
                                pst[:, j * 512 + qoff: (j + 1) * 512],
                                K_sb[j * 64:(j + 1) * 64,
                                     hp * 4096 + kt * 128: hp * 4096 + (kt + 1) * 128],
                                Q_sb[j * 64:(j + 1) * 64,
                                     hp * 2048 + c * 512 + qoff: hp * 2048 + (c + 1) * 512],
                                start=True, stop=True,
                                tile_position=(64 * j, 0)),
                                reads=[f"K{w}_{hp}_{kt // 4}",
                                       f"Q{w}_{hp}_{2 * c}", f"Q{w}_{hp}_{2 * c + 1}"],
                                writes=[pkeys[j]])

                    emit_st(0)
                    for kt in range(nkt):
                        live = 512 if kt < 8 * c else LIVE[kt - 8 * c]
                        qoff = 512 - live
                        pst, ptb = (ps_st0, 0) if kt % 2 == 0 else (ps_st1, 1)
                        pkeys = [("st0a", "st0b"), ("st1a", "st1b")][ptb]
                        b.op("scalar", lambda e, pst=pst, ptb=ptb: e.activation(
                            out=PT[ptb][:], in_=pst[:], func=AF.Exp, scale=0.125),
                            reads=list(pkeys), writes=[f"pt{ptb}"])
                        if kt + 1 < nkt:
                            emit_st(kt + 1)
                        if kt >= 8 * c:
                            r = kt - 8 * c
                            for j in range(2):
                                b.op("vector", lambda e, ptb=ptb, j=j, qoff=qoff,
                                     r=r: e.tensor_mul(
                                    out=PT[ptb][:, j * 512 + qoff: j * 512 + qoff + 128],
                                    in0=PT[ptb][:, j * 512 + qoff: j * 512 + qoff + 128],
                                    in1=masks_bf[:, r * 128:(r + 1) * 128]),
                                    reads=[f"pt{ptb}", "masks"], writes=[f"pt{ptb}"])
                        for j in range(2):
                            hw = 2 * hp + j
                            b.op("tensor", lambda e, hw=hw, kt=kt, ptb=ptb, j=j,
                                 qoff=qoff, c=c, nkt=nkt: e.matmul(
                                ps_av[0:65, j * 512 + qoff: (j + 1) * 512],
                                Va_sb[:, (hw * NT + kt) * 65: (hw * NT + kt) * 65 + 65],
                                PT[ptb][:, j * 512 + qoff: (j + 1) * 512],
                                start=(kt == 0), stop=(kt == nkt - 1)),
                                reads=[f"pt{ptb}", f"Va{w}_{hw}_{kt}"],
                                writes=[f"av{j}"])
                    for j in range(2):
                        hv = f"av{j}"
                        b.op("scalar", lambda e, j=j: e.activation(
                            out=recip_sb[64:65, 0:512],
                            in_=ps_av[64:65, j * 512:(j + 1) * 512], func=AF.Ln),
                            reads=[hv], writes=["recip"])
                        b.op("scalar", lambda e: e.activation(
                            out=recipb_sb[64:65, 0:512], in_=recip_sb[64:65, 0:512],
                            func=AF.Exp, scale=-1.0),
                            reads=["recip"], writes=["recipb"])
                        b.op("tensor", lambda e: e.matmul(
                            ps_lb[0:64, :], ones_bf[64:65, 0:64],
                            recipb_sb[64:65, 0:512], start=True, stop=True),
                            reads=["recipb", "ones"], writes=["lb"])
                        b.op("vector", lambda e: e.tensor_copy(
                            out=lb_sb[0:64, :], in_=ps_lb[0:64, :]),
                            reads=["lb"], writes=["lbs"])
                        if j == 0:
                            b.op("vector", lambda e, j=j, fch=fch, c=c: e.tensor_mul(
                                out=aT[0:64, fch * 2048 + c * 512: fch * 2048 + (c + 1) * 512],
                                in0=ps_av[0:64, j * 512:(j + 1) * 512],
                                in1=lb_sb[0:64, :]),
                                reads=[hv, "lbs"], writes=[f"aT_{fch}_{c}L"])
                        else:
                            sgb = (hp * 4 + c) % 2
                            b.op("vector", lambda e, j=j, sgb=sgb: e.tensor_mul(
                                out=stage[sgb][0:64, :],
                                in0=ps_av[0:64, j * 512:(j + 1) * 512],
                                in1=lb_sb[0:64, :]),
                                reads=[hv, "lbs"], writes=[f"stg{sgb}"])
                            b.op("tensor", lambda e, sgb=sgb: e.matmul(
                                ps_av[64:128, 0:512], ident_bf[0:64, 0:64],
                                stage[sgb][0:64, :], start=True, stop=True,
                                tile_position=(0, 64)),
                                reads=[f"stg{sgb}", "ident"],
                                writes=["avhop", "av0"])
                            b.op("vector", lambda e, fch=fch, c=c: e.tensor_copy(
                                out=aT[64:128, fch * 2048 + c * 512: fch * 2048 + (c + 1) * 512],
                                in_=ps_av[64:128, 0:512]),
                                reads=["avhop", "av0"], writes=[f"aT_{fch}_{c}H"])

        if "aT" in dbg:
            b.op("sync", lambda e: e.dma_start(out=dbg["aT"], in_=aT[:]),
                 reads=[f"aT_{fc_}_{c}{h}" for fc_ in range(8) for c in range(4)
                        for h in "LH"],
                 dma=True, force_await=True)

        if phases < 4:
            b.lower()
            _finish(b, block, sems)
            return nc

        _va_keys = [f"Va{w}_{hw}_{t}" for w in range(2) for hw in range(8)
                    for t in range(NT)]
        for f in range(8):
            for s_ in range(NOWN):
                b.takeover(f"xno_{f}_{s_}", _va_keys)
        _q_keys = [f"Q{w}_{occ}_{tc}" for w in range(2) for occ in range(4)
                   for tc in range(8)]
        for kk in (["xnk", "xsT0", "xsT1", "y1s0", "y1s1", "b2r0", "b2r1", "O"]):
            b.takeover(kk, _q_keys)
        for f in range(8):
            for s in range(NOWN):
                b.op("sync", lambda e, f=f, s=s: e.dma_start(
                    out=xno_sb[:, f * 2048 + s * 128: f * 2048 + (s + 1) * 128],
                    in_=xn_tm[(2 * s) * 128:(2 * s + 1) * 128, f * 128:(f + 1) * 128],
                    transpose=True),
                    reads=[f"xntm{2 * s}"], writes=[f"xno_{f}_{s}"], dma=True)

        for s in range(NOWN):
            for f in range(8):
                b.op("tensor", lambda e, f=f, s=s: e.matmul(
                    ps_lb[:, 0:32],
                    xno_sb[:, f * 2048 + s * 128: f * 2048 + (s + 1) * 128],
                    rwT_bf[:, f * 32:(f + 1) * 32],
                    start=(f == 0), stop=False),
                    reads=[f"xno_{f}_{s}", "rwT"], writes=["lb"])
            b.op("tensor", lambda e: e.matmul(
                ps_lb[:, 0:32], ones_bf[0:1, 0:128], rb_bf[0:1, :],
                start=False, stop=True),
                reads=["ones", "rb"], writes=["lb"])
            b.op("vector", lambda e: e.tensor_copy(
                out=logits_sb[:], in_=ps_lb[:, 0:32]),
                reads=["lb"], writes=["lg"])
            if "lg" in dbg:
                b.op("sync", lambda e, s=s: e.dma_start(
                    out=dbg["lg"][:, s * 32:(s + 1) * 32], in_=logits_sb[:]),
                    reads=["lg"], dma=True, force_await=True)
            b.op("vector", lambda e: e.reduce_max(
                out=S[:, 10:11], in_=logits_sb[:], axis=AX.X),
                reads=["lg"], writes=["lmax"], sync_out=True)
            b.op("vector", lambda e: e.tensor_scalar(
                out=logits_sb[:], in0=logits_sb[:], scalar1=S[:, 10:11],
                scalar2=None, op0=ALU.is_ge), reads=["lg", "lmax"], writes=["lg"])
            b.op("vector", lambda e: e.tensor_mul(
                out=logits_sb[:], in0=logits_sb[:], in1=iota_d[:]),
                reads=["lg", "iota"], writes=["lg"])
            b.op("vector", lambda e: e.reduce_max(
                out=S[:, 11:12], in_=logits_sb[:], axis=AX.X),
                reads=["lg"], writes=["l31"], sync_out=True)
            b.op("vector", lambda e, s=s: e.tensor_scalar(
                out=eid_sb[:, s:s + 1], in0=S[:, 11:12], scalar1=-1.0,
                scalar2=31.0, op0=ALU.mult, op1=ALU.add),
                reads=["l31"], writes=[f"eid{s}"])

        if "eid" in dbg:
            b.op("sync", lambda e: e.dma_start(out=dbg["eid"], in_=eid_sb[:]),
                 reads=[f"eid{s}" for s in range(NOWN)], dma=True,
                 force_await=True)

        if phases < 5:
            b.lower()
            _finish(b, block, sems)
            return nc

        for s in range(NOWN):
            b.op("vector", lambda e, s=s: e.tensor_scalar(
                out=O_sb, in0=iota_a[:], scalar1=eid_sb[:, s:s + 1],
                scalar2=None, op0=ALU.is_equal),
                reads=[f"eid{s}", "iotaa"], writes=["O"])
            b.op("tensor", lambda e, s=s: e.matmul(
                ps_st1[:, 0:32], ltu_bf[:], O_sb, start=True, stop=(s == 0)),
                reads=["O", "ltu"], writes=["st1a"])
            for sp in range(s):
                b.op("tensor", lambda e, sp=sp, s=s: e.matmul(
                    ps_st1[:, 0:32], ones_bf[0:1, 0:128],
                    cnt_bf[0:1, sp * 32:(sp + 1) * 32],
                    start=False, stop=(sp == s - 1)),
                    reads=[f"cnt{sp}", "ones"], writes=["st1a"])
            b.op("tensor", lambda e: e.matmul(
                ps_st1[0:1, 512:544], ones_bf[:, 0:1], O_sb,
                start=True, stop=True), reads=["O", "ones"], writes=["st1b"])
            b.op("vector", lambda e: e.tensor_copy(out=rank_t[:],
                                                   in_=ps_st1[:, 0:32]),
                 reads=["st1a"], writes=["rankt"])
            b.op("vector", lambda e: e.tensor_mul(out=tmp32[:], in0=rank_t[:],
                                                  in1=O_sb),
                 reads=["rankt", "O"], writes=["tmpr"])
            b.op("vector", lambda e: e.reduce_sum(
                out=S[:, 12:13], in_=tmp32[:], axis=AX.X),
                reads=["tmpr"], writes=["slotr"], sync_out=True)
            b.op("vector", lambda e, s=s: e.scalar_tensor_tensor(
                out=S[:, 13:14], in0=eid_sb[:, s:s + 1], scalar=192.0,
                in1=S[:, 12:13], op0=ALU.mult, op1=ALU.add),
                reads=[f"eid{s}", "slotr"], writes=["slotf"])
            b.op("vector", lambda e, s=s: e.tensor_copy(
                out=idx_sb[:, s:s + 1], in_=S[:, 13:14]),
                reads=["slotf"], writes=[f"idx{s}"])
            b.op("vector", lambda e, s=s: e.tensor_copy(
                out=cnt_bf[0:1, s * 32:(s + 1) * 32], in_=ps_st1[0:1, 512:544]),
                reads=["st1b"], writes=[f"cnt{s}"])
            b.op("gpsimd", lambda e, s=s: e.dma_start(
                out=xnk, in_=xn_tm[(2 * s) * 128:(2 * s + 1) * 128, :]),
                reads=[f"xntm{2 * s}"], writes=["xnk"], dma=True)
            b.op("gpsimd", lambda e, s=s: e.indirect_dma_start(
                out=xs_dram[0:128, :],
                out_offset=IndirectOffsetOnAxis(idx_sb[:, s:s + 1], 0),
                in_=xnk, in_offset=None),
                reads=["xnk", f"idx{s}"], writes=[f"xs{s}"], dma=True)

        if phases < 6:
            b.lower()
            _finish(b, block, sems)
            return nc

        b.takeover("y2s0", ["xn_f0"])
        b.takeover("y2s1", ["xn_f1"])
        _xs_keys = [f"xs{s}" for s in range(NOWN)]
        t1x = [t1_f, out_sb[0]]
        for kk in ("t1x0", "t1x1"):
            b.takeover(kk, ["t1"])
        for f in range(8):
            b.op("gpsimd", lambda e, f=f: e.dma_start(
                out=wfcT_bf[:, f * 1024:(f + 1) * 1024],
                in_=wfcT[f * 128:(f + 1) * 128, :]),
                writes=[f"wfc{f}"], dma=True)

        def emit_fc(s_):
            bu = s_ % 2
            c = s_ // 4
            b.op("sync", lambda e, s_=s_, bu=bu: e.dma_start(
                out=x_t[bu][:], in_=x[(2 * s_) * 128:(2 * s_ + 1) * 128, :]),
                writes=[f"x_t{bu}"], dma=True)
            for half in range(2):
                for f in range(8):
                    b.op("tensor", lambda e, f=f, half=half, s_=s_, c=c: e.matmul(
                        ps_lb[:, 0:512],
                        aT[:, f * 2048 + s_ * 128: f * 2048 + (s_ + 1) * 128],
                        wfcT_bf[:, f * 1024 + half * 512: f * 1024 + (half + 1) * 512],
                        start=(f == 0), stop=False),
                        reads=[f"wfc{f}", f"aT_{f}_{c}L", f"aT_{f}_{c}H"],
                        writes=["lb"])
                b.op("tensor", lambda e, half=half: e.matmul(
                    ps_lb[:, 0:512], ones_bf[0:1, 0:128],
                    bfc_bf[0:1, half * 512:(half + 1) * 512],
                    start=False, stop=True),
                    reads=["ones", "bfc"], writes=["lb"])
                b.op("vector", lambda e, half=half, bu=bu: e.tensor_add(
                    out=t1x[bu][:, half * 512:(half + 1) * 512],
                    in0=ps_lb[:, 0:512],
                    in1=x_t[bu][:, half * 512:(half + 1) * 512]),
                    reads=["lb", f"x_t{bu}"], writes=[f"t1x{bu}"])
            b.op("sync", lambda e, s_=s_, bu=bu: e.dma_start(
                out=fcx_dram[s_ * 128:(s_ + 1) * 128, :], in_=t1x[bu][:]),
                reads=[f"t1x{bu}"], writes=[f"fcx{s_}"], dma=True)

        for e_ in range(NE):
            wb = e_ % 2
            w1k, w2k = f"w1b{wb}", f"w2b{wb}"
            b.op("gpsimd", lambda e, e_=e_, wb=wb: e.dma_start(
                out=w1_bf[wb][:], in_=w1[e_, :, :].rearrange(
                    "(f p) h -> p f h", p=128)),
                writes=[w1k], dma=True)
            b.op("gpsimd", lambda e, e_=e_, wb=wb: e.dma_start(
                out=w2_bf[wb][:], in_=w2[e_, :, :].rearrange(
                    "(hc p) o -> p hc o", p=128)),
                writes=[w2k], dma=True)
            b.op("gpsimd", lambda e, e_=e_, wb=wb: e.dma_start(
                out=b2r[wb], in_=b2[e_:e_ + 1, :]),
                writes=[f"b2r{wb}"], dma=True)
            for f in range(8):
                b.op("scalar", lambda e, e_=e_, f=f, wb=wb: e.dma_start(
                    out=xsT[wb][:, f * 192:(f + 1) * 192],
                    in_=xs_dram[e_ * 192:(e_ + 1) * 192, f * 128:(f + 1) * 128],
                    transpose=True),
                    reads=_xs_keys, writes=[f"xsT{wb}_{f}"], dma=True)
            for hc in range(2):
                po_y1 = hc * 512
                for f in range(8):
                    b.op("tensor", lambda e, hc=hc, f=f, wb=wb,
                         po_y1=po_y1: e.matmul(
                        ps_av[:, po_y1:po_y1 + 192],
                        w1_bf[wb][:, f * 256 + hc * 128: f * 256 + (hc + 1) * 128],
                        xsT[wb][:, f * 192:(f + 1) * 192],
                        start=(f == 0), stop=(f == 7)),
                        reads=[w1k, f"xsT{wb}_{f}"], writes=[f"av{hc}"])
                b.op("vector", lambda e, hc=hc, e_=e_, po_y1=po_y1,
                     wb=wb: e.tensor_scalar(
                    out=y1s[wb][:, hc * 192:(hc + 1) * 192],
                    in0=ps_av[:, po_y1:po_y1 + 192],
                    scalar1=b1t_sb[:, 2 * e_ + hc: 2 * e_ + hc + 1],
                    scalar2=0.0, op0=ALU.add, op1=ALU.max),
                    reads=[f"av{hc}", "b1t"], writes=[f"y1s{wb}"])
            for tl in range(2):
                nsl = 128 if tl == 0 else 64
                pst = ps_st0 if tl == 0 else ps_st1
                pk = [("st0a", "st0b"), ("st1a", "st1b")][tl]
                for half in range(2):
                    for hc in range(2):
                        b.op("tensor", lambda e, tl=tl, half=half, hc=hc,
                             pst=pst, wb=wb, nsl=nsl: e.matmul(
                            pst[0:nsl, half * 512:(half + 1) * 512],
                            y1s[wb][:, hc * 192 + tl * 128: hc * 192 + tl * 128 + nsl],
                            w2_bf[wb][:, hc * 1024 + half * 512: hc * 1024 + (half + 1) * 512],
                            start=(hc == 0), stop=False),
                            reads=[f"y1s{wb}", w2k], writes=[pk[half]])
                    b.op("tensor", lambda e, half=half, pst=pst, wb=wb,
                         nsl=nsl: e.matmul(
                        pst[0:nsl, half * 512:(half + 1) * 512],
                        ones_bf[0:1, 0:nsl],
                        b2r[wb][0:1, half * 512:(half + 1) * 512],
                        start=False, stop=True),
                        reads=["ones", f"b2r{wb}"], writes=[pk[half]])
                yb = tl
                b.op("vector", lambda e, pst=pst, nsl=nsl, yb=yb: e.tensor_scalar(
                    out=y2sb[yb][0:nsl, :], in0=pst[0:nsl, :],
                    scalar1=0.0, scalar2=None, op0=ALU.max),
                    reads=list(pk), writes=[f"y2s{yb}"])
                b.op("sync", lambda e, e_=e_, tl=tl, nsl=nsl, yb=yb: e.dma_start(
                    out=y2_dram[e_ * 192 + tl * 128: e_ * 192 + tl * 128 + nsl, :],
                    in_=y2sb[yb][0:nsl, :]),
                    reads=[f"y2s{yb}"], writes=[f"y2d_{e_}_{tl}"], dma=True)
            if e_ % 2 == 1:
                emit_fc(e_ // 2)

        if phases < 7:
            b.lower()
            _finish(b, block, sems)
            return nc

        _y2_keys = [f"y2d_{e_}_{tl}" for e_ in range(NE) for tl in range(2)]
        ffg = [ffng, xn_f[0]]
        b.takeover("ffg1", ["y2s0"])
        for s_ in range(NOWN):
            bu = s_ % 2
            b.op("sync", lambda e, s_=s_, bu=bu: e.dma_start(
                out=x_t[bu][:], in_=fcx_dram[s_ * 128:(s_ + 1) * 128, :]),
                reads=[f"fcx{s_}"], writes=[f"x_t{bu}"], dma=True)
            b.op("gpsimd", lambda e, s_=s_, bu=bu: e.indirect_dma_start(
                out=ffg[bu][:, 0:1024], out_offset=None,
                in_=y2_dram[0:128, :],
                in_offset=IndirectOffsetOnAxis(idx_sb[:, s_:s_ + 1], 0)),
                reads=_y2_keys + [f"idx{s_}"], writes=[f"ffg{bu}"], dma=True)
            b.op("vector", lambda e, s_=s_, bu=bu: e.tensor_add(
                out=out_sb[1][:], in0=x_t[bu][:], in1=ffg[bu][:, 0:1024]),
                reads=[f"x_t{bu}", f"ffg{bu}"], writes=["outsb", "t1x1"])
            b.op("sync", lambda e, s_=s_: e.dma_start(
                out=out[s_ * 128:(s_ + 1) * 128, :], in_=out_sb[1][:]),
                reads=["outsb"], writes=[f"odma{s_}"], dma=True,
                force_await=True)

        b.lower()
        _finish(b, block, sems)
    return nc


def _finish(b, block, sems):
    @block.sync
    def _(e):
        b.emit("sync", e, sems)
        for en in ENGINES:
            if b.final[en]:
                e.wait_ge(sems[en], b.final[en])

    @block.gpsimd
    def _(e):
        b.emit("gpsimd", e, sems)

    @block.vector
    def _(e):
        b.emit("vector", e, sems)

    @block.scalar
    def _(e):
        b.emit("scalar", e, sems)

    @block.tensor
    def _(e):
        b.emit("tensor", e, sems)



def _core_inputs(inputs, core):
    bidx, par = core // 2, core % 2
    x = np.asarray(inputs["L0_attn_x"], np.float32)[bidx]
    if par == 1:
        xr = x.reshape(32, 128, D)
        perm = np.arange(32)
        perm = perm + 1 - 2 * (perm % 2)
        x = np.ascontiguousarray(xr[perm].reshape(4096, D))
    else:
        x = np.ascontiguousarray(x)

    w_qkv = np.asarray(inputs["L0_attn_w_qkv"], np.float32)
    b_qkv = np.asarray(inputs["L0_attn_b_qkv"], np.float32)
    rows = []
    for w in range(2):
        for comp in range(3):
            for hw in range(8):
                h = w * 8 + hw
                base = comp * 1024 + h * 64
                rows.append(np.arange(base, base + 64))
    rows = np.concatenate(rows)
    wqkvT = np.ascontiguousarray(w_qkv[rows].T)
    bq = np.ascontiguousarray(b_qkv[rows].reshape(24, 128).T)

    w_fc = np.asarray(inputs["L0_attn_w_fc"], np.float32)
    wfcT = np.ascontiguousarray(w_fc.T)
    bfc = np.ascontiguousarray(inputs["L0_attn_b_fc"], dtype=np.float32).reshape(1, D)

    rwT = np.ascontiguousarray(np.asarray(inputs["L0_ffn_router_w"], np.float32).T)
    rb = np.ascontiguousarray(np.asarray(inputs["L0_ffn_router_b"], np.float32).reshape(1, 32))
    w1 = np.ascontiguousarray(np.asarray(inputs["L0_ffn_weights1"], np.float32))
    b1 = np.asarray(inputs["L0_ffn_bias1"], np.float32)
    b1t = np.ascontiguousarray(
        b1.reshape(NE, 2, 128).transpose(2, 0, 1).reshape(128, 2 * NE))
    w2 = np.ascontiguousarray(np.asarray(inputs["L0_ffn_weights2"], np.float32))
    b2 = np.ascontiguousarray(np.asarray(inputs["L0_ffn_bias2"], np.float32))

    mk = np.zeros((8, 128, 128), np.float32)
    tri = np.triu(np.ones((128, 128), np.float32))
    for r in range(8):
        if r % 2 == 0:
            mk[r] = tri
        else:
            mk[r] = 0.0 if par == 0 else 1.0

    return {"x": x, "wqkvT": wqkvT, "bqkv": bq, "wfcT": wfcT, "bfc": bfc,
            "rwT": rwT, "rb": rb, "w1": w1, "b1t": b1t, "w2": w2, "b2": b2,
            "masks": np.ascontiguousarray(mk),
            "ident": np.eye(128, dtype=np.float32),
            "ltu": np.triu(np.ones((128, 128), np.float32), 1)}


def kernel(**inputs):
    nc = build_nc()
    in_maps = [_core_inputs(inputs, c) for c in range(8)]
    res = run_bass_kernel_spmd(nc, in_maps, core_ids=list(range(8)))
    out = np.empty((4, 4096, D), np.float32)
    for c in range(8):
        bidx, par = c // 2, c % 2
        o = res.results[c]["out"].reshape(NOWN, 128, D)
        ob = out[bidx].reshape(32, 128, D)
        for s in range(NOWN):
            ob[2 * s + par] = o[s]
    return out


# version 17
# speedup vs baseline: 2.1805x; 1.0099x over previous
import contextlib
import numpy as np

import concourse.bass as bass
import concourse.mybir as mybir
from concourse.bass_core_rust import IndirectOffsetOnAxis
from concourse.bass_utils import run_bass_kernel_spmd

F32 = mybir.dt.float32
BF16 = mybir.dt.bfloat16
AF = mybir.ActivationFunctionType
ALU = mybir.AluOpType
AX = mybir.AxisListType

ENGINES = ("sync", "gpsimd", "vector", "scalar", "tensor")
LIVE = [512, 512, 384, 384, 256, 256, 128, 128]
D = 1024
NT = 32
NOWN = 16
NH = 16
NE = 32
PHASES_ALL = 99


class Build:

    def __init__(self):
        self.ops = []
        self.last_w = {}
        self.readers = {}

    def op(self, eng, fn, reads=(), writes=(), dma=False, force_await=False,
           sync_out=False):
        deps = set()
        for k in reads:
            w = self.last_w.get(k)
            if w is not None:
                deps.add(w)
        for k in writes:
            w = self.last_w.get(k)
            if w is not None:
                deps.add(w)
            for r in self.readers.get(k, ()):
                deps.add(r)
        i = len(self.ops)
        self.ops.append({"eng": eng, "fn": fn, "dma": dma, "deps": deps,
                         "force": force_await, "sync_out": sync_out})
        for k in reads:
            self.readers.setdefault(k, set()).add(i)
        for k in writes:
            self.last_w[k] = i
            self.readers[k] = set()
        return i

    def takeover(self, new_key, old_keys):
        rd = set()
        w = None
        for ok in old_keys:
            rd |= self.readers.get(ok, set())
            lw = self.last_w.get(ok)
            if lw is not None and (w is None or lw > w):
                w = lw
        if w is not None:
            self.last_w[new_key] = w
        self.readers[new_key] = rd

    NLANES = 4

    def lower(self):
        seq = {e: [] for e in ENGINES}
        lane_ctr = {}
        for i, o in enumerate(self.ops):
            seq[o["eng"]].append(i)
            if o["dma"]:
                n = lane_ctr.get(o["eng"], 0)
                lane_ctr[o["eng"]] = n + 1
                o["sem"] = f'{o["eng"]}_d{n % self.NLANES}'
            else:
                o["sem"] = o["eng"]
        def _edge_needed(od, o):
            if od["eng"] != o["eng"] or od["dma"]:
                return True
            if od["eng"] == "tensor":
                return False
            return True

        awaited = set(i for i, o in enumerate(self.ops) if o["force"])
        for o in self.ops:
            for d in o["deps"]:
                if _edge_needed(self.ops[d], o):
                    awaited.add(d)
        cnt = {}
        for e in ENGINES:
            for i in seq[e]:
                o = self.ops[i]
                if o["dma"]:
                    o["inc"] = 16
                elif i in awaited:
                    o["inc"] = 1
                else:
                    o["inc"] = 0
                c = cnt.get(o["sem"], 0) + o["inc"]
                cnt[o["sem"]] = c
                o["cum"] = c
        for o in self.ops:
            w = {}
            for d in o["deps"]:
                od = self.ops[d]
                if not _edge_needed(od, o):
                    continue
                w[od["sem"]] = max(w.get(od["sem"], 0), od["cum"])
            o["waits"] = w
        self.seq = seq
        self.final = dict(cnt)
        self.sem_names = sorted(set(cnt) | set(ENGINES))

    def emit(self, ename, eng, sems):
        waited = {}
        for i in self.seq[ename]:
            o = self.ops[i]
            for de in sorted(o["waits"]):
                v = o["waits"][de]
                if waited.get(de, -1) < v:
                    eng.wait_ge(sems[de], v)
                    waited[de] = v
            inst = o["fn"](eng)
            if o["inc"]:
                inst.then_inc(sems[o["sem"]], o["inc"])


def build_nc(phases=PHASES_ALL, debug=()):
    nc = bass.Bass("TRN2", target_bir_lowering=False)

    x = nc.dram_tensor("x", [4096, D], F32, kind="ExternalInput").ap()
    wqkvT = nc.dram_tensor("wqkvT", [D, 3072], F32, kind="ExternalInput").ap()
    bqkv = nc.dram_tensor("bqkv", [128, 24], F32, kind="ExternalInput").ap()
    wfcT = nc.dram_tensor("wfcT", [D, D], F32, kind="ExternalInput").ap()
    bfc = nc.dram_tensor("bfc", [1, D], F32, kind="ExternalInput").ap()
    rwT = nc.dram_tensor("rwT", [D, 32], F32, kind="ExternalInput").ap()
    rb = nc.dram_tensor("rb", [1, 32], F32, kind="ExternalInput").ap()
    w1 = nc.dram_tensor("w1", [NE, D, 256], F32, kind="ExternalInput").ap()
    b1t = nc.dram_tensor("b1t", [128, 2 * NE], F32, kind="ExternalInput").ap()
    w2 = nc.dram_tensor("w2", [NE, 256, D], F32, kind="ExternalInput").ap()
    b2 = nc.dram_tensor("b2", [NE, D], F32, kind="ExternalInput").ap()
    masks = nc.dram_tensor("masks", [8, 128, 128], F32, kind="ExternalInput").ap()
    ltu = nc.dram_tensor("ltu", [128, 128], F32, kind="ExternalInput").ap()
    ident = nc.dram_tensor("ident", [128, 128], F32, kind="ExternalInput").ap()
    out = nc.dram_tensor("out", [2048, D], F32, kind="ExternalOutput").ap()

    xn_tm = nc.dram_tensor("xn_tm", [4096, D], BF16).ap()
    xs_dram = nc.dram_tensor("xs_dram", [NE * 192, D], BF16).ap()
    fcx_dram = nc.dram_tensor("fcx_dram", [2048, D], F32).ap()
    y2_dram = nc.dram_tensor("y2_dram", [NE * 192, D], F32).ap()

    dbg = {}
    for name in debug:
        shp = {"xn": [4096, D], "K": [128, 4 * 4096], "Q": [128, 4 * 2048],
               "Va": [128, 8 * 32 * 65], "aT": [128, 8 * 2048],
               "eid": [128, 16], "ffn": [128, NOWN * D],
               "lg": [128, 16 * 32]}[name]
        dt = F32 if name in ("eid", "lg") else (BF16 if name != "xn" else BF16)
        dbg[name] = nc.dram_tensor("dbg_" + name, shp, dt, kind="ExternalOutput").ap()

    b = Build()

    with contextlib.ExitStack() as stk:
        def sb(name, shape, dt):
            return stk.enter_context(nc.sbuf_tensor(name, shape, dt))

        def ps(name, shape, dt):
            return stk.enter_context(nc.psum_tensor(name, shape, dt))

        ident_bf = sb("ident_bf", [128, 128], BF16)
        masks_bf = sb("masks_bf", [128, 8 * 128], BF16)
        ones_bf = sb("ones_bf", [128, 512], BF16)
        eps_sb = sb("eps_sb", [128, 1], F32)
        bqkv_sb = sb("bqkv_sb", [128, 24], F32)
        b1t_sb = sb("b1t_sb", [128, 64], F32)
        rwT_bf = sb("rwT_bf", [128, 8 * 32], BF16)
        rb_bf = sb("rb_bf", [1, 32], BF16)
        bfc_bf = sb("bfc_bf", [1, D], BF16)
        iota_d = sb("iota_d", [128, 32], F32)
        x_t = [sb("x_t0", [128, D], F32), sb("x_t1", [128, D], F32)]
        xn_f = [sb("xn_f0", [128, D], F32), sb("xn_f1", [128, D], F32)]
        stats = sb("stats", [128, 32], F32)
        t1_f = sb("t1_f", [128, D], F32)
        xsq = t1_f
        A1 = sb("A1", [128, 16384], BF16)
        A2 = sb("A2", [128, 16640], BF16)
        A3 = sb("A3", [128, 16384], BF16)
        A4 = sb("A4", [128, 8704], BF16)
        K_sb = A1
        ffn_bf = A1
        Va_sb = A2[:, 0:16640]
        xno_sb = A2[:, 0:16384]
        wqkv_bf = A3[:, 0:12288]
        wfcT_bf = A3[:, 0:8192]
        w1_bf = [A3[:, 8192:10240], A3[:, 10240:12288]]
        w2_bf = [A3[:, 12288:14336], A3[:, 14336:16384]]
        Q_sb = A4[:, 0:8192]
        xnk = A4[:, 0:1024]
        xsT = [A4[:, 1024:2560], A4[:, 2560:4096]]
        y1s = [A4[:, 4096:4480], A4[:, 4480:4864]]
        _b2r_base = 4864
        b2r = [A4[0:1, 4864:5888], A4[0:1, 5888:6912]]
        O_sb = A4[:, 6912:6944]
        xnt_t = [sb("xnt0", [128, 8 * 512], BF16), sb("xnt1", [128, 8 * 512], BF16)]
        vfm = [sb("vfm0", [128, 512], BF16), sb("vfm1", [128, 512], BF16)]
        PT = [sb("PT0", [128, 1024], BF16), sb("PT1", [128, 1024], BF16)]
        aT = sb("aT", [128, 8 * 2048], BF16)
        lb_sb = sb("lb_sb", [128, 512], F32)
        stage = [sb("stage0", [128, 512], BF16), sb("stage1", [128, 512], BF16)]
        recip_sb = lb_sb
        recipb_sb = stage[0]
        logits_sb = sb("logits_sb", [128, 32], F32)
        eid_sb = sb("eid_sb", [128, 16], F32)
        ltu_bf = sb("ltu_bf", [128, 128], BF16)
        iota_a = sb("iota_a", [128, 32], F32)
        rank_t = sb("rank_t", [128, 32], F32)
        tmp32 = sb("tmp32", [128, 32], F32)
        idx_sb = sb("idx_sb", [128, 16], mybir.dt.int32)
        cnt_bf = sb("cnt_bf", [1, 16 * 32], BF16)
        y2sb = xn_f
        ffng = sb("ffng", [128, 1024], F32)
        _osb = sb("out_sb0", [128, D], F32)
        out_sb = [_osb, _osb]

        ps_st0 = ps("ps_st0", [128, 1024], F32)
        ps_st1 = ps("ps_st1", [128, 1024], F32)
        ps_av = ps("ps_av", [128, 1024], F32)
        ps_lb = ps("ps_lb", [128, 512], F32)
        ps_tbf = ps("ps_tbf", [128, 1024], BF16)
        ST = [(ps_st0, "st0a", 0), (ps_st0, "st0b", 512),
              (ps_st1, "st1a", 0), (ps_st1, "st1b", 512)]

        sems = {e: stk.enter_context(nc.semaphore("sem_" + e)) for e in ENGINES}
        for qe in ("sync", "gpsimd", "scalar", "vector", "tensor"):
            for ln in range(Build.NLANES):
                nm = f"{qe}_d{ln}"
                sems[nm] = stk.enter_context(nc.semaphore("sem_" + nm))
        block = stk.enter_context(nc.Block())

        b.op("sync", lambda e: e.dma_start(out=bqkv_sb[:], in_=bqkv),
             writes=["bqkv"], dma=True)
        b.op("gpsimd", lambda e: e.dma_start(out=ident_bf[:], in_=ident),
             writes=["ident"], dma=True)
        b.op("gpsimd", lambda e: e.dma_start(
            out=masks_bf[:], in_=masks.rearrange("r p q -> p r q")),
            writes=["masks"], dma=True)
        b.op("vector", lambda e: e.memset(ones_bf[:], 1.0), writes=["ones"])
        b.op("vector", lambda e: e.memset(eps_sb[:], 1e-5), writes=["eps"])
        b.op("sync", lambda e: e.dma_start(out=b1t_sb[:], in_=b1t),
             writes=["b1t"], dma=True)
        b.op("gpsimd", lambda e: e.dma_start(
            out=rwT_bf[:], in_=rwT.rearrange("(f p) e -> p f e", p=128)),
            writes=["rwT"], dma=True)
        b.op("gpsimd", lambda e: e.dma_start(out=rb_bf[:], in_=rb),
             writes=["rb"], dma=True)
        b.op("gpsimd", lambda e: e.dma_start(out=bfc_bf[:], in_=bfc),
             writes=["bfc"], dma=True)
        b.op("gpsimd", lambda e: e.iota(
            iota_d[:], pattern=[[-1, 32]], base=31, channel_multiplier=0,
            allow_small_or_imprecise_dtypes=True), writes=["iota"])
        b.op("gpsimd", lambda e: e.iota(
            iota_a[:], pattern=[[1, 32]], base=0, channel_multiplier=0,
            allow_small_or_imprecise_dtypes=True), writes=["iotaa"])
        b.op("gpsimd", lambda e: e.dma_start(out=ltu_bf[:], in_=ltu),
             writes=["ltu"], dma=True)


        S = stats

        def ln_tile(tt):
            bu = tt % 2
            xk, qk = f"x_t{bu}", f"xn_f{bu}"
            SB = 0 if bu == 0 else 16
            b.op("sync", lambda e, tt=tt, bu=bu: e.dma_start(
                out=x_t[bu][:], in_=x[tt * 128:(tt + 1) * 128, :]),
                writes=[xk], dma=True)
            b.op("vector", lambda e, bu=bu: e.scalar_tensor_tensor(
                out=xsq[:], in0=x_t[bu][:], scalar=1.0, in1=x_t[bu][:],
                op0=ALU.mult, op1=ALU.mult, accum_out=S[:, SB + 1:SB + 2]),
                reads=[xk], writes=["xsq" + str(bu), "s2" + str(bu)], sync_out=True)
            b.op("vector", lambda e, bu=bu: e.reduce_sum(
                out=S[:, SB + 0:SB + 1], in_=x_t[bu][:], axis=AX.X),
                reads=[xk], writes=["s1" + str(bu)], sync_out=True)
            b.op("vector", lambda e: e.tensor_scalar(
                out=S[:, SB + 2:SB + 3], in0=S[:, SB + 0:SB + 1], scalar1=1.0 / D, scalar2=None,
                op0=ALU.mult), reads=["s1" + str(bu)], writes=["mean" + str(bu)])
            b.op("vector", lambda e: e.tensor_scalar(
                out=S[:, SB + 3:SB + 4], in0=S[:, SB + 1:SB + 2], scalar1=1.0 / D, scalar2=None,
                op0=ALU.mult), reads=["s2" + str(bu)], writes=["ex2" + str(bu)])
            b.op("vector", lambda e: e.tensor_scalar(
                out=S[:, SB + 4:SB + 5], in0=S[:, SB + 2:SB + 3], scalar1=S[:, SB + 2:SB + 3], scalar2=None,
                op0=ALU.mult), reads=["mean" + str(bu)], writes=["m2" + str(bu)])
            b.op("vector", lambda e: e.tensor_sub(
                out=S[:, SB + 5:SB + 6], in0=S[:, SB + 3:SB + 4], in1=S[:, SB + 4:SB + 5]),
                reads=["ex2" + str(bu), "m2" + str(bu)], writes=["var" + str(bu)])
            b.op("scalar", lambda e: e.activation(
                out=S[:, SB + 6:SB + 7], in_=S[:, SB + 5:SB + 6], func=AF.Ln, bias=eps_sb[:]),
                reads=["var" + str(bu), "eps"], writes=["lnv" + str(bu)])
            b.op("scalar", lambda e: e.activation(
                out=S[:, SB + 7:SB + 8], in_=S[:, SB + 6:SB + 7], func=AF.Exp, scale=-0.5),
                reads=["lnv" + str(bu)], writes=["rstd" + str(bu)])
            b.op("vector", lambda e: e.tensor_scalar(
                out=S[:, SB + 8:SB + 9], in0=S[:, SB + 2:SB + 3], scalar1=S[:, SB + 7:SB + 8], scalar2=-1.0,
                op0=ALU.mult, op1=ALU.mult), reads=["mean" + str(bu), "rstd" + str(bu)], writes=["nmr" + str(bu)])
            b.op("vector", lambda e, bu=bu: e.tensor_scalar(
                out=xn_f[bu][:], in0=x_t[bu][:], scalar1=S[:, SB + 7:SB + 8],
                scalar2=S[:, SB + 8:SB + 9], op0=ALU.mult, op1=ALU.add),
                reads=[xk, "rstd" + str(bu), "nmr" + str(bu)], writes=[qk])
            b.op("gpsimd", lambda e, tt=tt, bu=bu: e.dma_start(
                out=xn_tm[tt * 128:(tt + 1) * 128, :], in_=xn_f[bu][:]),
                reads=[qk], writes=[f"xntm{tt}"], dma=True)

        for tt in range(NT):
            ln_tile(tt)

        if "xn" in dbg:
            for tt in range(NT):
                bu = tt % 2
                b.op("gpsimd", lambda e, tt=tt, bu=bu: e.dma_start(
                    out=stage[bu][:], in_=xn_tm[tt * 128:(tt + 1) * 128, 0:512]),
                    reads=[f"xntm{tt}"], writes=[f"dstg{bu}"], dma=True)
                b.op("gpsimd", lambda e, tt=tt, bu=bu: e.dma_start(
                    out=dbg["xn"][tt * 128:(tt + 1) * 128, 0:512], in_=stage[bu][:]),
                    reads=[f"dstg{bu}"], dma=True, force_await=True)
                b.op("gpsimd", lambda e, tt=tt, bu=bu: e.dma_start(
                    out=stage[bu][:], in_=xn_tm[tt * 128:(tt + 1) * 128, 512:1024]),
                    reads=[f"xntm{tt}", f"dstg{bu}x"], writes=[f"dstg{bu}"], dma=True)
                b.op("gpsimd", lambda e, tt=tt, bu=bu: e.dma_start(
                    out=dbg["xn"][tt * 128:(tt + 1) * 128, 512:1024], in_=stage[bu][:]),
                    reads=[f"dstg{bu}"], dma=True, force_await=True)

        if phases < 2:
            b.lower()
            _finish(b, block, sems)
            return nc

        for w in range(2):
            if w == 1:
                for occ in range(4):
                    for tc in range(8):
                        b.takeover(f"K1_{occ}_{tc}", [f"K0_{occ}_{tc}"])
                        b.takeover(f"Q1_{occ}_{tc}", [f"Q0_{occ}_{tc}"])
                for hw in range(8):
                    for t in range(NT):
                        b.takeover(f"Va1_{hw}_{t}", [f"Va0_{hw}_{t}"])
            for f in range(8):
                b.op("gpsimd", lambda e, f=f, w=w: e.dma_start(
                    out=wqkv_bf[:, f * 1536:(f + 1) * 1536],
                    in_=wqkvT[f * 128:(f + 1) * 128, w * 1536:(w + 1) * 1536]),
                    writes=[f"wq{f}"], dma=True)
            b.op("vector", lambda e: e.memset(Va_sb[:], 1.0),
                 writes=[f"Va{w}_{hw}_{t}" for hw in range(8) for t in range(NT)])

            for tc in range(8):
                bu = tc % 2
                for f in range(8):
                    b.op("sync", lambda e, tc=tc, f=f, bu=bu: e.dma_start(
                        out=xnt_t[bu][:, f * 512:(f + 1) * 512],
                        in_=xn_tm[tc * 512:(tc + 1) * 512, f * 128:(f + 1) * 128],
                        transpose=True),
                        reads=[f"xntm{4 * tc + j}" for j in range(4)],
                        writes=[f"xnt{bu}_{f}"], dma=True)
                for oc in range(12):
                    pst, pk, po = ST[oc % 4]
                    comp = oc // 4
                    occ = oc % 4
                    gcol = w * 12 + oc
                    mm = []
                    for f in range(8):
                        mm.append(b.op("tensor", lambda e, f=f, oc=oc, pst=pst,
                                       po=po, bu=bu: e.matmul(
                            pst[:, po:po + 512],
                            wqkv_bf[:, f * 1536 + oc * 128: f * 1536 + (oc + 1) * 128],
                            xnt_t[bu][:, f * 512:(f + 1) * 512],
                            start=(f == 0), stop=(f == 7)),
                            reads=[f"wq{f}", f"xnt{bu}_{f}"], writes=[pk]))
                    if comp == 0:
                        b.op("vector", lambda e, pst=pst, po=po, occ=occ, tc=tc,
                             gcol=gcol, w=w: e.tensor_scalar(
                            out=Q_sb[:, occ * 2048 + tc * 256: occ * 2048 + (tc + 1) * 256]
                                .rearrange("p (a c) -> p a c", a=2),
                            in0=pst[:, po:po + 512].rearrange(
                                "p (a c) -> p a c", a=4)[:, 0::2, :],
                            scalar1=bqkv_sb[:, gcol:gcol + 1], scalar2=None,
                            op0=ALU.add),
                            reads=[pk, "bqkv"], writes=[f"Q{w}_{occ}_{tc}"])
                    elif comp == 1:
                        b.op("vector", lambda e, pst=pst, po=po, occ=occ, tc=tc,
                             gcol=gcol: e.tensor_scalar(
                            out=K_sb[:, occ * 4096 + tc * 512: occ * 4096 + (tc + 1) * 512],
                            in0=pst[:, po:po + 512],
                            scalar1=bqkv_sb[:, gcol:gcol + 1], scalar2=None,
                            op0=ALU.add),
                            reads=[pk, "bqkv"], writes=[f"K{w}_{occ}_{tc}"])
                    else:
                        vb = tc % 2
                        b.op("vector", lambda e, pst=pst, po=po, vb=vb,
                             gcol=gcol: e.tensor_scalar(
                            out=vfm[vb][:], in0=pst[:, po:po + 512],
                            scalar1=bqkv_sb[:, gcol:gcol + 1], scalar2=None,
                            op0=ALU.add),
                            reads=[pk, "bqkv"], writes=[f"vfm{vb}"])
                        for j in range(4):
                            tt = 4 * tc + j
                            hw0 = occ * 2
                            b.op("tensor", lambda e, vb=vb, j=j: e.transpose(
                                ps_tbf[:, 0:128], vfm[vb][:, j * 128:(j + 1) * 128],
                                ident_bf[:]),
                                reads=[f"vfm{vb}", "ident"], writes=["tbf"])
                            for hh in range(2):
                                hw = hw0 + hh
                                off = (hw * NT + tt) * 65
                                b.op("vector", lambda e, hh=hh, off=off: e.tensor_copy(
                                    out=Va_sb[:, off:off + 64],
                                    in_=ps_tbf[:, hh * 64:hh * 64 + 64]),
                                    reads=["tbf"],
                                    writes=[f"Va{w}_{hw0 + hh}_{tt}"])

            if "K" in dbg and w == 0:
                b.op("sync", lambda e: e.dma_start(out=dbg["K"], in_=K_sb[:]),
                     reads=[f"K{w}_{occ}_{tc}" for occ in range(4) for tc in range(8)],
                     dma=True, force_await=True)
            if "Q" in dbg and w == 0:
                b.op("sync", lambda e: e.dma_start(out=dbg["Q"], in_=Q_sb[:]),
                     reads=[f"Q{w}_{occ}_{tc}" for occ in range(4) for tc in range(8)],
                     dma=True, force_await=True)
            if "Va" in dbg and w == 0:
                b.op("sync", lambda e: e.dma_start(out=dbg["Va"], in_=Va_sb[:]),
                     reads=[f"Va{w}_{hw}_{t}" for hw in range(8) for t in range(NT)],
                     dma=True, force_await=True)

            if phases < 3:
                break

            for hp in range(4):
                for c in range(4):
                    nkt = 8 * c + 8
                    fch = w * 4 + hp

                    def emit_st(kt, hp=hp, c=c, w=w):
                        live = 512 if kt < 8 * c else LIVE[kt - 8 * c]
                        qoff = 512 - live
                        pst = ps_st0 if kt % 2 == 0 else ps_st1
                        pkeys = [("st0a", "st0b"), ("st1a", "st1b")][kt % 2]
                        for j in range(2):
                            b.op("tensor", lambda e, pst=pst, j=j, hp=hp, kt=kt,
                                 c=c, qoff=qoff: e.matmul(
                                pst[:, j * 512 + qoff: (j + 1) * 512],
                                K_sb[j * 64:(j + 1) * 64,
                                     hp * 4096 + kt * 128: hp * 4096 + (kt + 1) * 128],
                                Q_sb[j * 64:(j + 1) * 64,
                                     hp * 2048 + c * 512 + qoff: hp * 2048 + (c + 1) * 512],
                                start=True, stop=True,
                                tile_position=(64 * j, 0)),
                                reads=[f"K{w}_{hp}_{kt // 4}",
                                       f"Q{w}_{hp}_{2 * c}", f"Q{w}_{hp}_{2 * c + 1}"],
                                writes=[pkeys[j]])

                    emit_st(0)
                    for kt in range(nkt):
                        live = 512 if kt < 8 * c else LIVE[kt - 8 * c]
                        qoff = 512 - live
                        pst, ptb = (ps_st0, 0) if kt % 2 == 0 else (ps_st1, 1)
                        pkeys = [("st0a", "st0b"), ("st1a", "st1b")][ptb]
                        b.op("scalar", lambda e, pst=pst, ptb=ptb: e.activation(
                            out=PT[ptb][:], in_=pst[:], func=AF.Exp, scale=0.125),
                            reads=list(pkeys), writes=[f"pt{ptb}"])
                        if kt + 1 < nkt:
                            emit_st(kt + 1)
                        if kt >= 8 * c:
                            r = kt - 8 * c
                            for j in range(2):
                                b.op("vector", lambda e, ptb=ptb, j=j, qoff=qoff,
                                     r=r: e.tensor_mul(
                                    out=PT[ptb][:, j * 512 + qoff: j * 512 + qoff + 128],
                                    in0=PT[ptb][:, j * 512 + qoff: j * 512 + qoff + 128],
                                    in1=masks_bf[:, r * 128:(r + 1) * 128]),
                                    reads=[f"pt{ptb}", "masks"], writes=[f"pt{ptb}"])
                        for j in range(2):
                            hw = 2 * hp + j
                            b.op("tensor", lambda e, hw=hw, kt=kt, ptb=ptb, j=j,
                                 qoff=qoff, c=c, nkt=nkt: e.matmul(
                                ps_av[0:65, j * 512 + qoff: (j + 1) * 512],
                                Va_sb[:, (hw * NT + kt) * 65: (hw * NT + kt) * 65 + 65],
                                PT[ptb][:, j * 512 + qoff: (j + 1) * 512],
                                start=(kt == 0), stop=(kt == nkt - 1)),
                                reads=[f"pt{ptb}", f"Va{w}_{hw}_{kt}"],
                                writes=[f"av{j}"])
                    for j in range(2):
                        hv = f"av{j}"
                        b.op("scalar", lambda e, j=j: e.activation(
                            out=recip_sb[64:65, 0:512],
                            in_=ps_av[64:65, j * 512:(j + 1) * 512], func=AF.Ln),
                            reads=[hv], writes=["recip"])
                        b.op("scalar", lambda e: e.activation(
                            out=recipb_sb[64:65, 0:512], in_=recip_sb[64:65, 0:512],
                            func=AF.Exp, scale=-1.0),
                            reads=["recip"], writes=["recipb"])
                        b.op("tensor", lambda e: e.matmul(
                            ps_lb[0:64, :], ones_bf[64:65, 0:64],
                            recipb_sb[64:65, 0:512], start=True, stop=True),
                            reads=["recipb", "ones"], writes=["lb"])
                        b.op("vector", lambda e: e.tensor_copy(
                            out=lb_sb[0:64, :], in_=ps_lb[0:64, :]),
                            reads=["lb"], writes=["lbs"])
                        if j == 0:
                            b.op("vector", lambda e, j=j, fch=fch, c=c: e.tensor_mul(
                                out=aT[0:64, fch * 2048 + c * 512: fch * 2048 + (c + 1) * 512],
                                in0=ps_av[0:64, j * 512:(j + 1) * 512],
                                in1=lb_sb[0:64, :]),
                                reads=[hv, "lbs"], writes=[f"aT_{fch}_{c}L"])
                        else:
                            sgb = (hp * 4 + c) % 2
                            b.op("vector", lambda e, j=j, sgb=sgb: e.tensor_mul(
                                out=stage[sgb][0:64, :],
                                in0=ps_av[0:64, j * 512:(j + 1) * 512],
                                in1=lb_sb[0:64, :]),
                                reads=[hv, "lbs"], writes=[f"stg{sgb}"])
                            b.op("tensor", lambda e, sgb=sgb: e.matmul(
                                ps_av[64:128, 0:512], ident_bf[0:64, 0:64],
                                stage[sgb][0:64, :], start=True, stop=True,
                                tile_position=(0, 64)),
                                reads=[f"stg{sgb}", "ident"],
                                writes=["avhop", "av0"])
                            b.op("vector", lambda e, fch=fch, c=c: e.tensor_copy(
                                out=aT[64:128, fch * 2048 + c * 512: fch * 2048 + (c + 1) * 512],
                                in_=ps_av[64:128, 0:512]),
                                reads=["avhop", "av0"], writes=[f"aT_{fch}_{c}H"])

        if "aT" in dbg:
            b.op("sync", lambda e: e.dma_start(out=dbg["aT"], in_=aT[:]),
                 reads=[f"aT_{fc_}_{c}{h}" for fc_ in range(8) for c in range(4)
                        for h in "LH"],
                 dma=True, force_await=True)

        if phases < 4:
            b.lower()
            _finish(b, block, sems)
            return nc

        _va_keys = [f"Va{w}_{hw}_{t}" for w in range(2) for hw in range(8)
                    for t in range(NT)]
        for f in range(8):
            for s_ in range(NOWN):
                b.takeover(f"xno_{f}_{s_}", _va_keys)
        _q_keys = [f"Q{w}_{occ}_{tc}" for w in range(2) for occ in range(4)
                   for tc in range(8)]
        for kk in (["xnk", "xsT0", "xsT1", "y1s0", "y1s1", "b2r0", "b2r1", "O"]):
            b.takeover(kk, _q_keys)
        for f in range(8):
            for s in range(NOWN):
                b.op("sync", lambda e, f=f, s=s: e.dma_start(
                    out=xno_sb[:, f * 2048 + s * 128: f * 2048 + (s + 1) * 128],
                    in_=xn_tm[(2 * s) * 128:(2 * s + 1) * 128, f * 128:(f + 1) * 128],
                    transpose=True),
                    reads=[f"xntm{2 * s}"], writes=[f"xno_{f}_{s}"], dma=True)

        for s in range(NOWN):
            for f in range(8):
                b.op("tensor", lambda e, f=f, s=s: e.matmul(
                    ps_lb[:, 0:32],
                    xno_sb[:, f * 2048 + s * 128: f * 2048 + (s + 1) * 128],
                    rwT_bf[:, f * 32:(f + 1) * 32],
                    start=(f == 0), stop=False),
                    reads=[f"xno_{f}_{s}", "rwT"], writes=["lb"])
            b.op("tensor", lambda e: e.matmul(
                ps_lb[:, 0:32], ones_bf[0:1, 0:128], rb_bf[0:1, :],
                start=False, stop=True),
                reads=["ones", "rb"], writes=["lb"])
            b.op("vector", lambda e: e.tensor_copy(
                out=logits_sb[:], in_=ps_lb[:, 0:32]),
                reads=["lb"], writes=["lg"])
            if "lg" in dbg:
                b.op("sync", lambda e, s=s: e.dma_start(
                    out=dbg["lg"][:, s * 32:(s + 1) * 32], in_=logits_sb[:]),
                    reads=["lg"], dma=True, force_await=True)
            b.op("vector", lambda e: e.reduce_max(
                out=S[:, 10:11], in_=logits_sb[:], axis=AX.X),
                reads=["lg"], writes=["lmax"], sync_out=True)
            b.op("vector", lambda e: e.tensor_scalar(
                out=logits_sb[:], in0=logits_sb[:], scalar1=S[:, 10:11],
                scalar2=None, op0=ALU.is_ge), reads=["lg", "lmax"], writes=["lg"])
            b.op("vector", lambda e: e.tensor_mul(
                out=logits_sb[:], in0=logits_sb[:], in1=iota_d[:]),
                reads=["lg", "iota"], writes=["lg"])
            b.op("vector", lambda e: e.reduce_max(
                out=S[:, 11:12], in_=logits_sb[:], axis=AX.X),
                reads=["lg"], writes=["l31"], sync_out=True)
            b.op("vector", lambda e, s=s: e.tensor_scalar(
                out=eid_sb[:, s:s + 1], in0=S[:, 11:12], scalar1=-1.0,
                scalar2=31.0, op0=ALU.mult, op1=ALU.add),
                reads=["l31"], writes=[f"eid{s}"])

        if "eid" in dbg:
            b.op("sync", lambda e: e.dma_start(out=dbg["eid"], in_=eid_sb[:]),
                 reads=[f"eid{s}" for s in range(NOWN)], dma=True,
                 force_await=True)

        if phases < 5:
            b.lower()
            _finish(b, block, sems)
            return nc

        for s in range(NOWN):
            b.op("vector", lambda e, s=s: e.tensor_scalar(
                out=O_sb, in0=iota_a[:], scalar1=eid_sb[:, s:s + 1],
                scalar2=None, op0=ALU.is_equal),
                reads=[f"eid{s}", "iotaa"], writes=["O"])
            b.op("tensor", lambda e, s=s: e.matmul(
                ps_st1[:, 0:32], ltu_bf[:], O_sb, start=True, stop=(s == 0)),
                reads=["O", "ltu"], writes=["st1a"])
            for sp in range(s):
                b.op("tensor", lambda e, sp=sp, s=s: e.matmul(
                    ps_st1[:, 0:32], ones_bf[0:1, 0:128],
                    cnt_bf[0:1, sp * 32:(sp + 1) * 32],
                    start=False, stop=(sp == s - 1)),
                    reads=[f"cnt{sp}", "ones"], writes=["st1a"])
            b.op("tensor", lambda e: e.matmul(
                ps_st1[0:1, 512:544], ones_bf[:, 0:1], O_sb,
                start=True, stop=True), reads=["O", "ones"], writes=["st1b"])
            b.op("vector", lambda e: e.tensor_copy(out=rank_t[:],
                                                   in_=ps_st1[:, 0:32]),
                 reads=["st1a"], writes=["rankt"])
            b.op("vector", lambda e: e.tensor_mul(out=tmp32[:], in0=rank_t[:],
                                                  in1=O_sb),
                 reads=["rankt", "O"], writes=["tmpr"])
            b.op("vector", lambda e: e.reduce_sum(
                out=S[:, 12:13], in_=tmp32[:], axis=AX.X),
                reads=["tmpr"], writes=["slotr"], sync_out=True)
            b.op("vector", lambda e, s=s: e.scalar_tensor_tensor(
                out=S[:, 13:14], in0=eid_sb[:, s:s + 1], scalar=192.0,
                in1=S[:, 12:13], op0=ALU.mult, op1=ALU.add),
                reads=[f"eid{s}", "slotr"], writes=["slotf"])
            b.op("vector", lambda e, s=s: e.tensor_copy(
                out=idx_sb[:, s:s + 1], in_=S[:, 13:14]),
                reads=["slotf"], writes=[f"idx{s}"])
            b.op("vector", lambda e, s=s: e.tensor_copy(
                out=cnt_bf[0:1, s * 32:(s + 1) * 32], in_=ps_st1[0:1, 512:544]),
                reads=["st1b"], writes=[f"cnt{s}"])
            b.op("gpsimd", lambda e, s=s: e.dma_start(
                out=xnk, in_=xn_tm[(2 * s) * 128:(2 * s + 1) * 128, :]),
                reads=[f"xntm{2 * s}"], writes=["xnk"], dma=True)
            b.op("gpsimd", lambda e, s=s: e.indirect_dma_start(
                out=xs_dram[0:128, :],
                out_offset=IndirectOffsetOnAxis(idx_sb[:, s:s + 1], 0),
                in_=xnk, in_offset=None),
                reads=["xnk", f"idx{s}"], writes=[f"xs{s}"], dma=True)

        if phases < 6:
            b.lower()
            _finish(b, block, sems)
            return nc

        b.takeover("y2s0", ["xn_f0"])
        b.takeover("y2s1", ["xn_f1"])
        b.takeover("tbB", ["tbf"])
        _xs_keys = [f"xs{s}" for s in range(NOWN)]
        t1x = [t1_f, out_sb[0]]
        for kk in ("t1x0", "t1x1"):
            b.takeover(kk, ["t1"])
        for f in range(8):
            b.op("gpsimd", lambda e, f=f: e.dma_start(
                out=wfcT_bf[:, f * 1024:(f + 1) * 1024],
                in_=wfcT[f * 128:(f + 1) * 128, :]),
                writes=[f"wfc{f}"], dma=True)

        def emit_fc(s_):
            bu = s_ % 2
            c = s_ // 4
            b.op("sync", lambda e, s_=s_, bu=bu: e.dma_start(
                out=x_t[bu][:], in_=x[(2 * s_) * 128:(2 * s_ + 1) * 128, :]),
                writes=[f"x_t{bu}"], dma=True)
            for half in range(2):
                for f in range(8):
                    b.op("tensor", lambda e, f=f, half=half, s_=s_, c=c: e.matmul(
                        ps_lb[:, 0:512],
                        aT[:, f * 2048 + s_ * 128: f * 2048 + (s_ + 1) * 128],
                        wfcT_bf[:, f * 1024 + half * 512: f * 1024 + (half + 1) * 512],
                        start=(f == 0), stop=False),
                        reads=[f"wfc{f}", f"aT_{f}_{c}L", f"aT_{f}_{c}H"],
                        writes=["lb"])
                b.op("tensor", lambda e, half=half: e.matmul(
                    ps_lb[:, 0:512], ones_bf[0:1, 0:128],
                    bfc_bf[0:1, half * 512:(half + 1) * 512],
                    start=False, stop=True),
                    reads=["ones", "bfc"], writes=["lb"])
                b.op("vector", lambda e, half=half, bu=bu: e.tensor_add(
                    out=t1x[bu][:, half * 512:(half + 1) * 512],
                    in0=ps_lb[:, 0:512],
                    in1=x_t[bu][:, half * 512:(half + 1) * 512]),
                    reads=["lb", f"x_t{bu}"], writes=[f"t1x{bu}"])
            b.op("sync", lambda e, s_=s_, bu=bu: e.dma_start(
                out=fcx_dram[s_ * 128:(s_ + 1) * 128, :], in_=t1x[bu][:]),
                reads=[f"t1x{bu}"], writes=[f"fcx{s_}"], dma=True)

        def moe_loads(e_):
            wb = e_ % 2
            b.op("gpsimd", lambda e, e_=e_, wb=wb: e.dma_start(
                out=w1_bf[wb][:], in_=w1[e_, :, :].rearrange(
                    "(f p) h -> p f h", p=128)),
                writes=[f"w1b{wb}"], dma=True)
            b.op("gpsimd", lambda e, e_=e_, wb=wb: e.dma_start(
                out=w2_bf[wb][:], in_=w2[e_, :, :].rearrange(
                    "(hc p) o -> p hc o", p=128)),
                writes=[f"w2b{wb}"], dma=True)
            b.op("gpsimd", lambda e, e_=e_, wb=wb: e.dma_start(
                out=b2r[wb], in_=b2[e_:e_ + 1, :]),
                writes=[f"b2r{wb}"], dma=True)
            b.op("scalar", lambda e, e_=e_: e.dma_start(
                out=PT[0][:], in_=xs_dram[e_ * 192:e_ * 192 + 128, :]),
                reads=_xs_keys, writes=["pt0"], dma=True)
            b.op("sync", lambda e, e_=e_: e.dma_start(
                out=PT[1][:], in_=xs_dram[e_ * 192 + 64:e_ * 192 + 192, :]),
                reads=_xs_keys, writes=["pt1"], dma=True)

        def moe_T(e_, half):
            wb = e_ % 2
            for f in range(8):
                b.op("tensor", lambda e, half=half, f=f: e.transpose(
                    ps_tbf[:, f * 128:(f + 1) * 128],
                    PT[half][:, f * 128:(f + 1) * 128], ident_bf[:]),
                    reads=[f"pt{half}", "ident"], writes=[f"tb{f}", "tbB"])
            if half == 0:
                b.op("vector", lambda e, wb=wb: e.tensor_copy(
                    out=xsT[wb][:].rearrange("p (f c) -> p f c", c=192)[:, :, 0:128],
                    in_=ps_tbf[:].rearrange("p (f c) -> p f c", c=128)),
                    reads=[f"tb{f}" for f in range(8)] + ["tbB"],
                    writes=[f"xsT{wb}_{f}" for f in range(8)])
            else:
                b.op("vector", lambda e, wb=wb: e.tensor_copy(
                    out=xsT[wb][:].rearrange("p (f c) -> p f c", c=192)[:, :, 128:192],
                    in_=ps_tbf[:].rearrange("p (f c) -> p f c", c=128)[:, :, 64:128]),
                    reads=[f"tb{f}" for f in range(8)] + ["tbB"],
                    writes=[f"xsT{wb}_{f}" for f in range(8)])

        def moe_y1(e_):
            wb = e_ % 2
            w1k = f"w1b{wb}"
            for hc in range(2):
                po_y1 = hc * 512
                for f in range(8):
                    b.op("tensor", lambda e, hc=hc, f=f, wb=wb,
                         po_y1=po_y1: e.matmul(
                        ps_av[:, po_y1:po_y1 + 192],
                        w1_bf[wb][:, f * 256 + hc * 128: f * 256 + (hc + 1) * 128],
                        xsT[wb][:, f * 192:(f + 1) * 192],
                        start=(f == 0), stop=(f == 7)),
                        reads=[w1k, f"xsT{wb}_{f}"], writes=[f"av{hc}"])
                b.op("vector", lambda e, hc=hc, e_=e_, po_y1=po_y1,
                     wb=wb: e.tensor_scalar(
                    out=y1s[wb][:, hc * 192:(hc + 1) * 192],
                    in0=ps_av[:, po_y1:po_y1 + 192],
                    scalar1=b1t_sb[:, 2 * e_ + hc: 2 * e_ + hc + 1],
                    scalar2=0.0, op0=ALU.add, op1=ALU.max),
                    reads=[f"av{hc}", "b1t"], writes=[f"y1s{wb}"])

        def moe_y2(e_):
            wb = e_ % 2
            w2k = f"w2b{wb}"
            for tl in range(2):
                nsl = 128 if tl == 0 else 64
                pst = ps_st0 if tl == 0 else ps_st1
                pk = [("st0a", "st0b"), ("st1a", "st1b")][tl]
                for half in range(2):
                    for hc in range(2):
                        b.op("tensor", lambda e, tl=tl, half=half, hc=hc,
                             pst=pst, wb=wb, nsl=nsl: e.matmul(
                            pst[0:nsl, half * 512:(half + 1) * 512],
                            y1s[wb][:, hc * 192 + tl * 128: hc * 192 + tl * 128 + nsl],
                            w2_bf[wb][:, hc * 1024 + half * 512: hc * 1024 + (half + 1) * 512],
                            start=(hc == 0), stop=False),
                            reads=[f"y1s{wb}", w2k], writes=[pk[half]])
                    b.op("tensor", lambda e, half=half, pst=pst, wb=wb,
                         nsl=nsl: e.matmul(
                        pst[0:nsl, half * 512:(half + 1) * 512],
                        ones_bf[0:1, 0:nsl],
                        b2r[wb][0:1, half * 512:(half + 1) * 512],
                        start=False, stop=True),
                        reads=["ones", f"b2r{wb}"], writes=[pk[half]])
                yb = tl
                b.op("vector", lambda e, pst=pst, nsl=nsl, yb=yb: e.tensor_scalar(
                    out=y2sb[yb][0:nsl, :], in0=pst[0:nsl, :],
                    scalar1=0.0, scalar2=None, op0=ALU.max),
                    reads=list(pk), writes=[f"y2s{yb}"])
                b.op("sync", lambda e, e_=e_, tl=tl, nsl=nsl, yb=yb: e.dma_start(
                    out=y2_dram[e_ * 192 + tl * 128: e_ * 192 + tl * 128 + nsl, :],
                    in_=y2sb[yb][0:nsl, :]),
                    reads=[f"y2s{yb}"], writes=[f"y2d_{e_}_{tl}"], dma=True)

        moe_loads(0)
        moe_T(0, 0)
        moe_T(0, 1)
        for e_ in range(NE):
            if e_ + 1 < NE:
                moe_loads(e_ + 1)
            moe_y1(e_)
            if e_ + 1 < NE:
                moe_T(e_ + 1, 0)
            moe_y2(e_)
            if e_ + 1 < NE:
                moe_T(e_ + 1, 1)
            if e_ % 2 == 1:
                emit_fc(e_ // 2)

        if phases < 7:
            b.lower()
            _finish(b, block, sems)
            return nc

        _y2_keys = [f"y2d_{e_}_{tl}" for e_ in range(NE) for tl in range(2)]
        ffg = [ffng, xn_f[0]]
        b.takeover("ffg1", ["y2s0"])
        for s_ in range(NOWN):
            bu = s_ % 2
            b.op("sync", lambda e, s_=s_, bu=bu: e.dma_start(
                out=x_t[bu][:], in_=fcx_dram[s_ * 128:(s_ + 1) * 128, :]),
                reads=[f"fcx{s_}"], writes=[f"x_t{bu}"], dma=True)
            b.op("gpsimd", lambda e, s_=s_, bu=bu: e.indirect_dma_start(
                out=ffg[bu][:, 0:1024], out_offset=None,
                in_=y2_dram[0:128, :],
                in_offset=IndirectOffsetOnAxis(idx_sb[:, s_:s_ + 1], 0)),
                reads=_y2_keys + [f"idx{s_}"], writes=[f"ffg{bu}"], dma=True)
            b.op("vector", lambda e, s_=s_, bu=bu: e.tensor_add(
                out=out_sb[1][:], in0=x_t[bu][:], in1=ffg[bu][:, 0:1024]),
                reads=[f"x_t{bu}", f"ffg{bu}"], writes=["outsb", "t1x1"])
            b.op("sync", lambda e, s_=s_: e.dma_start(
                out=out[s_ * 128:(s_ + 1) * 128, :], in_=out_sb[1][:]),
                reads=["outsb"], writes=[f"odma{s_}"], dma=True,
                force_await=True)

        b.lower()
        _finish(b, block, sems)
    return nc


def _finish(b, block, sems):
    @block.sync
    def _(e):
        b.emit("sync", e, sems)
        for sn, v in sorted(b.final.items()):
            if v:
                e.wait_ge(sems[sn], v)

    @block.gpsimd
    def _(e):
        b.emit("gpsimd", e, sems)

    @block.vector
    def _(e):
        b.emit("vector", e, sems)

    @block.scalar
    def _(e):
        b.emit("scalar", e, sems)

    @block.tensor
    def _(e):
        b.emit("tensor", e, sems)



def _core_inputs(inputs, core):
    bidx, par = core // 2, core % 2
    x = np.asarray(inputs["L0_attn_x"], np.float32)[bidx]
    if par == 1:
        xr = x.reshape(32, 128, D)
        perm = np.arange(32)
        perm = perm + 1 - 2 * (perm % 2)
        x = np.ascontiguousarray(xr[perm].reshape(4096, D))
    else:
        x = np.ascontiguousarray(x)

    w_qkv = np.asarray(inputs["L0_attn_w_qkv"], np.float32)
    b_qkv = np.asarray(inputs["L0_attn_b_qkv"], np.float32)
    rows = []
    for w in range(2):
        for comp in range(3):
            for hw in range(8):
                h = w * 8 + hw
                base = comp * 1024 + h * 64
                rows.append(np.arange(base, base + 64))
    rows = np.concatenate(rows)
    wqkvT = np.ascontiguousarray(w_qkv[rows].T)
    bq = np.ascontiguousarray(b_qkv[rows].reshape(24, 128).T)

    w_fc = np.asarray(inputs["L0_attn_w_fc"], np.float32)
    wfcT = np.ascontiguousarray(w_fc.T)
    bfc = np.ascontiguousarray(inputs["L0_attn_b_fc"], dtype=np.float32).reshape(1, D)

    rwT = np.ascontiguousarray(np.asarray(inputs["L0_ffn_router_w"], np.float32).T)
    rb = np.ascontiguousarray(np.asarray(inputs["L0_ffn_router_b"], np.float32).reshape(1, 32))
    w1 = np.ascontiguousarray(np.asarray(inputs["L0_ffn_weights1"], np.float32))
    b1 = np.asarray(inputs["L0_ffn_bias1"], np.float32)
    b1t = np.ascontiguousarray(
        b1.reshape(NE, 2, 128).transpose(2, 0, 1).reshape(128, 2 * NE))
    w2 = np.ascontiguousarray(np.asarray(inputs["L0_ffn_weights2"], np.float32))
    b2 = np.ascontiguousarray(np.asarray(inputs["L0_ffn_bias2"], np.float32))

    mk = np.zeros((8, 128, 128), np.float32)
    tri = np.triu(np.ones((128, 128), np.float32))
    for r in range(8):
        if r % 2 == 0:
            mk[r] = tri
        else:
            mk[r] = 0.0 if par == 0 else 1.0

    return {"x": x, "wqkvT": wqkvT, "bqkv": bq, "wfcT": wfcT, "bfc": bfc,
            "rwT": rwT, "rb": rb, "w1": w1, "b1t": b1t, "w2": w2, "b2": b2,
            "masks": np.ascontiguousarray(mk),
            "ident": np.eye(128, dtype=np.float32),
            "ltu": np.triu(np.ones((128, 128), np.float32), 1)}


def kernel(**inputs):
    nc = build_nc()
    in_maps = [_core_inputs(inputs, c) for c in range(8)]
    res = run_bass_kernel_spmd(nc, in_maps, core_ids=list(range(8)))
    out = np.empty((4, 4096, D), np.float32)
    for c in range(8):
        bidx, par = c // 2, c % 2
        o = res.results[c]["out"].reshape(NOWN, 128, D)
        ob = out[bidx].reshape(32, 128, D)
        for s in range(NOWN):
            ob[2 * s + par] = o[s]
    return out


# version 18 (final)
# speedup vs baseline: 2.1844x; 1.0018x over previous
import contextlib
import numpy as np

import concourse.bass as bass
import concourse.mybir as mybir
from concourse.bass_core_rust import IndirectOffsetOnAxis
from concourse.bass_utils import run_bass_kernel_spmd

F32 = mybir.dt.float32
BF16 = mybir.dt.bfloat16
AF = mybir.ActivationFunctionType
ALU = mybir.AluOpType
AX = mybir.AxisListType

ENGINES = ("sync", "gpsimd", "vector", "scalar", "tensor")
LIVE = [512, 512, 384, 384, 256, 256, 128, 128]
D = 1024
NT = 32
NOWN = 16
NH = 16
NE = 32
PHASES_ALL = 99


class Build:

    def __init__(self):
        self.ops = []
        self.last_w = {}
        self.readers = {}

    def op(self, eng, fn, reads=(), writes=(), dma=False, force_await=False,
           sync_out=False):
        deps = set()
        for k in reads:
            w = self.last_w.get(k)
            if w is not None:
                deps.add(w)
        for k in writes:
            w = self.last_w.get(k)
            if w is not None:
                deps.add(w)
            for r in self.readers.get(k, ()):
                deps.add(r)
        i = len(self.ops)
        self.ops.append({"eng": eng, "fn": fn, "dma": dma, "deps": deps,
                         "force": force_await, "sync_out": sync_out})
        for k in reads:
            self.readers.setdefault(k, set()).add(i)
        for k in writes:
            self.last_w[k] = i
            self.readers[k] = set()
        return i

    def takeover(self, new_key, old_keys):
        rd = set()
        w = None
        for ok in old_keys:
            rd |= self.readers.get(ok, set())
            lw = self.last_w.get(ok)
            if lw is not None and (w is None or lw > w):
                w = lw
        if w is not None:
            self.last_w[new_key] = w
        self.readers[new_key] = rd

    NLANES = 4

    def lower(self):
        seq = {e: [] for e in ENGINES}
        lane_ctr = {}
        for i, o in enumerate(self.ops):
            seq[o["eng"]].append(i)
            if o["dma"]:
                n = lane_ctr.get(o["eng"], 0)
                lane_ctr[o["eng"]] = n + 1
                o["sem"] = f'{o["eng"]}_d{n % self.NLANES}'
            else:
                o["sem"] = o["eng"]
        def _edge_needed(od, o):
            if od["eng"] != o["eng"] or od["dma"]:
                return True
            if od["eng"] == "tensor":
                return False
            return True

        awaited = set(i for i, o in enumerate(self.ops) if o["force"])
        for o in self.ops:
            for d in o["deps"]:
                if _edge_needed(self.ops[d], o):
                    awaited.add(d)
        cnt = {}
        for e in ENGINES:
            for i in seq[e]:
                o = self.ops[i]
                if o["dma"]:
                    o["inc"] = 16
                elif i in awaited:
                    o["inc"] = 1
                else:
                    o["inc"] = 0
                c = cnt.get(o["sem"], 0) + o["inc"]
                cnt[o["sem"]] = c
                o["cum"] = c
        for o in self.ops:
            w = {}
            for d in o["deps"]:
                od = self.ops[d]
                if not _edge_needed(od, o):
                    continue
                w[od["sem"]] = max(w.get(od["sem"], 0), od["cum"])
            o["waits"] = w
        self.seq = seq
        self.final = dict(cnt)
        self.sem_names = sorted(set(cnt) | set(ENGINES))

    def emit(self, ename, eng, sems):
        waited = {}
        for i in self.seq[ename]:
            o = self.ops[i]
            for de in sorted(o["waits"]):
                v = o["waits"][de]
                if waited.get(de, -1) < v:
                    eng.wait_ge(sems[de], v)
                    waited[de] = v
            inst = o["fn"](eng)
            if o["inc"]:
                inst.then_inc(sems[o["sem"]], o["inc"])


def build_nc(phases=PHASES_ALL, debug=()):
    nc = bass.Bass("TRN2", target_bir_lowering=False)

    x = nc.dram_tensor("x", [4096, D], F32, kind="ExternalInput").ap()
    wqkvT = nc.dram_tensor("wqkvT", [D, 3072], F32, kind="ExternalInput").ap()
    bqkv = nc.dram_tensor("bqkv", [128, 24], F32, kind="ExternalInput").ap()
    wfcT = nc.dram_tensor("wfcT", [D, D], F32, kind="ExternalInput").ap()
    bfc = nc.dram_tensor("bfc", [1, D], F32, kind="ExternalInput").ap()
    rwT = nc.dram_tensor("rwT", [D, 32], F32, kind="ExternalInput").ap()
    rb = nc.dram_tensor("rb", [1, 32], F32, kind="ExternalInput").ap()
    w1 = nc.dram_tensor("w1", [NE, D, 256], F32, kind="ExternalInput").ap()
    b1t = nc.dram_tensor("b1t", [128, 2 * NE], F32, kind="ExternalInput").ap()
    w2 = nc.dram_tensor("w2", [NE, 256, D], F32, kind="ExternalInput").ap()
    b2 = nc.dram_tensor("b2", [NE, D], F32, kind="ExternalInput").ap()
    masks = nc.dram_tensor("masks", [8, 128, 128], F32, kind="ExternalInput").ap()
    ltu = nc.dram_tensor("ltu", [128, 128], F32, kind="ExternalInput").ap()
    ident = nc.dram_tensor("ident", [128, 128], F32, kind="ExternalInput").ap()
    out = nc.dram_tensor("out", [2048, D], F32, kind="ExternalOutput").ap()

    xn_tm = nc.dram_tensor("xn_tm", [4096, D], BF16).ap()
    xs_dram = nc.dram_tensor("xs_dram", [NE * 192, D], BF16).ap()
    fcx_dram = nc.dram_tensor("fcx_dram", [2048, D], F32).ap()
    y2_dram = nc.dram_tensor("y2_dram", [NE * 192, D], F32).ap()

    dbg = {}
    for name in debug:
        shp = {"xn": [4096, D], "K": [128, 4 * 4096], "Q": [128, 4 * 2048],
               "Va": [128, 8 * 32 * 65], "aT": [128, 8 * 2048],
               "eid": [128, 16], "ffn": [128, NOWN * D],
               "lg": [128, 16 * 32]}[name]
        dt = F32 if name in ("eid", "lg") else (BF16 if name != "xn" else BF16)
        dbg[name] = nc.dram_tensor("dbg_" + name, shp, dt, kind="ExternalOutput").ap()

    b = Build()

    with contextlib.ExitStack() as stk:
        def sb(name, shape, dt):
            return stk.enter_context(nc.sbuf_tensor(name, shape, dt))

        def ps(name, shape, dt):
            return stk.enter_context(nc.psum_tensor(name, shape, dt))

        ident_bf = sb("ident_bf", [128, 128], BF16)
        masks_bf = sb("masks_bf", [128, 8 * 128], BF16)
        ones_bf = sb("ones_bf", [128, 512], BF16)
        eps_sb = sb("eps_sb", [128, 1], F32)
        bqkv_sb = sb("bqkv_sb", [128, 24], F32)
        b1t_sb = sb("b1t_sb", [128, 64], F32)
        rwT_bf = sb("rwT_bf", [128, 8 * 32], BF16)
        rb_bf = sb("rb_bf", [1, 32], BF16)
        bfc_bf = sb("bfc_bf", [1, D], BF16)
        iota_d = sb("iota_d", [128, 32], F32)
        x_t = [sb("x_t0", [128, D], F32), sb("x_t1", [128, D], F32)]
        xn_f = [sb("xn_f0", [128, D], F32), sb("xn_f1", [128, D], F32)]
        stats = sb("stats", [128, 32], F32)
        t1_f = sb("t1_f", [128, D], F32)
        xsq = t1_f
        A1 = sb("A1", [128, 16384], BF16)
        A2 = sb("A2", [128, 16640], BF16)
        A3 = sb("A3", [128, 16384], BF16)
        A4 = sb("A4", [128, 8704], BF16)
        K_sb = A1
        ffn_bf = A1
        Va_sb = A2[:, 0:16640]
        xno_sb = A2[:, 0:16384]
        wqkv_bf = A3[:, 0:12288]
        wfcT_bf = A3[:, 0:8192]
        w1_bf = [A3[:, 8192:10240], A3[:, 10240:12288]]
        w2_bf = [A3[:, 12288:14336], A3[:, 14336:16384]]
        Q_sb = A4[:, 0:8192]
        xnk = A4[:, 0:1024]
        xsT = [A4[:, 1024:2560], A4[:, 2560:4096]]
        y1s = [A4[:, 4096:4480], A4[:, 4480:4864]]
        _b2r_base = 4864
        b2r = [A4[0:1, 4864:5888], A4[0:1, 5888:6912]]
        O_sb = A4[:, 6912:6944]
        xnt_t = [sb("xnt0", [128, 8 * 512], BF16), sb("xnt1", [128, 8 * 512], BF16)]
        vfm = [sb("vfm0", [128, 512], BF16), sb("vfm1", [128, 512], BF16)]
        PT = [sb("PT0", [128, 1024], BF16), sb("PT1", [128, 1024], BF16)]
        aT = sb("aT", [128, 8 * 2048], BF16)
        lb_sb = sb("lb_sb", [128, 512], F32)
        stage = [sb("stage0", [128, 512], BF16), sb("stage1", [128, 512], BF16)]
        recip_sb = lb_sb
        recipb_sb = stage[0]
        logits_sb = sb("logits_sb", [128, 32], F32)
        eid_sb = sb("eid_sb", [128, 16], F32)
        ltu_bf = sb("ltu_bf", [128, 128], BF16)
        iota_a = sb("iota_a", [128, 32], F32)
        rank_t = sb("rank_t", [128, 32], F32)
        tmp32 = sb("tmp32", [128, 32], F32)
        idx_sb = sb("idx_sb", [128, 16], mybir.dt.int32)
        cnt_bf = sb("cnt_bf", [1, 16 * 32], BF16)
        y2sb = xn_f
        ffng = sb("ffng", [128, 1024], F32)
        _osb = sb("out_sb0", [128, D], F32)
        out_sb = [_osb, _osb]

        ps_st0 = ps("ps_st0", [128, 1024], F32)
        ps_st1 = ps("ps_st1", [128, 1024], F32)
        ps_av = ps("ps_av", [128, 1024], F32)
        ps_lb = ps("ps_lb", [128, 512], F32)
        ps_tbf = ps("ps_tbf", [128, 1024], BF16)
        ST = [(ps_st0, "st0a", 0), (ps_st0, "st0b", 512),
              (ps_st1, "st1a", 0), (ps_st1, "st1b", 512)]

        sems = {e: stk.enter_context(nc.semaphore("sem_" + e)) for e in ENGINES}
        for qe in ("sync", "gpsimd", "scalar", "vector", "tensor"):
            for ln in range(Build.NLANES):
                nm = f"{qe}_d{ln}"
                sems[nm] = stk.enter_context(nc.semaphore("sem_" + nm))
        block = stk.enter_context(nc.Block())

        b.op("sync", lambda e: e.dma_start(out=bqkv_sb[:], in_=bqkv),
             writes=["bqkv"], dma=True)
        b.op("gpsimd", lambda e: e.dma_start(out=ident_bf[:], in_=ident),
             writes=["ident"], dma=True)
        b.op("gpsimd", lambda e: e.dma_start(
            out=masks_bf[:], in_=masks.rearrange("r p q -> p r q")),
            writes=["masks"], dma=True)
        b.op("vector", lambda e: e.memset(ones_bf[:], 1.0), writes=["ones"])
        b.op("vector", lambda e: e.memset(eps_sb[:], 1e-5), writes=["eps"])
        b.op("sync", lambda e: e.dma_start(out=b1t_sb[:], in_=b1t),
             writes=["b1t"], dma=True)
        b.op("gpsimd", lambda e: e.dma_start(
            out=rwT_bf[:], in_=rwT.rearrange("(f p) e -> p f e", p=128)),
            writes=["rwT"], dma=True)
        b.op("gpsimd", lambda e: e.dma_start(out=rb_bf[:], in_=rb),
             writes=["rb"], dma=True)
        b.op("gpsimd", lambda e: e.dma_start(out=bfc_bf[:], in_=bfc),
             writes=["bfc"], dma=True)
        b.op("gpsimd", lambda e: e.iota(
            iota_d[:], pattern=[[-1, 32]], base=31, channel_multiplier=0,
            allow_small_or_imprecise_dtypes=True), writes=["iota"])
        b.op("gpsimd", lambda e: e.iota(
            iota_a[:], pattern=[[1, 32]], base=0, channel_multiplier=0,
            allow_small_or_imprecise_dtypes=True), writes=["iotaa"])
        b.op("gpsimd", lambda e: e.dma_start(out=ltu_bf[:], in_=ltu),
             writes=["ltu"], dma=True)


        S = stats

        def ln_tile(tt):
            bu = tt % 2
            xk, qk = f"x_t{bu}", f"xn_f{bu}"
            SB = 0 if bu == 0 else 16
            b.op("sync", lambda e, tt=tt, bu=bu: e.dma_start(
                out=x_t[bu][:], in_=x[tt * 128:(tt + 1) * 128, :]),
                writes=[xk], dma=True)
            b.op("vector", lambda e, bu=bu: e.scalar_tensor_tensor(
                out=xsq[:], in0=x_t[bu][:], scalar=1.0, in1=x_t[bu][:],
                op0=ALU.mult, op1=ALU.mult, accum_out=S[:, SB + 1:SB + 2]),
                reads=[xk], writes=["xsq" + str(bu), "s2" + str(bu)], sync_out=True)
            b.op("vector", lambda e, bu=bu: e.reduce_sum(
                out=S[:, SB + 0:SB + 1], in_=x_t[bu][:], axis=AX.X),
                reads=[xk], writes=["s1" + str(bu)], sync_out=True)
            b.op("vector", lambda e: e.tensor_scalar(
                out=S[:, SB + 2:SB + 3], in0=S[:, SB + 0:SB + 1], scalar1=1.0 / D, scalar2=None,
                op0=ALU.mult), reads=["s1" + str(bu)], writes=["mean" + str(bu)])
            b.op("vector", lambda e: e.tensor_scalar(
                out=S[:, SB + 3:SB + 4], in0=S[:, SB + 1:SB + 2], scalar1=1.0 / D, scalar2=None,
                op0=ALU.mult), reads=["s2" + str(bu)], writes=["ex2" + str(bu)])
            b.op("vector", lambda e: e.tensor_scalar(
                out=S[:, SB + 4:SB + 5], in0=S[:, SB + 2:SB + 3], scalar1=S[:, SB + 2:SB + 3], scalar2=None,
                op0=ALU.mult), reads=["mean" + str(bu)], writes=["m2" + str(bu)])
            b.op("vector", lambda e: e.tensor_sub(
                out=S[:, SB + 5:SB + 6], in0=S[:, SB + 3:SB + 4], in1=S[:, SB + 4:SB + 5]),
                reads=["ex2" + str(bu), "m2" + str(bu)], writes=["var" + str(bu)])
            b.op("scalar", lambda e: e.activation(
                out=S[:, SB + 6:SB + 7], in_=S[:, SB + 5:SB + 6], func=AF.Ln, bias=eps_sb[:]),
                reads=["var" + str(bu), "eps"], writes=["lnv" + str(bu)])
            b.op("scalar", lambda e: e.activation(
                out=S[:, SB + 7:SB + 8], in_=S[:, SB + 6:SB + 7], func=AF.Exp, scale=-0.5),
                reads=["lnv" + str(bu)], writes=["rstd" + str(bu)])
            b.op("vector", lambda e: e.tensor_scalar(
                out=S[:, SB + 8:SB + 9], in0=S[:, SB + 2:SB + 3], scalar1=S[:, SB + 7:SB + 8], scalar2=-1.0,
                op0=ALU.mult, op1=ALU.mult), reads=["mean" + str(bu), "rstd" + str(bu)], writes=["nmr" + str(bu)])
            b.op("vector", lambda e, bu=bu: e.tensor_scalar(
                out=xn_f[bu][:], in0=x_t[bu][:], scalar1=S[:, SB + 7:SB + 8],
                scalar2=S[:, SB + 8:SB + 9], op0=ALU.mult, op1=ALU.add),
                reads=[xk, "rstd" + str(bu), "nmr" + str(bu)], writes=[qk])
            b.op("gpsimd", lambda e, tt=tt, bu=bu: e.dma_start(
                out=xn_tm[tt * 128:(tt + 1) * 128, :], in_=xn_f[bu][:]),
                reads=[qk], writes=[f"xntm{tt}"], dma=True)

        for tt in range(NT):
            ln_tile(tt)

        if "xn" in dbg:
            for tt in range(NT):
                bu = tt % 2
                b.op("gpsimd", lambda e, tt=tt, bu=bu: e.dma_start(
                    out=stage[bu][:], in_=xn_tm[tt * 128:(tt + 1) * 128, 0:512]),
                    reads=[f"xntm{tt}"], writes=[f"dstg{bu}"], dma=True)
                b.op("gpsimd", lambda e, tt=tt, bu=bu: e.dma_start(
                    out=dbg["xn"][tt * 128:(tt + 1) * 128, 0:512], in_=stage[bu][:]),
                    reads=[f"dstg{bu}"], dma=True, force_await=True)
                b.op("gpsimd", lambda e, tt=tt, bu=bu: e.dma_start(
                    out=stage[bu][:], in_=xn_tm[tt * 128:(tt + 1) * 128, 512:1024]),
                    reads=[f"xntm{tt}", f"dstg{bu}x"], writes=[f"dstg{bu}"], dma=True)
                b.op("gpsimd", lambda e, tt=tt, bu=bu: e.dma_start(
                    out=dbg["xn"][tt * 128:(tt + 1) * 128, 512:1024], in_=stage[bu][:]),
                    reads=[f"dstg{bu}"], dma=True, force_await=True)

        if phases < 2:
            b.lower()
            _finish(b, block, sems)
            return nc

        for w in range(2):
            if w == 1:
                for occ in range(4):
                    for tc in range(8):
                        b.takeover(f"K1_{occ}_{tc}", [f"K0_{occ}_{tc}"])
                        b.takeover(f"Q1_{occ}_{tc}", [f"Q0_{occ}_{tc}"])
                for hw in range(8):
                    for t in range(NT):
                        b.takeover(f"Va1_{hw}_{t}", [f"Va0_{hw}_{t}"])
            for f in range(8):
                b.op("gpsimd", lambda e, f=f, w=w: e.dma_start(
                    out=wqkv_bf[:, f * 1536:(f + 1) * 1536],
                    in_=wqkvT[f * 128:(f + 1) * 128, w * 1536:(w + 1) * 1536]),
                    writes=[f"wq{f}"], dma=True)
            b.op("vector", lambda e: e.memset(Va_sb[:], 1.0),
                 writes=[f"Va{w}_{hw}_{t}" for hw in range(8) for t in range(NT)])

            for tc in range(8):
                bu = tc % 2
                for f in range(8):
                    b.op("scalar", lambda e, tc=tc, f=f, bu=bu: e.dma_start(
                        out=xnt_t[bu][:, f * 512:(f + 1) * 512],
                        in_=xn_tm[tc * 512:(tc + 1) * 512, f * 128:(f + 1) * 128],
                        transpose=True),
                        reads=[f"xntm{4 * tc + j}" for j in range(4)],
                        writes=[f"xnt{bu}_{f}"], dma=True)
                for oc in range(12):
                    pst, pk, po = ST[oc % 4]
                    comp = oc // 4
                    occ = oc % 4
                    gcol = w * 12 + oc
                    mm = []
                    for f in range(8):
                        mm.append(b.op("tensor", lambda e, f=f, oc=oc, pst=pst,
                                       po=po, bu=bu: e.matmul(
                            pst[:, po:po + 512],
                            wqkv_bf[:, f * 1536 + oc * 128: f * 1536 + (oc + 1) * 128],
                            xnt_t[bu][:, f * 512:(f + 1) * 512],
                            start=(f == 0), stop=(f == 7)),
                            reads=[f"wq{f}", f"xnt{bu}_{f}"], writes=[pk]))
                    if comp == 0:
                        b.op("vector", lambda e, pst=pst, po=po, occ=occ, tc=tc,
                             gcol=gcol, w=w: e.tensor_scalar(
                            out=Q_sb[:, occ * 2048 + tc * 256: occ * 2048 + (tc + 1) * 256]
                                .rearrange("p (a c) -> p a c", a=2),
                            in0=pst[:, po:po + 512].rearrange(
                                "p (a c) -> p a c", a=4)[:, 0::2, :],
                            scalar1=bqkv_sb[:, gcol:gcol + 1], scalar2=None,
                            op0=ALU.add),
                            reads=[pk, "bqkv"], writes=[f"Q{w}_{occ}_{tc}"])
                    elif comp == 1:
                        b.op("vector", lambda e, pst=pst, po=po, occ=occ, tc=tc,
                             gcol=gcol: e.tensor_scalar(
                            out=K_sb[:, occ * 4096 + tc * 512: occ * 4096 + (tc + 1) * 512],
                            in0=pst[:, po:po + 512],
                            scalar1=bqkv_sb[:, gcol:gcol + 1], scalar2=None,
                            op0=ALU.add),
                            reads=[pk, "bqkv"], writes=[f"K{w}_{occ}_{tc}"])
                    else:
                        vb = tc % 2
                        b.op("vector", lambda e, pst=pst, po=po, vb=vb,
                             gcol=gcol: e.tensor_scalar(
                            out=vfm[vb][:], in0=pst[:, po:po + 512],
                            scalar1=bqkv_sb[:, gcol:gcol + 1], scalar2=None,
                            op0=ALU.add),
                            reads=[pk, "bqkv"], writes=[f"vfm{vb}"])
                        for j in range(4):
                            tt = 4 * tc + j
                            hw0 = occ * 2
                            b.op("tensor", lambda e, vb=vb, j=j: e.transpose(
                                ps_tbf[:, 0:128], vfm[vb][:, j * 128:(j + 1) * 128],
                                ident_bf[:]),
                                reads=[f"vfm{vb}", "ident"], writes=["tbf"])
                            for hh in range(2):
                                hw = hw0 + hh
                                off = (hw * NT + tt) * 65
                                b.op("vector", lambda e, hh=hh, off=off: e.tensor_copy(
                                    out=Va_sb[:, off:off + 64],
                                    in_=ps_tbf[:, hh * 64:hh * 64 + 64]),
                                    reads=["tbf"],
                                    writes=[f"Va{w}_{hw0 + hh}_{tt}"])

            if "K" in dbg and w == 0:
                b.op("sync", lambda e: e.dma_start(out=dbg["K"], in_=K_sb[:]),
                     reads=[f"K{w}_{occ}_{tc}" for occ in range(4) for tc in range(8)],
                     dma=True, force_await=True)
            if "Q" in dbg and w == 0:
                b.op("sync", lambda e: e.dma_start(out=dbg["Q"], in_=Q_sb[:]),
                     reads=[f"Q{w}_{occ}_{tc}" for occ in range(4) for tc in range(8)],
                     dma=True, force_await=True)
            if "Va" in dbg and w == 0:
                b.op("sync", lambda e: e.dma_start(out=dbg["Va"], in_=Va_sb[:]),
                     reads=[f"Va{w}_{hw}_{t}" for hw in range(8) for t in range(NT)],
                     dma=True, force_await=True)

            if phases < 3:
                break

            for hp in range(4):
                for c in range(4):
                    nkt = 8 * c + 8
                    fch = w * 4 + hp

                    def emit_st(kt, hp=hp, c=c, w=w):
                        live = 512 if kt < 8 * c else LIVE[kt - 8 * c]
                        qoff = 512 - live
                        pst = ps_st0 if kt % 2 == 0 else ps_st1
                        pkeys = [("st0a", "st0b"), ("st1a", "st1b")][kt % 2]
                        for j in range(2):
                            b.op("tensor", lambda e, pst=pst, j=j, hp=hp, kt=kt,
                                 c=c, qoff=qoff: e.matmul(
                                pst[:, j * 512 + qoff: (j + 1) * 512],
                                K_sb[j * 64:(j + 1) * 64,
                                     hp * 4096 + kt * 128: hp * 4096 + (kt + 1) * 128],
                                Q_sb[j * 64:(j + 1) * 64,
                                     hp * 2048 + c * 512 + qoff: hp * 2048 + (c + 1) * 512],
                                start=True, stop=True,
                                tile_position=(64 * j, 0)),
                                reads=[f"K{w}_{hp}_{kt // 4}",
                                       f"Q{w}_{hp}_{2 * c}", f"Q{w}_{hp}_{2 * c + 1}"],
                                writes=[pkeys[j]])

                    emit_st(0)
                    for kt in range(nkt):
                        live = 512 if kt < 8 * c else LIVE[kt - 8 * c]
                        qoff = 512 - live
                        pst, ptb = (ps_st0, 0) if kt % 2 == 0 else (ps_st1, 1)
                        pkeys = [("st0a", "st0b"), ("st1a", "st1b")][ptb]
                        b.op("scalar", lambda e, pst=pst, ptb=ptb: e.activation(
                            out=PT[ptb][:], in_=pst[:], func=AF.Exp, scale=0.125),
                            reads=list(pkeys), writes=[f"pt{ptb}"])
                        if kt + 1 < nkt:
                            emit_st(kt + 1)
                        if kt >= 8 * c:
                            r = kt - 8 * c
                            for j in range(2):
                                b.op("vector", lambda e, ptb=ptb, j=j, qoff=qoff,
                                     r=r: e.tensor_mul(
                                    out=PT[ptb][:, j * 512 + qoff: j * 512 + qoff + 128],
                                    in0=PT[ptb][:, j * 512 + qoff: j * 512 + qoff + 128],
                                    in1=masks_bf[:, r * 128:(r + 1) * 128]),
                                    reads=[f"pt{ptb}", "masks"], writes=[f"pt{ptb}"])
                        for j in range(2):
                            hw = 2 * hp + j
                            b.op("tensor", lambda e, hw=hw, kt=kt, ptb=ptb, j=j,
                                 qoff=qoff, c=c, nkt=nkt: e.matmul(
                                ps_av[0:65, j * 512 + qoff: (j + 1) * 512],
                                Va_sb[:, (hw * NT + kt) * 65: (hw * NT + kt) * 65 + 65],
                                PT[ptb][:, j * 512 + qoff: (j + 1) * 512],
                                start=(kt == 0), stop=(kt == nkt - 1)),
                                reads=[f"pt{ptb}", f"Va{w}_{hw}_{kt}"],
                                writes=[f"av{j}"])
                    for j in range(2):
                        hv = f"av{j}"
                        b.op("scalar", lambda e, j=j: e.activation(
                            out=recip_sb[64:65, 0:512],
                            in_=ps_av[64:65, j * 512:(j + 1) * 512], func=AF.Ln),
                            reads=[hv], writes=["recip"])
                        b.op("scalar", lambda e: e.activation(
                            out=recipb_sb[64:65, 0:512], in_=recip_sb[64:65, 0:512],
                            func=AF.Exp, scale=-1.0),
                            reads=["recip"], writes=["recipb"])
                        b.op("tensor", lambda e: e.matmul(
                            ps_lb[0:64, :], ones_bf[64:65, 0:64],
                            recipb_sb[64:65, 0:512], start=True, stop=True),
                            reads=["recipb", "ones"], writes=["lb"])
                        b.op("vector", lambda e: e.tensor_copy(
                            out=lb_sb[0:64, :], in_=ps_lb[0:64, :]),
                            reads=["lb"], writes=["lbs"])
                        if j == 0:
                            b.op("vector", lambda e, j=j, fch=fch, c=c: e.tensor_mul(
                                out=aT[0:64, fch * 2048 + c * 512: fch * 2048 + (c + 1) * 512],
                                in0=ps_av[0:64, j * 512:(j + 1) * 512],
                                in1=lb_sb[0:64, :]),
                                reads=[hv, "lbs"], writes=[f"aT_{fch}_{c}L"])
                        else:
                            sgb = (hp * 4 + c) % 2
                            b.op("vector", lambda e, j=j, sgb=sgb: e.tensor_mul(
                                out=stage[sgb][0:64, :],
                                in0=ps_av[0:64, j * 512:(j + 1) * 512],
                                in1=lb_sb[0:64, :]),
                                reads=[hv, "lbs"], writes=[f"stg{sgb}"])
                            b.op("tensor", lambda e, sgb=sgb: e.matmul(
                                ps_av[64:128, 0:512], ident_bf[0:64, 0:64],
                                stage[sgb][0:64, :], start=True, stop=True,
                                tile_position=(0, 64)),
                                reads=[f"stg{sgb}", "ident"],
                                writes=["avhop", "av0"])
                            b.op("vector", lambda e, fch=fch, c=c: e.tensor_copy(
                                out=aT[64:128, fch * 2048 + c * 512: fch * 2048 + (c + 1) * 512],
                                in_=ps_av[64:128, 0:512]),
                                reads=["avhop", "av0"], writes=[f"aT_{fch}_{c}H"])

        if "aT" in dbg:
            b.op("sync", lambda e: e.dma_start(out=dbg["aT"], in_=aT[:]),
                 reads=[f"aT_{fc_}_{c}{h}" for fc_ in range(8) for c in range(4)
                        for h in "LH"],
                 dma=True, force_await=True)

        if phases < 4:
            b.lower()
            _finish(b, block, sems)
            return nc

        _va_keys = [f"Va{w}_{hw}_{t}" for w in range(2) for hw in range(8)
                    for t in range(NT)]
        for f in range(8):
            for s_ in range(NOWN):
                b.takeover(f"xno_{f}_{s_}", _va_keys)
        _q_keys = [f"Q{w}_{occ}_{tc}" for w in range(2) for occ in range(4)
                   for tc in range(8)]
        for kk in (["xnk", "xsT0", "xsT1", "y1s0", "y1s1", "b2r0", "b2r1", "O"]):
            b.takeover(kk, _q_keys)
        for f in range(8):
            for s in range(NOWN):
                b.op("sync", lambda e, f=f, s=s: e.dma_start(
                    out=xno_sb[:, f * 2048 + s * 128: f * 2048 + (s + 1) * 128],
                    in_=xn_tm[(2 * s) * 128:(2 * s + 1) * 128, f * 128:(f + 1) * 128],
                    transpose=True),
                    reads=[f"xntm{2 * s}"], writes=[f"xno_{f}_{s}"], dma=True)

        for s in range(NOWN):
            for f in range(8):
                b.op("tensor", lambda e, f=f, s=s: e.matmul(
                    ps_lb[:, 0:32],
                    xno_sb[:, f * 2048 + s * 128: f * 2048 + (s + 1) * 128],
                    rwT_bf[:, f * 32:(f + 1) * 32],
                    start=(f == 0), stop=False),
                    reads=[f"xno_{f}_{s}", "rwT"], writes=["lb"])
            b.op("tensor", lambda e: e.matmul(
                ps_lb[:, 0:32], ones_bf[0:1, 0:128], rb_bf[0:1, :],
                start=False, stop=True),
                reads=["ones", "rb"], writes=["lb"])
            b.op("vector", lambda e: e.tensor_copy(
                out=logits_sb[:], in_=ps_lb[:, 0:32]),
                reads=["lb"], writes=["lg"])
            if "lg" in dbg:
                b.op("sync", lambda e, s=s: e.dma_start(
                    out=dbg["lg"][:, s * 32:(s + 1) * 32], in_=logits_sb[:]),
                    reads=["lg"], dma=True, force_await=True)
            b.op("vector", lambda e: e.reduce_max(
                out=S[:, 10:11], in_=logits_sb[:], axis=AX.X),
                reads=["lg"], writes=["lmax"], sync_out=True)
            b.op("vector", lambda e: e.tensor_scalar(
                out=logits_sb[:], in0=logits_sb[:], scalar1=S[:, 10:11],
                scalar2=None, op0=ALU.is_ge), reads=["lg", "lmax"], writes=["lg"])
            b.op("vector", lambda e: e.tensor_mul(
                out=logits_sb[:], in0=logits_sb[:], in1=iota_d[:]),
                reads=["lg", "iota"], writes=["lg"])
            b.op("vector", lambda e: e.reduce_max(
                out=S[:, 11:12], in_=logits_sb[:], axis=AX.X),
                reads=["lg"], writes=["l31"], sync_out=True)
            b.op("vector", lambda e, s=s: e.tensor_scalar(
                out=eid_sb[:, s:s + 1], in0=S[:, 11:12], scalar1=-1.0,
                scalar2=31.0, op0=ALU.mult, op1=ALU.add),
                reads=["l31"], writes=[f"eid{s}"])

        if "eid" in dbg:
            b.op("sync", lambda e: e.dma_start(out=dbg["eid"], in_=eid_sb[:]),
                 reads=[f"eid{s}" for s in range(NOWN)], dma=True,
                 force_await=True)

        if phases < 5:
            b.lower()
            _finish(b, block, sems)
            return nc

        for s in range(NOWN):
            b.op("vector", lambda e, s=s: e.tensor_scalar(
                out=O_sb, in0=iota_a[:], scalar1=eid_sb[:, s:s + 1],
                scalar2=None, op0=ALU.is_equal),
                reads=[f"eid{s}", "iotaa"], writes=["O"])
            b.op("tensor", lambda e, s=s: e.matmul(
                ps_st1[:, 0:32], ltu_bf[:], O_sb, start=True, stop=(s == 0)),
                reads=["O", "ltu"], writes=["st1a"])
            for sp in range(s):
                b.op("tensor", lambda e, sp=sp, s=s: e.matmul(
                    ps_st1[:, 0:32], ones_bf[0:1, 0:128],
                    cnt_bf[0:1, sp * 32:(sp + 1) * 32],
                    start=False, stop=(sp == s - 1)),
                    reads=[f"cnt{sp}", "ones"], writes=["st1a"])
            b.op("tensor", lambda e: e.matmul(
                ps_st1[0:1, 512:544], ones_bf[:, 0:1], O_sb,
                start=True, stop=True), reads=["O", "ones"], writes=["st1b"])
            b.op("vector", lambda e: e.tensor_copy(out=rank_t[:],
                                                   in_=ps_st1[:, 0:32]),
                 reads=["st1a"], writes=["rankt"])
            b.op("vector", lambda e: e.tensor_mul(out=tmp32[:], in0=rank_t[:],
                                                  in1=O_sb),
                 reads=["rankt", "O"], writes=["tmpr"])
            b.op("vector", lambda e: e.reduce_sum(
                out=S[:, 12:13], in_=tmp32[:], axis=AX.X),
                reads=["tmpr"], writes=["slotr"], sync_out=True)
            b.op("vector", lambda e, s=s: e.scalar_tensor_tensor(
                out=S[:, 13:14], in0=eid_sb[:, s:s + 1], scalar=192.0,
                in1=S[:, 12:13], op0=ALU.mult, op1=ALU.add),
                reads=[f"eid{s}", "slotr"], writes=["slotf"])
            b.op("vector", lambda e, s=s: e.tensor_copy(
                out=idx_sb[:, s:s + 1], in_=S[:, 13:14]),
                reads=["slotf"], writes=[f"idx{s}"])
            b.op("vector", lambda e, s=s: e.tensor_copy(
                out=cnt_bf[0:1, s * 32:(s + 1) * 32], in_=ps_st1[0:1, 512:544]),
                reads=["st1b"], writes=[f"cnt{s}"])
            b.op("gpsimd", lambda e, s=s: e.dma_start(
                out=xnk, in_=xn_tm[(2 * s) * 128:(2 * s + 1) * 128, :]),
                reads=[f"xntm{2 * s}"], writes=["xnk"], dma=True)
            b.op("gpsimd", lambda e, s=s: e.indirect_dma_start(
                out=xs_dram[0:128, :],
                out_offset=IndirectOffsetOnAxis(idx_sb[:, s:s + 1], 0),
                in_=xnk, in_offset=None),
                reads=["xnk", f"idx{s}"], writes=[f"xs{s}"], dma=True)

        if phases < 6:
            b.lower()
            _finish(b, block, sems)
            return nc

        b.takeover("y2s0", ["xn_f0"])
        b.takeover("y2s1", ["xn_f1"])
        b.takeover("tbB", ["tbf"])
        _xs_keys = [f"xs{s}" for s in range(NOWN)]
        t1x = [t1_f, out_sb[0]]
        for kk in ("t1x0", "t1x1"):
            b.takeover(kk, ["t1"])
        for f in range(8):
            b.op("gpsimd", lambda e, f=f: e.dma_start(
                out=wfcT_bf[:, f * 1024:(f + 1) * 1024],
                in_=wfcT[f * 128:(f + 1) * 128, :]),
                writes=[f"wfc{f}"], dma=True)

        def emit_fc(s_):
            bu = s_ % 2
            c = s_ // 4
            b.op("sync", lambda e, s_=s_, bu=bu: e.dma_start(
                out=x_t[bu][:], in_=x[(2 * s_) * 128:(2 * s_ + 1) * 128, :]),
                writes=[f"x_t{bu}"], dma=True)
            for half in range(2):
                for f in range(8):
                    b.op("tensor", lambda e, f=f, half=half, s_=s_, c=c: e.matmul(
                        ps_lb[:, 0:512],
                        aT[:, f * 2048 + s_ * 128: f * 2048 + (s_ + 1) * 128],
                        wfcT_bf[:, f * 1024 + half * 512: f * 1024 + (half + 1) * 512],
                        start=(f == 0), stop=False),
                        reads=[f"wfc{f}", f"aT_{f}_{c}L", f"aT_{f}_{c}H"],
                        writes=["lb"])
                b.op("tensor", lambda e, half=half: e.matmul(
                    ps_lb[:, 0:512], ones_bf[0:1, 0:128],
                    bfc_bf[0:1, half * 512:(half + 1) * 512],
                    start=False, stop=True),
                    reads=["ones", "bfc"], writes=["lb"])
                b.op("vector", lambda e, half=half, bu=bu: e.tensor_add(
                    out=t1x[bu][:, half * 512:(half + 1) * 512],
                    in0=ps_lb[:, 0:512],
                    in1=x_t[bu][:, half * 512:(half + 1) * 512]),
                    reads=["lb", f"x_t{bu}"], writes=[f"t1x{bu}"])
            b.op("sync", lambda e, s_=s_, bu=bu: e.dma_start(
                out=fcx_dram[s_ * 128:(s_ + 1) * 128, :], in_=t1x[bu][:]),
                reads=[f"t1x{bu}"], writes=[f"fcx{s_}"], dma=True)

        def moe_loads(e_):
            wb = e_ % 2
            b.op("gpsimd", lambda e, e_=e_, wb=wb: e.dma_start(
                out=w1_bf[wb][:], in_=w1[e_, :, :].rearrange(
                    "(f p) h -> p f h", p=128)),
                writes=[f"w1b{wb}"], dma=True)
            b.op("gpsimd", lambda e, e_=e_, wb=wb: e.dma_start(
                out=w2_bf[wb][:], in_=w2[e_, :, :].rearrange(
                    "(hc p) o -> p hc o", p=128)),
                writes=[f"w2b{wb}"], dma=True)
            b.op("gpsimd", lambda e, e_=e_, wb=wb: e.dma_start(
                out=b2r[wb], in_=b2[e_:e_ + 1, :]),
                writes=[f"b2r{wb}"], dma=True)
            b.op("scalar", lambda e, e_=e_: e.dma_start(
                out=PT[0][:], in_=xs_dram[e_ * 192:e_ * 192 + 128, :]),
                reads=_xs_keys, writes=["pt0"], dma=True)
            b.op("sync", lambda e, e_=e_: e.dma_start(
                out=PT[1][:], in_=xs_dram[e_ * 192 + 64:e_ * 192 + 192, :]),
                reads=_xs_keys, writes=["pt1"], dma=True)

        def moe_T(e_, half):
            wb = e_ % 2
            for f in range(8):
                b.op("tensor", lambda e, half=half, f=f: e.transpose(
                    ps_tbf[:, f * 128:(f + 1) * 128],
                    PT[half][:, f * 128:(f + 1) * 128], ident_bf[:]),
                    reads=[f"pt{half}", "ident"], writes=[f"tb{f}", "tbB"])
            if half == 0:
                b.op("vector", lambda e, wb=wb: e.tensor_copy(
                    out=xsT[wb][:].rearrange("p (f c) -> p f c", c=192)[:, :, 0:128],
                    in_=ps_tbf[:].rearrange("p (f c) -> p f c", c=128)),
                    reads=[f"tb{f}" for f in range(8)] + ["tbB"],
                    writes=[f"xsT{wb}_{f}" for f in range(8)])
            else:
                b.op("vector", lambda e, wb=wb: e.tensor_copy(
                    out=xsT[wb][:].rearrange("p (f c) -> p f c", c=192)[:, :, 128:192],
                    in_=ps_tbf[:].rearrange("p (f c) -> p f c", c=128)[:, :, 64:128]),
                    reads=[f"tb{f}" for f in range(8)] + ["tbB"],
                    writes=[f"xsT{wb}_{f}" for f in range(8)])

        def moe_y1(e_):
            wb = e_ % 2
            w1k = f"w1b{wb}"
            for hc in range(2):
                po_y1 = hc * 512
                for f in range(8):
                    b.op("tensor", lambda e, hc=hc, f=f, wb=wb,
                         po_y1=po_y1: e.matmul(
                        ps_av[:, po_y1:po_y1 + 192],
                        w1_bf[wb][:, f * 256 + hc * 128: f * 256 + (hc + 1) * 128],
                        xsT[wb][:, f * 192:(f + 1) * 192],
                        start=(f == 0), stop=(f == 7)),
                        reads=[w1k, f"xsT{wb}_{f}"], writes=[f"av{hc}"])
                b.op("vector", lambda e, hc=hc, e_=e_, po_y1=po_y1,
                     wb=wb: e.tensor_scalar(
                    out=y1s[wb][:, hc * 192:(hc + 1) * 192],
                    in0=ps_av[:, po_y1:po_y1 + 192],
                    scalar1=b1t_sb[:, 2 * e_ + hc: 2 * e_ + hc + 1],
                    scalar2=0.0, op0=ALU.add, op1=ALU.max),
                    reads=[f"av{hc}", "b1t"], writes=[f"y1s{wb}"])

        def moe_y2(e_):
            wb = e_ % 2
            w2k = f"w2b{wb}"
            for tl in range(2):
                nsl = 128 if tl == 0 else 64
                pst = ps_st0 if tl == 0 else ps_st1
                pk = [("st0a", "st0b"), ("st1a", "st1b")][tl]
                for half in range(2):
                    for hc in range(2):
                        b.op("tensor", lambda e, tl=tl, half=half, hc=hc,
                             pst=pst, wb=wb, nsl=nsl: e.matmul(
                            pst[0:nsl, half * 512:(half + 1) * 512],
                            y1s[wb][:, hc * 192 + tl * 128: hc * 192 + tl * 128 + nsl],
                            w2_bf[wb][:, hc * 1024 + half * 512: hc * 1024 + (half + 1) * 512],
                            start=(hc == 0), stop=False),
                            reads=[f"y1s{wb}", w2k], writes=[pk[half]])
                    b.op("tensor", lambda e, half=half, pst=pst, wb=wb,
                         nsl=nsl: e.matmul(
                        pst[0:nsl, half * 512:(half + 1) * 512],
                        ones_bf[0:1, 0:nsl],
                        b2r[wb][0:1, half * 512:(half + 1) * 512],
                        start=False, stop=True),
                        reads=["ones", f"b2r{wb}"], writes=[pk[half]])
                yb = tl
                b.op("vector", lambda e, pst=pst, nsl=nsl, yb=yb: e.tensor_scalar(
                    out=y2sb[yb][0:nsl, :], in0=pst[0:nsl, :],
                    scalar1=0.0, scalar2=None, op0=ALU.max),
                    reads=list(pk), writes=[f"y2s{yb}"])
                b.op("sync", lambda e, e_=e_, tl=tl, nsl=nsl, yb=yb: e.dma_start(
                    out=y2_dram[e_ * 192 + tl * 128: e_ * 192 + tl * 128 + nsl, :],
                    in_=y2sb[yb][0:nsl, :]),
                    reads=[f"y2s{yb}"], writes=[f"y2d_{e_}_{tl}"], dma=True)

        moe_loads(0)
        moe_T(0, 0)
        moe_T(0, 1)
        for e_ in range(NE):
            if e_ + 1 < NE:
                moe_loads(e_ + 1)
            moe_y1(e_)
            if e_ + 1 < NE:
                moe_T(e_ + 1, 0)
            moe_y2(e_)
            if e_ + 1 < NE:
                moe_T(e_ + 1, 1)
            if e_ % 2 == 1:
                emit_fc(e_ // 2)

        if phases < 7:
            b.lower()
            _finish(b, block, sems)
            return nc

        _y2_keys = [f"y2d_{e_}_{tl}" for e_ in range(NE) for tl in range(2)]
        ffg = [ffng, xn_f[0]]
        b.takeover("ffg1", ["y2s0"])
        for s_ in range(NOWN):
            bu = s_ % 2
            b.op("sync", lambda e, s_=s_, bu=bu: e.dma_start(
                out=x_t[bu][:], in_=fcx_dram[s_ * 128:(s_ + 1) * 128, :]),
                reads=[f"fcx{s_}"], writes=[f"x_t{bu}"], dma=True)
            b.op("gpsimd", lambda e, s_=s_, bu=bu: e.indirect_dma_start(
                out=ffg[bu][:, 0:1024], out_offset=None,
                in_=y2_dram[0:128, :],
                in_offset=IndirectOffsetOnAxis(idx_sb[:, s_:s_ + 1], 0)),
                reads=_y2_keys + [f"idx{s_}"], writes=[f"ffg{bu}"], dma=True)
            b.op("vector", lambda e, s_=s_, bu=bu: e.tensor_add(
                out=out_sb[1][:], in0=x_t[bu][:], in1=ffg[bu][:, 0:1024]),
                reads=[f"x_t{bu}", f"ffg{bu}"], writes=["outsb", "t1x1"])
            b.op("sync", lambda e, s_=s_: e.dma_start(
                out=out[s_ * 128:(s_ + 1) * 128, :], in_=out_sb[1][:]),
                reads=["outsb"], writes=[f"odma{s_}"], dma=True,
                force_await=True)

        b.lower()
        _finish(b, block, sems)
    return nc


def _finish(b, block, sems):
    @block.sync
    def _(e):
        b.emit("sync", e, sems)
        for sn, v in sorted(b.final.items()):
            if v:
                e.wait_ge(sems[sn], v)

    @block.gpsimd
    def _(e):
        b.emit("gpsimd", e, sems)

    @block.vector
    def _(e):
        b.emit("vector", e, sems)

    @block.scalar
    def _(e):
        b.emit("scalar", e, sems)

    @block.tensor
    def _(e):
        b.emit("tensor", e, sems)



def _core_inputs(inputs, core):
    bidx, par = core // 2, core % 2
    x = np.asarray(inputs["L0_attn_x"], np.float32)[bidx]
    if par == 1:
        xr = x.reshape(32, 128, D)
        perm = np.arange(32)
        perm = perm + 1 - 2 * (perm % 2)
        x = np.ascontiguousarray(xr[perm].reshape(4096, D))
    else:
        x = np.ascontiguousarray(x)

    w_qkv = np.asarray(inputs["L0_attn_w_qkv"], np.float32)
    b_qkv = np.asarray(inputs["L0_attn_b_qkv"], np.float32)
    rows = []
    for w in range(2):
        for comp in range(3):
            for hw in range(8):
                h = w * 8 + hw
                base = comp * 1024 + h * 64
                rows.append(np.arange(base, base + 64))
    rows = np.concatenate(rows)
    wqkvT = np.ascontiguousarray(w_qkv[rows].T)
    bq = np.ascontiguousarray(b_qkv[rows].reshape(24, 128).T)

    w_fc = np.asarray(inputs["L0_attn_w_fc"], np.float32)
    wfcT = np.ascontiguousarray(w_fc.T)
    bfc = np.ascontiguousarray(inputs["L0_attn_b_fc"], dtype=np.float32).reshape(1, D)

    rwT = np.ascontiguousarray(np.asarray(inputs["L0_ffn_router_w"], np.float32).T)
    rb = np.ascontiguousarray(np.asarray(inputs["L0_ffn_router_b"], np.float32).reshape(1, 32))
    w1 = np.ascontiguousarray(np.asarray(inputs["L0_ffn_weights1"], np.float32))
    b1 = np.asarray(inputs["L0_ffn_bias1"], np.float32)
    b1t = np.ascontiguousarray(
        b1.reshape(NE, 2, 128).transpose(2, 0, 1).reshape(128, 2 * NE))
    w2 = np.ascontiguousarray(np.asarray(inputs["L0_ffn_weights2"], np.float32))
    b2 = np.ascontiguousarray(np.asarray(inputs["L0_ffn_bias2"], np.float32))

    mk = np.zeros((8, 128, 128), np.float32)
    tri = np.triu(np.ones((128, 128), np.float32))
    for r in range(8):
        if r % 2 == 0:
            mk[r] = tri
        else:
            mk[r] = 0.0 if par == 0 else 1.0

    return {"x": x, "wqkvT": wqkvT, "bqkv": bq, "wfcT": wfcT, "bfc": bfc,
            "rwT": rwT, "rb": rb, "w1": w1, "b1t": b1t, "w2": w2, "b2": b2,
            "masks": np.ascontiguousarray(mk),
            "ident": np.eye(128, dtype=np.float32),
            "ltu": np.triu(np.ones((128, 128), np.float32), 1)}


def kernel(**inputs):
    nc = build_nc()
    in_maps = [_core_inputs(inputs, c) for c in range(8)]
    res = run_bass_kernel_spmd(nc, in_maps, core_ids=list(range(8)))
    out = np.empty((4, 4096, D), np.float32)
    for c in range(8):
        bidx, par = c // 2, c % 2
        o = res.results[c]["out"].reshape(NOWN, 128, D)
        ob = out[bidx].reshape(32, 128, D)
        for s in range(NOWN):
            ob[2 * s + par] = o[s]
    return out


# version 19
# speedup vs baseline: 2.2185x; 1.0156x over previous
import contextlib
import numpy as np

import concourse.bass as bass
import concourse.mybir as mybir
from concourse.bass_core_rust import IndirectOffsetOnAxis
from concourse.bass_utils import run_bass_kernel_spmd

F32 = mybir.dt.float32
BF16 = mybir.dt.bfloat16
AF = mybir.ActivationFunctionType
ALU = mybir.AluOpType
AX = mybir.AxisListType

ENGINES = ("sync", "gpsimd", "vector", "scalar", "tensor")
LIVE = [512, 512, 384, 384, 256, 256, 128, 128]
D = 1024
NT = 32
NOWN = 16
NH = 16
NE = 32
PHASES_ALL = 99


class Build:

    def __init__(self):
        self.ops = []
        self.last_w = {}
        self.readers = {}

    def op(self, eng, fn, reads=(), writes=(), dma=False, force_await=False,
           sync_out=False):
        deps = set()
        for k in reads:
            w = self.last_w.get(k)
            if w is not None:
                deps.add(w)
        for k in writes:
            w = self.last_w.get(k)
            if w is not None:
                deps.add(w)
            for r in self.readers.get(k, ()):
                deps.add(r)
        i = len(self.ops)
        self.ops.append({"eng": eng, "fn": fn, "dma": dma, "deps": deps,
                         "force": force_await, "sync_out": sync_out})
        for k in reads:
            self.readers.setdefault(k, set()).add(i)
        for k in writes:
            self.last_w[k] = i
            self.readers[k] = set()
        return i

    def takeover(self, new_key, old_keys):
        rd = set()
        w = None
        for ok in old_keys:
            rd |= self.readers.get(ok, set())
            lw = self.last_w.get(ok)
            if lw is not None and (w is None or lw > w):
                w = lw
        if w is not None:
            self.last_w[new_key] = w
        self.readers[new_key] = rd

    NLANES = 4

    def lower(self):
        seq = {e: [] for e in ENGINES}
        lane_ctr = {}
        for i, o in enumerate(self.ops):
            seq[o["eng"]].append(i)
            if o["dma"]:
                n = lane_ctr.get(o["eng"], 0)
                lane_ctr[o["eng"]] = n + 1
                o["sem"] = f'{o["eng"]}_d{n % self.NLANES}'
            else:
                o["sem"] = o["eng"]
        def _edge_needed(od, o):
            if od["eng"] != o["eng"] or od["dma"]:
                return True
            if od["eng"] == "tensor":
                return False
            return True

        awaited = set(i for i, o in enumerate(self.ops) if o["force"])
        for o in self.ops:
            for d in o["deps"]:
                if _edge_needed(self.ops[d], o):
                    awaited.add(d)
        cnt = {}
        for e in ENGINES:
            for i in seq[e]:
                o = self.ops[i]
                if o["dma"]:
                    o["inc"] = 16
                elif i in awaited:
                    o["inc"] = 1
                else:
                    o["inc"] = 0
                c = cnt.get(o["sem"], 0) + o["inc"]
                cnt[o["sem"]] = c
                o["cum"] = c
        for o in self.ops:
            w = {}
            for d in o["deps"]:
                od = self.ops[d]
                if not _edge_needed(od, o):
                    continue
                w[od["sem"]] = max(w.get(od["sem"], 0), od["cum"])
            o["waits"] = w
        self.seq = seq
        self.final = dict(cnt)
        self.sem_names = sorted(set(cnt) | set(ENGINES))

    def emit(self, ename, eng, sems):
        waited = {}
        for i in self.seq[ename]:
            o = self.ops[i]
            for de in sorted(o["waits"]):
                v = o["waits"][de]
                if waited.get(de, -1) < v:
                    eng.wait_ge(sems[de], v)
                    waited[de] = v
            inst = o["fn"](eng)
            if o["inc"]:
                inst.then_inc(sems[o["sem"]], o["inc"])


def build_nc(phases=PHASES_ALL, debug=()):
    nc = bass.Bass("TRN2", target_bir_lowering=False)

    x = nc.dram_tensor("x", [4096, D], F32, kind="ExternalInput").ap()
    wqkvT = nc.dram_tensor("wqkvT", [D, 3072], F32, kind="ExternalInput").ap()
    bqkv = nc.dram_tensor("bqkv", [128, 24], F32, kind="ExternalInput").ap()
    wfcT = nc.dram_tensor("wfcT", [D, D], F32, kind="ExternalInput").ap()
    bfc = nc.dram_tensor("bfc", [1, D], F32, kind="ExternalInput").ap()
    rwT = nc.dram_tensor("rwT", [D, 32], F32, kind="ExternalInput").ap()
    rb = nc.dram_tensor("rb", [1, 32], F32, kind="ExternalInput").ap()
    w1 = nc.dram_tensor("w1", [NE, D, 256], F32, kind="ExternalInput").ap()
    b1t = nc.dram_tensor("b1t", [128, 2 * NE], F32, kind="ExternalInput").ap()
    w2 = nc.dram_tensor("w2", [NE, 256, D], F32, kind="ExternalInput").ap()
    b2 = nc.dram_tensor("b2", [NE, D], F32, kind="ExternalInput").ap()
    masks = nc.dram_tensor("masks", [8, 128, 256], F32, kind="ExternalInput").ap()
    ltu = nc.dram_tensor("ltu", [128, 128], F32, kind="ExternalInput").ap()
    ident = nc.dram_tensor("ident", [128, 128], F32, kind="ExternalInput").ap()
    out = nc.dram_tensor("out", [2048, D], F32, kind="ExternalOutput").ap()

    xn_tm = nc.dram_tensor("xn_tm", [4096, D], BF16).ap()
    xs_dram = nc.dram_tensor("xs_dram", [NE * 192, D], BF16).ap()
    fcx_dram = nc.dram_tensor("fcx_dram", [2048, D], F32).ap()
    y2_dram = nc.dram_tensor("y2_dram", [NE * 192, D], F32).ap()

    dbg = {}
    for name in debug:
        shp = {"xn": [4096, D], "K": [128, 4 * 4096], "Q": [128, 4 * 2048],
               "Va": [128, 8 * 32 * 65], "aT": [128, 8 * 2048],
               "eid": [128, 16], "ffn": [128, NOWN * D],
               "lg": [128, 16 * 32]}[name]
        dt = F32 if name in ("eid", "lg") else (BF16 if name != "xn" else BF16)
        dbg[name] = nc.dram_tensor("dbg_" + name, shp, dt, kind="ExternalOutput").ap()

    b = Build()

    with contextlib.ExitStack() as stk:
        def sb(name, shape, dt):
            return stk.enter_context(nc.sbuf_tensor(name, shape, dt))

        def ps(name, shape, dt):
            return stk.enter_context(nc.psum_tensor(name, shape, dt))

        ident_bf = sb("ident_bf", [128, 128], BF16)
        masks_bf = sb("masks_bf", [128, 8 * 256], BF16)
        ones_bf = sb("ones_bf", [128, 512], BF16)
        eps_sb = sb("eps_sb", [128, 1], F32)
        bqkv_sb = sb("bqkv_sb", [128, 24], F32)
        b1t_sb = sb("b1t_sb", [128, 64], F32)
        rwT_bf = sb("rwT_bf", [128, 8 * 32], BF16)
        rb_bf = sb("rb_bf", [1, 32], BF16)
        bfc_bf = sb("bfc_bf", [1, D], BF16)
        iota_d = sb("iota_d", [128, 32], F32)
        x_t = [sb("x_t0", [128, D], F32), sb("x_t1", [128, D], F32)]
        xn_f = [sb("xn_f0", [128, D], F32), sb("xn_f1", [128, D], F32)]
        stats = sb("stats", [128, 32], F32)
        t1_f = sb("t1_f", [128, D], F32)
        xsq = t1_f
        A1 = sb("A1", [128, 16384], BF16)
        A2 = sb("A2", [128, 16640], BF16)
        A3 = sb("A3", [128, 16384], BF16)
        A4 = sb("A4", [128, 8704], BF16)
        K_sb = A1
        ffn_bf = A1
        Va_sb = A2[:, 0:16640]
        xno_sb = A2[:, 0:16384]
        wqkv_bf = A3[:, 0:12288]
        wfcT_bf = A3[:, 0:8192]
        w1_bf = [A3[:, 8192:10240], A3[:, 10240:12288]]
        w2_bf = [A3[:, 12288:14336], A3[:, 14336:16384]]
        Q_sb = A4[:, 0:8192]
        xnk = A4[:, 0:1024]
        xsT = [A4[:, 1024:2560], A4[:, 2560:4096]]
        y1s = [A4[:, 4096:4480], A4[:, 4480:4864]]
        _b2r_base = 4864
        b2r = [A4[0:1, 4864:5888], A4[0:1, 5888:6912]]
        O_sb = A4[:, 6912:6944]
        xnt_t = [sb("xnt0", [128, 8 * 512], BF16), sb("xnt1", [128, 8 * 512], BF16)]
        vfm = [sb("vfm0", [128, 512], BF16), sb("vfm1", [128, 512], BF16)]
        PT = [sb("PT0", [128, 1024], BF16), sb("PT1", [128, 1024], BF16)]
        aT = sb("aT", [128, 8 * 2048], BF16)
        lb_sb = sb("lb_sb", [128, 512], F32)
        stage = [sb("stage0", [128, 512], BF16), sb("stage1", [128, 512], BF16)]
        recip_sb = lb_sb
        recipb_sb = stage[0]
        logits_sb = sb("logits_sb", [128, 32], F32)
        eid_sb = sb("eid_sb", [128, 16], F32)
        ltu_bf = sb("ltu_bf", [128, 128], BF16)
        iota_a = sb("iota_a", [128, 32], F32)
        rank_t = sb("rank_t", [128, 32], F32)
        tmp32 = sb("tmp32", [128, 32], F32)
        idx_sb = sb("idx_sb", [128, 16], mybir.dt.int32)
        cnt_bf = sb("cnt_bf", [1, 16 * 32], BF16)
        y2sb = xn_f
        ffng = t1_f
        _osb = sb("out_sb0", [128, D], F32)
        out_sb = [_osb, _osb]

        ps_st0 = ps("ps_st0", [128, 1024], F32)
        ps_st1 = ps("ps_st1", [128, 1024], F32)
        ps_av = ps("ps_av", [128, 1024], F32)
        ps_lb = ps("ps_lb", [128, 512], F32)
        ps_tbf = ps("ps_tbf", [128, 1024], BF16)
        ST = [(ps_st0, "st0a", 0), (ps_st0, "st0b", 512),
              (ps_st1, "st1a", 0), (ps_st1, "st1b", 512)]

        sems = {e: stk.enter_context(nc.semaphore("sem_" + e)) for e in ENGINES}
        for qe in ("sync", "gpsimd", "scalar", "vector", "tensor"):
            for ln in range(Build.NLANES):
                nm = f"{qe}_d{ln}"
                sems[nm] = stk.enter_context(nc.semaphore("sem_" + nm))
        block = stk.enter_context(nc.Block())

        b.op("sync", lambda e: e.dma_start(out=bqkv_sb[:], in_=bqkv),
             writes=["bqkv"], dma=True)
        b.op("gpsimd", lambda e: e.dma_start(out=ident_bf[:], in_=ident),
             writes=["ident"], dma=True)
        b.op("gpsimd", lambda e: e.dma_start(
            out=masks_bf[:], in_=masks.rearrange("r p q -> p r q")),
            writes=["masks"], dma=True)
        b.op("vector", lambda e: e.memset(ones_bf[:], 1.0), writes=["ones"])
        b.op("vector", lambda e: e.memset(eps_sb[:], 1e-5), writes=["eps"])
        b.op("sync", lambda e: e.dma_start(out=b1t_sb[:], in_=b1t),
             writes=["b1t"], dma=True)
        b.op("gpsimd", lambda e: e.dma_start(
            out=rwT_bf[:], in_=rwT.rearrange("(f p) e -> p f e", p=128)),
            writes=["rwT"], dma=True)
        b.op("gpsimd", lambda e: e.dma_start(out=rb_bf[:], in_=rb),
             writes=["rb"], dma=True)
        b.op("gpsimd", lambda e: e.dma_start(out=bfc_bf[:], in_=bfc),
             writes=["bfc"], dma=True)
        b.op("gpsimd", lambda e: e.iota(
            iota_d[:], pattern=[[-1, 32]], base=31, channel_multiplier=0,
            allow_small_or_imprecise_dtypes=True), writes=["iota"])
        b.op("gpsimd", lambda e: e.iota(
            iota_a[:], pattern=[[1, 32]], base=0, channel_multiplier=0,
            allow_small_or_imprecise_dtypes=True), writes=["iotaa"])
        b.op("gpsimd", lambda e: e.dma_start(out=ltu_bf[:], in_=ltu),
             writes=["ltu"], dma=True)


        S = stats

        def ln_tile(tt):
            bu = tt % 2
            xk, qk = f"x_t{bu}", f"xn_f{bu}"
            SB = 0 if bu == 0 else 16
            b.op("sync", lambda e, tt=tt, bu=bu: e.dma_start(
                out=x_t[bu][:], in_=x[tt * 128:(tt + 1) * 128, :]),
                writes=[xk], dma=True)
            b.op("vector", lambda e, bu=bu: e.scalar_tensor_tensor(
                out=xsq[:], in0=x_t[bu][:], scalar=1.0, in1=x_t[bu][:],
                op0=ALU.mult, op1=ALU.mult, accum_out=S[:, SB + 1:SB + 2]),
                reads=[xk], writes=["xsq" + str(bu), "s2" + str(bu)], sync_out=True)
            b.op("vector", lambda e, bu=bu: e.reduce_sum(
                out=S[:, SB + 0:SB + 1], in_=x_t[bu][:], axis=AX.X),
                reads=[xk], writes=["s1" + str(bu)], sync_out=True)
            b.op("vector", lambda e: e.tensor_scalar(
                out=S[:, SB + 2:SB + 3], in0=S[:, SB + 0:SB + 1], scalar1=1.0 / D, scalar2=None,
                op0=ALU.mult), reads=["s1" + str(bu)], writes=["mean" + str(bu)])
            b.op("vector", lambda e: e.tensor_scalar(
                out=S[:, SB + 3:SB + 4], in0=S[:, SB + 1:SB + 2], scalar1=1.0 / D, scalar2=None,
                op0=ALU.mult), reads=["s2" + str(bu)], writes=["ex2" + str(bu)])
            b.op("vector", lambda e: e.tensor_scalar(
                out=S[:, SB + 4:SB + 5], in0=S[:, SB + 2:SB + 3], scalar1=S[:, SB + 2:SB + 3], scalar2=None,
                op0=ALU.mult), reads=["mean" + str(bu)], writes=["m2" + str(bu)])
            b.op("vector", lambda e: e.tensor_sub(
                out=S[:, SB + 5:SB + 6], in0=S[:, SB + 3:SB + 4], in1=S[:, SB + 4:SB + 5]),
                reads=["ex2" + str(bu), "m2" + str(bu)], writes=["var" + str(bu)])
            b.op("scalar", lambda e: e.activation(
                out=S[:, SB + 6:SB + 7], in_=S[:, SB + 5:SB + 6], func=AF.Ln, bias=eps_sb[:]),
                reads=["var" + str(bu), "eps"], writes=["lnv" + str(bu)])
            b.op("scalar", lambda e: e.activation(
                out=S[:, SB + 7:SB + 8], in_=S[:, SB + 6:SB + 7], func=AF.Exp, scale=-0.5),
                reads=["lnv" + str(bu)], writes=["rstd" + str(bu)])
            b.op("vector", lambda e: e.tensor_scalar(
                out=S[:, SB + 8:SB + 9], in0=S[:, SB + 2:SB + 3], scalar1=S[:, SB + 7:SB + 8], scalar2=-1.0,
                op0=ALU.mult, op1=ALU.mult), reads=["mean" + str(bu), "rstd" + str(bu)], writes=["nmr" + str(bu)])
            b.op("vector", lambda e, bu=bu: e.tensor_scalar(
                out=xn_f[bu][:], in0=x_t[bu][:], scalar1=S[:, SB + 7:SB + 8],
                scalar2=S[:, SB + 8:SB + 9], op0=ALU.mult, op1=ALU.add),
                reads=[xk, "rstd" + str(bu), "nmr" + str(bu)], writes=[qk])
            b.op("gpsimd", lambda e, tt=tt, bu=bu: e.dma_start(
                out=xn_tm[tt * 128:(tt + 1) * 128, :], in_=xn_f[bu][:]),
                reads=[qk], writes=[f"xntm{tt}"], dma=True)

        for tt in range(NT):
            ln_tile(tt)

        if "xn" in dbg:
            for tt in range(NT):
                bu = tt % 2
                b.op("gpsimd", lambda e, tt=tt, bu=bu: e.dma_start(
                    out=stage[bu][:], in_=xn_tm[tt * 128:(tt + 1) * 128, 0:512]),
                    reads=[f"xntm{tt}"], writes=[f"dstg{bu}"], dma=True)
                b.op("gpsimd", lambda e, tt=tt, bu=bu: e.dma_start(
                    out=dbg["xn"][tt * 128:(tt + 1) * 128, 0:512], in_=stage[bu][:]),
                    reads=[f"dstg{bu}"], dma=True, force_await=True)
                b.op("gpsimd", lambda e, tt=tt, bu=bu: e.dma_start(
                    out=stage[bu][:], in_=xn_tm[tt * 128:(tt + 1) * 128, 512:1024]),
                    reads=[f"xntm{tt}", f"dstg{bu}x"], writes=[f"dstg{bu}"], dma=True)
                b.op("gpsimd", lambda e, tt=tt, bu=bu: e.dma_start(
                    out=dbg["xn"][tt * 128:(tt + 1) * 128, 512:1024], in_=stage[bu][:]),
                    reads=[f"dstg{bu}"], dma=True, force_await=True)

        if phases < 2:
            b.lower()
            _finish(b, block, sems)
            return nc

        for w in range(2):
            if w == 1:
                for occ in range(4):
                    for tc in range(8):
                        b.takeover(f"K1_{occ}_{tc}", [f"K0_{occ}_{tc}"])
                        b.takeover(f"Q1_{occ}_{tc}", [f"Q0_{occ}_{tc}"])
                for hw in range(8):
                    for t in range(NT):
                        b.takeover(f"Va1_{hw}_{t}", [f"Va0_{hw}_{t}"])
            for f in range(8):
                b.op("gpsimd", lambda e, f=f, w=w: e.dma_start(
                    out=wqkv_bf[:, f * 1536:(f + 1) * 1536],
                    in_=wqkvT[f * 128:(f + 1) * 128, w * 1536:(w + 1) * 1536]),
                    writes=[f"wq{f}"], dma=True)
            b.op("vector", lambda e: e.memset(Va_sb[:], 1.0),
                 writes=[f"Va{w}_{hw}_{t}" for hw in range(8) for t in range(NT)])

            for tc in range(8):
                bu = tc % 2
                for f in range(8):
                    b.op("scalar", lambda e, tc=tc, f=f, bu=bu: e.dma_start(
                        out=xnt_t[bu][:, f * 512:(f + 1) * 512],
                        in_=xn_tm[tc * 512:(tc + 1) * 512, f * 128:(f + 1) * 128],
                        transpose=True),
                        reads=[f"xntm{4 * tc + j}" for j in range(4)],
                        writes=[f"xnt{bu}_{f}"], dma=True)
                for oc in range(12):
                    pst, pk, po = ST[oc % 4]
                    comp = oc // 4
                    occ = oc % 4
                    gcol = w * 12 + oc
                    mm = []
                    for f in range(8):
                        mm.append(b.op("tensor", lambda e, f=f, oc=oc, pst=pst,
                                       po=po, bu=bu: e.matmul(
                            pst[:, po:po + 512],
                            wqkv_bf[:, f * 1536 + oc * 128: f * 1536 + (oc + 1) * 128],
                            xnt_t[bu][:, f * 512:(f + 1) * 512],
                            start=(f == 0), stop=(f == 7)),
                            reads=[f"wq{f}", f"xnt{bu}_{f}"], writes=[pk]))
                    if comp == 0:
                        b.op("vector", lambda e, pst=pst, po=po, occ=occ, tc=tc,
                             gcol=gcol, w=w: e.tensor_scalar(
                            out=Q_sb[:, occ * 2048 + tc * 256: occ * 2048 + (tc + 1) * 256]
                                .rearrange("p (a c) -> p a c", a=2),
                            in0=pst[:, po:po + 512].rearrange(
                                "p (a c) -> p a c", a=4)[:, 0::2, :],
                            scalar1=bqkv_sb[:, gcol:gcol + 1], scalar2=None,
                            op0=ALU.add),
                            reads=[pk, "bqkv"], writes=[f"Q{w}_{occ}_{tc}"])
                    elif comp == 1:
                        b.op("vector", lambda e, pst=pst, po=po, occ=occ, tc=tc,
                             gcol=gcol: e.tensor_scalar(
                            out=K_sb[:, occ * 4096 + tc * 512: occ * 4096 + (tc + 1) * 512],
                            in0=pst[:, po:po + 512],
                            scalar1=bqkv_sb[:, gcol:gcol + 1], scalar2=None,
                            op0=ALU.add),
                            reads=[pk, "bqkv"], writes=[f"K{w}_{occ}_{tc}"])
                    else:
                        vb = tc % 2
                        b.op("vector", lambda e, pst=pst, po=po, vb=vb,
                             gcol=gcol: e.tensor_scalar(
                            out=vfm[vb][:], in0=pst[:, po:po + 512],
                            scalar1=bqkv_sb[:, gcol:gcol + 1], scalar2=None,
                            op0=ALU.add),
                            reads=[pk, "bqkv"], writes=[f"vfm{vb}"])
                        for j in range(4):
                            tt = 4 * tc + j
                            hw0 = occ * 2
                            b.op("tensor", lambda e, vb=vb, j=j: e.transpose(
                                ps_tbf[:, 0:128], vfm[vb][:, j * 128:(j + 1) * 128],
                                ident_bf[:]),
                                reads=[f"vfm{vb}", "ident"], writes=["tbf"])
                            for hh in range(2):
                                hw = hw0 + hh
                                off = (hw * NT + tt) * 65
                                b.op("vector", lambda e, hh=hh, off=off: e.tensor_copy(
                                    out=Va_sb[:, off:off + 64],
                                    in_=ps_tbf[:, hh * 64:hh * 64 + 64]),
                                    reads=["tbf"],
                                    writes=[f"Va{w}_{hw0 + hh}_{tt}"])

            if "K" in dbg and w == 0:
                b.op("sync", lambda e: e.dma_start(out=dbg["K"], in_=K_sb[:]),
                     reads=[f"K{w}_{occ}_{tc}" for occ in range(4) for tc in range(8)],
                     dma=True, force_await=True)
            if "Q" in dbg and w == 0:
                b.op("sync", lambda e: e.dma_start(out=dbg["Q"], in_=Q_sb[:]),
                     reads=[f"Q{w}_{occ}_{tc}" for occ in range(4) for tc in range(8)],
                     dma=True, force_await=True)
            if "Va" in dbg and w == 0:
                b.op("sync", lambda e: e.dma_start(out=dbg["Va"], in_=Va_sb[:]),
                     reads=[f"Va{w}_{hw}_{t}" for hw in range(8) for t in range(NT)],
                     dma=True, force_await=True)

            if phases < 3:
                break

            for hp in range(4):
                for c in range(4):
                    nkt = 8 * c + 8
                    fch = w * 4 + hp

                    def emit_st(kt, hp=hp, c=c, w=w):
                        live = 512 if kt < 8 * c else LIVE[kt - 8 * c]
                        qoff = 512 - live
                        pst = ps_st0 if kt % 2 == 0 else ps_st1
                        pkeys = [("st0a", "st0b"), ("st1a", "st1b")][kt % 2]
                        for j in range(2):
                            b.op("tensor", lambda e, pst=pst, j=j, hp=hp, kt=kt,
                                 c=c, qoff=qoff: e.matmul(
                                pst[:, j * 512 + qoff: (j + 1) * 512],
                                K_sb[j * 64:(j + 1) * 64,
                                     hp * 4096 + kt * 128: hp * 4096 + (kt + 1) * 128],
                                Q_sb[j * 64:(j + 1) * 64,
                                     hp * 2048 + c * 512 + qoff: hp * 2048 + (c + 1) * 512],
                                start=True, stop=True,
                                tile_position=(64 * j, 0)),
                                reads=[f"K{w}_{hp}_{kt // 4}",
                                       f"Q{w}_{hp}_{2 * c}", f"Q{w}_{hp}_{2 * c + 1}"],
                                writes=[pkeys[j]])

                    emit_st(0)
                    for kt in range(nkt):
                        live = 512 if kt < 8 * c else LIVE[kt - 8 * c]
                        qoff = 512 - live
                        pst, ptb = (ps_st0, 0) if kt % 2 == 0 else (ps_st1, 1)
                        pkeys = [("st0a", "st0b"), ("st1a", "st1b")][ptb]
                        b.op("scalar", lambda e, pst=pst, ptb=ptb: e.activation(
                            out=PT[ptb][:], in_=pst[:], func=AF.Exp, scale=0.125),
                            reads=list(pkeys), writes=[f"pt{ptb}"])
                        if kt + 1 < nkt:
                            emit_st(kt + 1)
                        if kt >= 8 * c:
                            r = kt - 8 * c
                            b.op("vector", lambda e, ptb=ptb, qoff=qoff,
                                 r=r: e.tensor_mul(
                                out=PT[ptb][:].rearrange(
                                    "p (j q) -> p j q", j=2)[:, :, qoff:qoff + 128],
                                in0=PT[ptb][:].rearrange(
                                    "p (j q) -> p j q", j=2)[:, :, qoff:qoff + 128],
                                in1=masks_bf[:, r * 256:(r + 1) * 256].rearrange(
                                    "p (j q) -> p j q", j=2)),
                                reads=[f"pt{ptb}", "masks"], writes=[f"pt{ptb}"])
                        for j in range(2):
                            hw = 2 * hp + j
                            b.op("tensor", lambda e, hw=hw, kt=kt, ptb=ptb, j=j,
                                 qoff=qoff, c=c, nkt=nkt: e.matmul(
                                ps_av[0:65, j * 512 + qoff: (j + 1) * 512],
                                Va_sb[:, (hw * NT + kt) * 65: (hw * NT + kt) * 65 + 65],
                                PT[ptb][:, j * 512 + qoff: (j + 1) * 512],
                                start=(kt == 0), stop=(kt == nkt - 1)),
                                reads=[f"pt{ptb}", f"Va{w}_{hw}_{kt}"],
                                writes=[f"av{j}"])
                    for j in range(2):
                        hv = f"av{j}"
                        b.op("scalar", lambda e, j=j: e.activation(
                            out=recip_sb[64:65, 0:512],
                            in_=ps_av[64:65, j * 512:(j + 1) * 512], func=AF.Ln),
                            reads=[hv], writes=["recip"])
                        b.op("scalar", lambda e: e.activation(
                            out=recipb_sb[64:65, 0:512], in_=recip_sb[64:65, 0:512],
                            func=AF.Exp, scale=-1.0),
                            reads=["recip"], writes=["recipb"])
                        b.op("tensor", lambda e: e.matmul(
                            ps_lb[0:64, :], ones_bf[64:65, 0:64],
                            recipb_sb[64:65, 0:512], start=True, stop=True),
                            reads=["recipb", "ones"], writes=["lb"])
                        b.op("vector", lambda e: e.tensor_copy(
                            out=lb_sb[0:64, :], in_=ps_lb[0:64, :]),
                            reads=["lb"], writes=["lbs"])
                        if j == 0:
                            b.op("vector", lambda e, j=j, fch=fch, c=c: e.tensor_mul(
                                out=aT[0:64, fch * 2048 + c * 512: fch * 2048 + (c + 1) * 512],
                                in0=ps_av[0:64, j * 512:(j + 1) * 512],
                                in1=lb_sb[0:64, :]),
                                reads=[hv, "lbs"], writes=[f"aT_{fch}_{c}L"])
                        else:
                            sgb = (hp * 4 + c) % 2
                            b.op("vector", lambda e, j=j, sgb=sgb: e.tensor_mul(
                                out=stage[sgb][0:64, :],
                                in0=ps_av[0:64, j * 512:(j + 1) * 512],
                                in1=lb_sb[0:64, :]),
                                reads=[hv, "lbs"], writes=[f"stg{sgb}"])
                            b.op("tensor", lambda e, sgb=sgb: e.matmul(
                                ps_av[64:128, 0:512], ident_bf[0:64, 0:64],
                                stage[sgb][0:64, :], start=True, stop=True,
                                tile_position=(0, 64)),
                                reads=[f"stg{sgb}", "ident"],
                                writes=["avhop", "av0"])
                            b.op("vector", lambda e, fch=fch, c=c: e.tensor_copy(
                                out=aT[64:128, fch * 2048 + c * 512: fch * 2048 + (c + 1) * 512],
                                in_=ps_av[64:128, 0:512]),
                                reads=["avhop", "av0"], writes=[f"aT_{fch}_{c}H"])

        if "aT" in dbg:
            b.op("sync", lambda e: e.dma_start(out=dbg["aT"], in_=aT[:]),
                 reads=[f"aT_{fc_}_{c}{h}" for fc_ in range(8) for c in range(4)
                        for h in "LH"],
                 dma=True, force_await=True)

        if phases < 4:
            b.lower()
            _finish(b, block, sems)
            return nc

        _va_keys = [f"Va{w}_{hw}_{t}" for w in range(2) for hw in range(8)
                    for t in range(NT)]
        for f in range(8):
            for s_ in range(NOWN):
                b.takeover(f"xno_{f}_{s_}", _va_keys)
        _q_keys = [f"Q{w}_{occ}_{tc}" for w in range(2) for occ in range(4)
                   for tc in range(8)]
        for kk in (["xnk", "xsT0", "xsT1", "y1s0", "y1s1", "b2r0", "b2r1", "O"]):
            b.takeover(kk, _q_keys)
        for f in range(8):
            for s in range(NOWN):
                b.op("sync", lambda e, f=f, s=s: e.dma_start(
                    out=xno_sb[:, f * 2048 + s * 128: f * 2048 + (s + 1) * 128],
                    in_=xn_tm[(2 * s) * 128:(2 * s + 1) * 128, f * 128:(f + 1) * 128],
                    transpose=True),
                    reads=[f"xntm{2 * s}"], writes=[f"xno_{f}_{s}"], dma=True)

        for s in range(NOWN):
            for f in range(8):
                b.op("tensor", lambda e, f=f, s=s: e.matmul(
                    ps_lb[:, 0:32],
                    xno_sb[:, f * 2048 + s * 128: f * 2048 + (s + 1) * 128],
                    rwT_bf[:, f * 32:(f + 1) * 32],
                    start=(f == 0), stop=False),
                    reads=[f"xno_{f}_{s}", "rwT"], writes=["lb"])
            b.op("tensor", lambda e: e.matmul(
                ps_lb[:, 0:32], ones_bf[0:1, 0:128], rb_bf[0:1, :],
                start=False, stop=True),
                reads=["ones", "rb"], writes=["lb"])
            b.op("vector", lambda e: e.tensor_copy(
                out=logits_sb[:], in_=ps_lb[:, 0:32]),
                reads=["lb"], writes=["lg"])
            if "lg" in dbg:
                b.op("sync", lambda e, s=s: e.dma_start(
                    out=dbg["lg"][:, s * 32:(s + 1) * 32], in_=logits_sb[:]),
                    reads=["lg"], dma=True, force_await=True)
            b.op("vector", lambda e: e.reduce_max(
                out=S[:, 10:11], in_=logits_sb[:], axis=AX.X),
                reads=["lg"], writes=["lmax"], sync_out=True)
            b.op("vector", lambda e: e.tensor_scalar(
                out=logits_sb[:], in0=logits_sb[:], scalar1=S[:, 10:11],
                scalar2=None, op0=ALU.is_ge), reads=["lg", "lmax"], writes=["lg"])
            b.op("vector", lambda e: e.tensor_mul(
                out=logits_sb[:], in0=logits_sb[:], in1=iota_d[:]),
                reads=["lg", "iota"], writes=["lg"])
            b.op("vector", lambda e: e.reduce_max(
                out=S[:, 11:12], in_=logits_sb[:], axis=AX.X),
                reads=["lg"], writes=["l31"], sync_out=True)
            b.op("vector", lambda e, s=s: e.tensor_scalar(
                out=eid_sb[:, s:s + 1], in0=S[:, 11:12], scalar1=-1.0,
                scalar2=31.0, op0=ALU.mult, op1=ALU.add),
                reads=["l31"], writes=[f"eid{s}"])

        if "eid" in dbg:
            b.op("sync", lambda e: e.dma_start(out=dbg["eid"], in_=eid_sb[:]),
                 reads=[f"eid{s}" for s in range(NOWN)], dma=True,
                 force_await=True)

        if phases < 5:
            b.lower()
            _finish(b, block, sems)
            return nc

        for s in range(NOWN):
            b.op("vector", lambda e, s=s: e.tensor_scalar(
                out=O_sb, in0=iota_a[:], scalar1=eid_sb[:, s:s + 1],
                scalar2=None, op0=ALU.is_equal),
                reads=[f"eid{s}", "iotaa"], writes=["O"])
            b.op("tensor", lambda e, s=s: e.matmul(
                ps_st1[:, 0:32], ltu_bf[:], O_sb, start=True, stop=(s == 0)),
                reads=["O", "ltu"], writes=["st1a"])
            for sp in range(s):
                b.op("tensor", lambda e, sp=sp, s=s: e.matmul(
                    ps_st1[:, 0:32], ones_bf[0:1, 0:128],
                    cnt_bf[0:1, sp * 32:(sp + 1) * 32],
                    start=False, stop=(sp == s - 1)),
                    reads=[f"cnt{sp}", "ones"], writes=["st1a"])
            b.op("tensor", lambda e: e.matmul(
                ps_st1[0:1, 512:544], ones_bf[:, 0:1], O_sb,
                start=True, stop=True), reads=["O", "ones"], writes=["st1b"])
            b.op("vector", lambda e: e.tensor_copy(out=rank_t[:],
                                                   in_=ps_st1[:, 0:32]),
                 reads=["st1a"], writes=["rankt"])
            b.op("vector", lambda e: e.tensor_mul(out=tmp32[:], in0=rank_t[:],
                                                  in1=O_sb),
                 reads=["rankt", "O"], writes=["tmpr"])
            b.op("vector", lambda e: e.reduce_sum(
                out=S[:, 12:13], in_=tmp32[:], axis=AX.X),
                reads=["tmpr"], writes=["slotr"], sync_out=True)
            b.op("vector", lambda e, s=s: e.scalar_tensor_tensor(
                out=S[:, 13:14], in0=eid_sb[:, s:s + 1], scalar=192.0,
                in1=S[:, 12:13], op0=ALU.mult, op1=ALU.add),
                reads=[f"eid{s}", "slotr"], writes=["slotf"])
            b.op("vector", lambda e, s=s: e.tensor_copy(
                out=idx_sb[:, s:s + 1], in_=S[:, 13:14]),
                reads=["slotf"], writes=[f"idx{s}"])
            b.op("vector", lambda e, s=s: e.tensor_copy(
                out=cnt_bf[0:1, s * 32:(s + 1) * 32], in_=ps_st1[0:1, 512:544]),
                reads=["st1b"], writes=[f"cnt{s}"])
            b.op("gpsimd", lambda e, s=s: e.dma_start(
                out=xnk, in_=xn_tm[(2 * s) * 128:(2 * s + 1) * 128, :]),
                reads=[f"xntm{2 * s}"], writes=["xnk"], dma=True)
            b.op("gpsimd", lambda e, s=s: e.indirect_dma_start(
                out=xs_dram[0:128, :],
                out_offset=IndirectOffsetOnAxis(idx_sb[:, s:s + 1], 0),
                in_=xnk, in_offset=None),
                reads=["xnk", f"idx{s}"], writes=[f"xs{s}"], dma=True)

        if phases < 6:
            b.lower()
            _finish(b, block, sems)
            return nc

        b.takeover("y2s0", ["xn_f0"])
        b.takeover("y2s1", ["xn_f1"])
        b.takeover("tbB", ["tbf"])
        _xs_keys = [f"xs{s}" for s in range(NOWN)]
        t1x = [t1_f, out_sb[0]]
        for kk in ("t1x0", "t1x1"):
            b.takeover(kk, ["t1"])
        for f in range(8):
            b.op("gpsimd", lambda e, f=f: e.dma_start(
                out=wfcT_bf[:, f * 1024:(f + 1) * 1024],
                in_=wfcT[f * 128:(f + 1) * 128, :]),
                writes=[f"wfc{f}"], dma=True)

        def emit_fc(s_):
            bu = s_ % 2
            c = s_ // 4
            b.op("sync", lambda e, s_=s_, bu=bu: e.dma_start(
                out=x_t[bu][:], in_=x[(2 * s_) * 128:(2 * s_ + 1) * 128, :]),
                writes=[f"x_t{bu}"], dma=True)
            for half in range(2):
                for f in range(8):
                    b.op("tensor", lambda e, f=f, half=half, s_=s_, c=c: e.matmul(
                        ps_lb[:, 0:512],
                        aT[:, f * 2048 + s_ * 128: f * 2048 + (s_ + 1) * 128],
                        wfcT_bf[:, f * 1024 + half * 512: f * 1024 + (half + 1) * 512],
                        start=(f == 0), stop=False),
                        reads=[f"wfc{f}", f"aT_{f}_{c}L", f"aT_{f}_{c}H"],
                        writes=["lb"])
                b.op("tensor", lambda e, half=half: e.matmul(
                    ps_lb[:, 0:512], ones_bf[0:1, 0:128],
                    bfc_bf[0:1, half * 512:(half + 1) * 512],
                    start=False, stop=True),
                    reads=["ones", "bfc"], writes=["lb"])
                b.op("vector", lambda e, half=half, bu=bu: e.tensor_add(
                    out=t1x[bu][:, half * 512:(half + 1) * 512],
                    in0=ps_lb[:, 0:512],
                    in1=x_t[bu][:, half * 512:(half + 1) * 512]),
                    reads=["lb", f"x_t{bu}"], writes=[f"t1x{bu}"])
            b.op("sync", lambda e, s_=s_, bu=bu: e.dma_start(
                out=fcx_dram[s_ * 128:(s_ + 1) * 128, :], in_=t1x[bu][:]),
                reads=[f"t1x{bu}"], writes=[f"fcx{s_}"], dma=True)

        def moe_loads(e_):
            wb = e_ % 2
            b.op("gpsimd", lambda e, e_=e_, wb=wb: e.dma_start(
                out=w1_bf[wb][:], in_=w1[e_, :, :].rearrange(
                    "(f p) h -> p f h", p=128)),
                writes=[f"w1b{wb}"], dma=True)
            b.op("gpsimd", lambda e, e_=e_, wb=wb: e.dma_start(
                out=w2_bf[wb][:], in_=w2[e_, :, :].rearrange(
                    "(hc p) o -> p hc o", p=128)),
                writes=[f"w2b{wb}"], dma=True)
            b.op("gpsimd", lambda e, e_=e_, wb=wb: e.dma_start(
                out=b2r[wb], in_=b2[e_:e_ + 1, :]),
                writes=[f"b2r{wb}"], dma=True)
            b.op("scalar", lambda e, e_=e_: e.dma_start(
                out=PT[0][:], in_=xs_dram[e_ * 192:e_ * 192 + 128, :]),
                reads=_xs_keys, writes=["pt0"], dma=True)
            b.op("sync", lambda e, e_=e_: e.dma_start(
                out=PT[1][:], in_=xs_dram[e_ * 192 + 64:e_ * 192 + 192, :]),
                reads=_xs_keys, writes=["pt1"], dma=True)

        def moe_T(e_, half):
            wb = e_ % 2
            for f in range(8):
                b.op("tensor", lambda e, half=half, f=f: e.transpose(
                    ps_tbf[:, f * 128:(f + 1) * 128],
                    PT[half][:, f * 128:(f + 1) * 128], ident_bf[:]),
                    reads=[f"pt{half}", "ident"], writes=[f"tb{f}", "tbB"])
            if half == 0:
                b.op("vector", lambda e, wb=wb: e.tensor_copy(
                    out=xsT[wb][:].rearrange("p (f c) -> p f c", c=192)[:, :, 0:128],
                    in_=ps_tbf[:].rearrange("p (f c) -> p f c", c=128)),
                    reads=[f"tb{f}" for f in range(8)] + ["tbB"],
                    writes=[f"xsT{wb}_{f}" for f in range(8)])
            else:
                b.op("vector", lambda e, wb=wb: e.tensor_copy(
                    out=xsT[wb][:].rearrange("p (f c) -> p f c", c=192)[:, :, 128:192],
                    in_=ps_tbf[:].rearrange("p (f c) -> p f c", c=128)[:, :, 64:128]),
                    reads=[f"tb{f}" for f in range(8)] + ["tbB"],
                    writes=[f"xsT{wb}_{f}" for f in range(8)])

        def moe_y1(e_):
            wb = e_ % 2
            w1k = f"w1b{wb}"
            for hc in range(2):
                po_y1 = hc * 512
                for f in range(8):
                    b.op("tensor", lambda e, hc=hc, f=f, wb=wb,
                         po_y1=po_y1: e.matmul(
                        ps_av[:, po_y1:po_y1 + 192],
                        w1_bf[wb][:, f * 256 + hc * 128: f * 256 + (hc + 1) * 128],
                        xsT[wb][:, f * 192:(f + 1) * 192],
                        start=(f == 0), stop=(f == 7)),
                        reads=[w1k, f"xsT{wb}_{f}"], writes=[f"av{hc}"])
                b.op("vector", lambda e, hc=hc, e_=e_, po_y1=po_y1,
                     wb=wb: e.tensor_scalar(
                    out=y1s[wb][:, hc * 192:(hc + 1) * 192],
                    in0=ps_av[:, po_y1:po_y1 + 192],
                    scalar1=b1t_sb[:, 2 * e_ + hc: 2 * e_ + hc + 1],
                    scalar2=0.0, op0=ALU.add, op1=ALU.max),
                    reads=[f"av{hc}", "b1t"], writes=[f"y1s{wb}"])

        def moe_y2(e_):
            wb = e_ % 2
            w2k = f"w2b{wb}"
            for tl in range(2):
                nsl = 128 if tl == 0 else 64
                pst = ps_st0 if tl == 0 else ps_st1
                pk = [("st0a", "st0b"), ("st1a", "st1b")][tl]
                for half in range(2):
                    for hc in range(2):
                        b.op("tensor", lambda e, tl=tl, half=half, hc=hc,
                             pst=pst, wb=wb, nsl=nsl: e.matmul(
                            pst[0:nsl, half * 512:(half + 1) * 512],
                            y1s[wb][:, hc * 192 + tl * 128: hc * 192 + tl * 128 + nsl],
                            w2_bf[wb][:, hc * 1024 + half * 512: hc * 1024 + (half + 1) * 512],
                            start=(hc == 0), stop=False),
                            reads=[f"y1s{wb}", w2k], writes=[pk[half]])
                    b.op("tensor", lambda e, half=half, pst=pst, wb=wb,
                         nsl=nsl: e.matmul(
                        pst[0:nsl, half * 512:(half + 1) * 512],
                        ones_bf[0:1, 0:nsl],
                        b2r[wb][0:1, half * 512:(half + 1) * 512],
                        start=False, stop=True),
                        reads=["ones", f"b2r{wb}"], writes=[pk[half]])
                yb = tl
                b.op("vector", lambda e, pst=pst, nsl=nsl, yb=yb: e.tensor_scalar(
                    out=y2sb[yb][0:nsl, :], in0=pst[0:nsl, :],
                    scalar1=0.0, scalar2=None, op0=ALU.max),
                    reads=list(pk), writes=[f"y2s{yb}"])
                b.op("sync", lambda e, e_=e_, tl=tl, nsl=nsl, yb=yb: e.dma_start(
                    out=y2_dram[e_ * 192 + tl * 128: e_ * 192 + tl * 128 + nsl, :],
                    in_=y2sb[yb][0:nsl, :]),
                    reads=[f"y2s{yb}"], writes=[f"y2d_{e_}_{tl}"], dma=True)

        moe_loads(0)
        moe_T(0, 0)
        moe_T(0, 1)
        for e_ in range(NE):
            if e_ + 1 < NE:
                moe_loads(e_ + 1)
            moe_y1(e_)
            if e_ + 1 < NE:
                moe_T(e_ + 1, 0)
            moe_y2(e_)
            if e_ + 1 < NE:
                moe_T(e_ + 1, 1)
            if e_ % 2 == 1:
                emit_fc(e_ // 2)

        if phases < 7:
            b.lower()
            _finish(b, block, sems)
            return nc

        _y2_keys = [f"y2d_{e_}_{tl}" for e_ in range(NE) for tl in range(2)]
        ffg = [ffng, xn_f[0]]
        b.takeover("ffg0", ["t1x0", "t1"])
        b.takeover("ffg1", ["y2s0"])
        for s_ in range(NOWN):
            bu = s_ % 2
            b.op("sync", lambda e, s_=s_, bu=bu: e.dma_start(
                out=x_t[bu][:], in_=fcx_dram[s_ * 128:(s_ + 1) * 128, :]),
                reads=[f"fcx{s_}"], writes=[f"x_t{bu}"], dma=True)
            b.op("gpsimd", lambda e, s_=s_, bu=bu: e.indirect_dma_start(
                out=ffg[bu][:, 0:1024], out_offset=None,
                in_=y2_dram[0:128, :],
                in_offset=IndirectOffsetOnAxis(idx_sb[:, s_:s_ + 1], 0)),
                reads=_y2_keys + [f"idx{s_}"], writes=[f"ffg{bu}"], dma=True)
            b.op("vector", lambda e, s_=s_, bu=bu: e.tensor_add(
                out=out_sb[1][:], in0=x_t[bu][:], in1=ffg[bu][:, 0:1024]),
                reads=[f"x_t{bu}", f"ffg{bu}"], writes=["outsb", "t1x1"])
            b.op("sync", lambda e, s_=s_: e.dma_start(
                out=out[s_ * 128:(s_ + 1) * 128, :], in_=out_sb[1][:]),
                reads=["outsb"], writes=[f"odma{s_}"], dma=True,
                force_await=True)

        b.lower()
        _finish(b, block, sems)
    return nc


def _finish(b, block, sems):
    @block.sync
    def _(e):
        b.emit("sync", e, sems)
        for sn, v in sorted(b.final.items()):
            if v:
                e.wait_ge(sems[sn], v)

    @block.gpsimd
    def _(e):
        b.emit("gpsimd", e, sems)

    @block.vector
    def _(e):
        b.emit("vector", e, sems)

    @block.scalar
    def _(e):
        b.emit("scalar", e, sems)

    @block.tensor
    def _(e):
        b.emit("tensor", e, sems)



def _core_inputs(inputs, core):
    bidx, par = core // 2, core % 2
    x = np.asarray(inputs["L0_attn_x"], np.float32)[bidx]
    if par == 1:
        xr = x.reshape(32, 128, D)
        perm = np.arange(32)
        perm = perm + 1 - 2 * (perm % 2)
        x = np.ascontiguousarray(xr[perm].reshape(4096, D))
    else:
        x = np.ascontiguousarray(x)

    w_qkv = np.asarray(inputs["L0_attn_w_qkv"], np.float32)
    b_qkv = np.asarray(inputs["L0_attn_b_qkv"], np.float32)
    rows = []
    for w in range(2):
        for comp in range(3):
            for hw in range(8):
                h = w * 8 + hw
                base = comp * 1024 + h * 64
                rows.append(np.arange(base, base + 64))
    rows = np.concatenate(rows)
    wqkvT = np.ascontiguousarray(w_qkv[rows].T)
    bq = np.ascontiguousarray(b_qkv[rows].reshape(24, 128).T)

    w_fc = np.asarray(inputs["L0_attn_w_fc"], np.float32)
    wfcT = np.ascontiguousarray(w_fc.T)
    bfc = np.ascontiguousarray(inputs["L0_attn_b_fc"], dtype=np.float32).reshape(1, D)

    rwT = np.ascontiguousarray(np.asarray(inputs["L0_ffn_router_w"], np.float32).T)
    rb = np.ascontiguousarray(np.asarray(inputs["L0_ffn_router_b"], np.float32).reshape(1, 32))
    w1 = np.ascontiguousarray(np.asarray(inputs["L0_ffn_weights1"], np.float32))
    b1 = np.asarray(inputs["L0_ffn_bias1"], np.float32)
    b1t = np.ascontiguousarray(
        b1.reshape(NE, 2, 128).transpose(2, 0, 1).reshape(128, 2 * NE))
    w2 = np.ascontiguousarray(np.asarray(inputs["L0_ffn_weights2"], np.float32))
    b2 = np.ascontiguousarray(np.asarray(inputs["L0_ffn_bias2"], np.float32))

    mk = np.zeros((8, 128, 256), np.float32)
    tri = np.triu(np.ones((128, 128), np.float32))
    for r in range(8):
        blk = tri if r % 2 == 0 else (
            np.zeros((128, 128), np.float32) if par == 0
            else np.ones((128, 128), np.float32))
        mk[r, :, 0:128] = blk
        mk[r, :, 128:256] = blk

    return {"x": x, "wqkvT": wqkvT, "bqkv": bq, "wfcT": wfcT, "bfc": bfc,
            "rwT": rwT, "rb": rb, "w1": w1, "b1t": b1t, "w2": w2, "b2": b2,
            "masks": np.ascontiguousarray(mk),
            "ident": np.eye(128, dtype=np.float32),
            "ltu": np.triu(np.ones((128, 128), np.float32), 1)}


def kernel(**inputs):
    nc = build_nc()
    in_maps = [_core_inputs(inputs, c) for c in range(8)]
    res = run_bass_kernel_spmd(nc, in_maps, core_ids=list(range(8)))
    out = np.empty((4, 4096, D), np.float32)
    for c in range(8):
        bidx, par = c // 2, c % 2
        o = res.results[c]["out"].reshape(NOWN, 128, D)
        ob = out[bidx].reshape(32, 128, D)
        for s in range(NOWN):
            ob[2 * s + par] = o[s]
    return out
